# version 01
# baseline (speedup 1.0000x reference)
import contextlib
import math

import numpy as np
import ml_dtypes

import concourse.bass as bass
import concourse.mybir as mybir
from concourse.bass_utils import run_bass_kernel_spmd

F32 = mybir.dt.float32
BF16 = mybir.dt.bfloat16
U8 = mybir.dt.uint8
AF = mybir.ActivationFunctionType
OP = mybir.AluOpType
AXX = mybir.AxisListType.X
BF = ml_dtypes.bfloat16

NCORE = 8
B, T, D = 4, 4096, 1024
TOK = B * T
TPC = TOK // NCORE
SEG_LIMIT = 30000

_prog_cache = {}


class Prog:

    def __init__(self):
        self.nc = bass.Bass("TRN2", target_bir_lowering=False)
        self._ops = []
        self._stack = contextlib.ExitStack()
        self._n = 0

    def sb(self, shape, dtype=F32, name=None):
        name = name or f"sb{self._n}"
        self._n += 1
        return self._stack.enter_context(self.nc.sbuf_tensor(name, list(shape), dtype))

    def ps(self, shape, dtype=F32, name=None):
        name = name or f"ps{self._n}"
        self._n += 1
        return self._stack.enter_context(self.nc.psum_tensor(name, list(shape), dtype))

    def inp(self, name, shape, dtype=F32):
        return self.nc.dram_tensor(name, list(shape), dtype, kind="ExternalInput").ap()

    def outp(self, name, shape, dtype=F32):
        return self.nc.dram_tensor(name, list(shape), dtype, kind="ExternalOutput").ap()

    def scratch(self, name, shape, dtype=F32):
        return self.nc.dram_tensor(name, list(shape), dtype, kind="Internal").ap()

    def op(self, eng, fn, inc=1, fence=False):
        self._ops.append((eng, fn, inc, fence))

    def dma(self, dst, src, eng="sync"):
        self.op(eng, lambda e, d=dst, s=src: e.dma_start(out=d, in_=s), inc=16)

    def vf(self, meth, *args, **kw):
        self.op("vector", lambda e: getattr(e, meth)(*args, **kw), 1, fence=True)

    def sf(self, meth, *args, **kw):
        self.op("scalar", lambda e: getattr(e, meth)(*args, **kw), 1, fence=True)

    @staticmethod
    def _needs_fence(meth, args, kw):
        def is_imm(x):
            return x is None or isinstance(x, (int, float))
        if meth in ("tensor_scalar", "tensor_single_scalar", "tensor_scalar_add",
                    "tensor_scalar_sub", "tensor_scalar_mul", "tensor_scalar_max",
                    "tensor_scalar_min"):
            sc1 = args[2] if len(args) > 2 else kw.get("scalar1")
            sc2 = args[3] if len(args) > 3 else kw.get("scalar2")
            return not (is_imm(sc1) and is_imm(sc2))
        if meth == "activation":
            return not is_imm(kw.get("bias", None))
        if meth == "scalar_tensor_tensor":
            sc = args[2] if len(args) > 2 else kw.get("scalar")
            return not is_imm(sc)
        return False

    def v(self, meth, *args, **kw):
        self.op("vector", lambda e: getattr(e, meth)(*args, **kw), 1, fence=True)

    def s(self, meth, *args, **kw):
        self.op("scalar", lambda e: getattr(e, meth)(*args, **kw), 1, fence=True)

    def t(self, meth, *args, **kw):
        self.op("tensor", lambda e: getattr(e, meth)(*args, **kw), 1)

    def g(self, meth, *args, **kw):
        self.op("gpsimd", lambda e: getattr(e, meth)(*args, **kw), 1, fence=True)

    def mm(self, psum, lhsT, rhs, start, stop):
        self.t("matmul", psum, lhsT, rhs, start=start, stop=stop)

    def recip_pos(self, out, in_):
        self.s("activation", out, in_, AF.Ln)
        self.s("activation", out, out, AF.Exp, scale=-1.0)

    def rsqrt_pos(self, out, in_, eps):
        self.v("tensor_scalar", out, in_, float(eps), None, OP.add)
        self.s("activation", out, out, AF.Ln)
        self.s("activation", out, out, AF.Exp, scale=-0.5)

    def maskc(self, out, S, m, madd):
        self.v("tensor_tensor", out, S, m, OP.mult)
        self.v("tensor_tensor", out, out, madd, OP.add)

    def build(self, final_eng="sync"):
        nc = self.nc
        sched = []
        seg = 0
        cnt = 0
        seg_totals = []
        prev_eng = None
        pending = {}
        engs_seen = set()
        for eng, fn, inc, fence in self._ops:
            if cnt + inc > SEG_LIMIT:
                seg_totals.append(cnt)
                for e2 in engs_seen:
                    pending.setdefault(e2, []).append((seg, cnt))
                seg += 1
                cnt = 0
                prev_eng = None
            waits = list(pending.pop(eng, []))
            if (prev_eng is not None and eng != prev_eng) or (fence and cnt > 0):
                waits.append((seg, cnt))
            sched.append((eng, fn, inc, seg, waits))
            cnt += inc
            prev_eng = eng
            engs_seen.add(eng)
        seg_totals.append(cnt)
        nseg = seg + 1
        per = {}
        for item in sched:
            per.setdefault(item[0], []).append(item)
        with self._stack:
            sems = [self._stack.enter_context(self.nc.semaphore(f"gsem{i}"))
                    for i in range(nseg)]
            with nc.Block() as block:
                def mk(items, final):
                    def emit(eng):
                        for _, fn, inc, sg, waits in items:
                            for (wsg, wval) in waits:
                                if wval > 0:
                                    eng.wait_ge(sems[wsg], wval)
                            instr = fn(eng)
                            instr.then_inc(sems[sg], inc)
                        if final:
                            eng.wait_ge(sems[nseg - 1], seg_totals[-1])
                            for i in range(nseg - 1):
                                eng.wait_ge(sems[i], seg_totals[i])
                    return emit
                for engname, items in per.items():
                    dec = getattr(block, engname)
                    dec(mk(items, final=(engname == final_eng)))
        return nc


def _f32(x):
    return np.ascontiguousarray(x, dtype=np.float32)


def _bf(x):
    return np.ascontiguousarray(np.asarray(x, dtype=np.float32).astype(BF))


def _rep(v, rows=128):
    v = np.asarray(v, dtype=np.float32).reshape(-1)
    return np.ascontiguousarray(np.broadcast_to(v[None, :], (rows, v.size)))



def _l0_rope_tables():
    half = 16
    inv = 1.0 / (10000.0 ** (np.arange(half, dtype=np.float64) * 2.0 / 32.0))
    ang = np.arange(T, dtype=np.float64)[:, None] * inv[None, :]
    return _f32(np.cos(ang)), _f32(np.sin(ang))


def _xpos_tables():
    half_idx = np.arange(0, 64, 2, dtype=np.float64)
    inv_freq = 1.0 / (10000.0 ** (half_idx / 64.0))
    t = np.arange(T, dtype=np.float64)
    freqs = t[:, None] * inv_freq[None, :]
    freqs = np.concatenate([freqs, freqs], axis=-1)
    base = (half_idx + 0.4 * 64.0) / (1.4 * 64.0)
    power = (t - T // 2) / 256.0
    scale = base[None, :] ** power[:, None]
    scale = np.concatenate([scale, scale], axis=-1)
    cos = np.cos(freqs)
    sin = np.sin(freqs)
    return (_f32(cos * scale), _f32(sin * scale),
            _f32(cos / scale), _f32(sin / scale))


def _l0_masks():
    CR, NB = 32, 128
    qpos = np.arange(T)
    cvalid = ((np.arange(NB)[None, :] + 1) * CR - 1) <= qpos[:, None]
    anyb = cvalid.any(-1)
    safe = np.where(anyb[:, None], cvalid, True)
    cvalid_f = _f32(cvalid)
    safe_f = _f32(safe)
    return cvalid_f, safe_f, _f32(anyb.astype(np.float32))


def _tri_masks():
    i = np.arange(128)[:, None]
    j = np.arange(512)[None, :]
    cdiag = np.stack([(j <= i + 128 * r) for r in range(4)]).astype(np.float32)
    wfirst = (np.arange(128)[None, :] >= np.arange(128)[:, None] + 1
              ).astype(np.float32)
    return _f32(cdiag), _f32(wfirst)


def _expand_E():
    E = np.zeros((128, 4096), dtype=np.float32)
    for jj in range(128):
        E[jj, jj * 32:(jj + 1) * 32] = 1.0
    return _bf(E)


def _pool_P4():
    P = np.zeros((128, 4), dtype=np.float32)
    for c in range(4):
        P[c * 32:(c + 1) * 32, c] = 1.0 / 32.0
    return _bf(P)



def build_ln0():
    p = Prog()
    x = p.inp("x", (TPC, D))
    xn = p.outp("xn", (TPC, D))
    xs = p.sb((128, D))
    xc = p.sb((128, D))
    sq = p.sb((128, D))
    st = p.sb((128, 4))
    for i in range(TPC // 128):
        p.dma(xs[:], x[i * 128:(i + 1) * 128, :])
        p.v("reduce_sum", st[:, 0:1], xs[:], axis=AXX)
        p.v("tensor_scalar", st[:, 1:2], st[:, 0:1], 1.0 / D, None, OP.mult)
        p.v("tensor_scalar", xc[:], xs[:], st[:, 1:2], None, OP.subtract)
        p.s("activation", sq[:], xc[:], AF.Square)
        p.v("reduce_sum", st[:, 2:3], sq[:], axis=AXX)
        p.v("tensor_scalar", st[:, 2:3], st[:, 2:3], 1.0 / D, None, OP.mult)
        p.rsqrt_pos(st[:, 3:4], st[:, 2:3], 1e-5)
        p.v("tensor_scalar", xc[:], xc[:], st[:, 3:4], None, OP.mult)
        p.dma(xn[i * 128:(i + 1) * 128, :], xc[:])
    return p.build()


def run_ln0(st):
    nc = _prog_cache.get("ln0")
    if nc is None:
        nc = _prog_cache["ln0"] = build_ln0()
    x = st["x0"]
    maps = [{"x": _f32(x[c * TPC:(c + 1) * TPC])} for c in range(NCORE)]
    res = run_bass_kernel_spmd(nc, maps, core_ids=list(range(NCORE)))
    st["xn0"] = np.concatenate([r["xn"] for r in res.results], axis=0)



def build_l0qkv():
    p = Prog()
    xnT = p.inp("xnT", (D, TPC))
    wqc = p.inp("wqc", (D, 256))
    wqup = p.inp("wqup", (256, 512))
    wk = p.inp("wk", (D, 64))
    wv = p.inp("wv", (D, 64))
    cosT = p.inp("cosT", (TPC, 16))
    sinT = p.inp("sinT", (TPC, 16))
    gq = p.inp("gq", (128, 64))
    gk = p.inp("gk", (128, 64))
    ident = p.inp("ident", (128, 128))
    qn = p.outp("qn", (TPC, 512))
    kn = p.outp("kn", (TPC, 64))
    vo = p.outp("vo", (TPC, 64))

    ids = p.sb((128, 128))
    p.dma(ids[:], ident[:])
    wqc_b = p.sb((128, 8 * 256))
    wqup_b = p.sb((128, 2 * 512))
    wk_b = p.sb((128, 8 * 64))
    wv_b = p.sb((128, 8 * 64))
    for c in range(8):
        p.dma(wqc_b[:, c * 256:(c + 1) * 256], wqc[c * 128:(c + 1) * 128, :])
        p.dma(wk_b[:, c * 64:(c + 1) * 64], wk[c * 128:(c + 1) * 128, :])
        p.dma(wv_b[:, c * 64:(c + 1) * 64], wv[c * 128:(c + 1) * 128, :])
    for c in range(2):
        p.dma(wqup_b[:, c * 512:(c + 1) * 512], wqup[c * 128:(c + 1) * 128, :])
    gq_s = p.sb((128, 64))
    gk_s = p.sb((128, 64))
    p.dma(gq_s[:], gq[:])
    p.dma(gk_s[:], gk[:])

    xb = p.sb((128, 8 * 128))
    q1p = p.ps((128, 256))
    q2p = p.ps((128, 512))
    ptr = p.ps((128, 128))
    q1b = p.sb((128, 256))
    q1T = p.sb((128, 2 * 128))
    qsb = p.sb((128, 512))
    qout = p.sb((128, 512))
    sqq = p.sb((128, 512))
    ksb = p.sb((128, 64))
    kout = p.sb((128, 64))
    vsb = p.sb((128, 64))
    cs_t = p.sb((128, 16))
    sn_t = p.sb((128, 16))
    ra = p.sb((128, 16))
    rb = p.sb((128, 16))
    red = p.sb((128, 4))

    def rope16(src, dst, c0):
        p.v("tensor_tensor", ra[:], src[:, c0:c0 + 16], cs_t[:], OP.mult)
        p.v("tensor_tensor", rb[:], src[:, c0 + 16:c0 + 32], sn_t[:], OP.mult)
        p.v("tensor_tensor", dst[:, c0:c0 + 16], ra[:], rb[:], OP.subtract)
        p.v("tensor_tensor", ra[:], src[:, c0:c0 + 16], sn_t[:], OP.mult)
        p.v("tensor_tensor", rb[:], src[:, c0 + 16:c0 + 32], cs_t[:], OP.mult)
        p.v("tensor_tensor", dst[:, c0 + 16:c0 + 32], ra[:], rb[:], OP.add)

    for tt in range(TPC // 128):
        t0 = tt * 128
        for c in range(8):
            p.dma(xb[:, c * 128:(c + 1) * 128], xnT[c * 128:(c + 1) * 128, t0:t0 + 128])
        p.dma(cs_t[:], cosT[t0:t0 + 128, :])
        p.dma(sn_t[:], sinT[t0:t0 + 128, :])
        for c in range(8):
            p.mm(q1p[:], xb[:, c * 128:(c + 1) * 128], wqc_b[:, c * 256:(c + 1) * 256],
                 start=(c == 0), stop=(c == 7))
        p.v("tensor_copy", q1b[:], q1p[:])
        for c in range(2):
            p.t("transpose", ptr[:], q1b[:, c * 128:(c + 1) * 128], ids[:])
            p.v("tensor_copy", q1T[:, c * 128:(c + 1) * 128], ptr[:])
        for c in range(2):
            p.mm(q2p[:], q1T[:, c * 128:(c + 1) * 128], wqup_b[:, c * 512:(c + 1) * 512],
                 start=(c == 0), stop=(c == 1))
        p.v("tensor_copy", qsb[:], q2p[:])
        for c in range(8):
            p.mm(q1p[:, 0:64], xb[:, c * 128:(c + 1) * 128], wk_b[:, c * 64:(c + 1) * 64],
                 start=(c == 0), stop=(c == 7))
        p.v("tensor_copy", ksb[:], q1p[:, 0:64])
        for c in range(8):
            p.mm(q1p[:, 64:128], xb[:, c * 128:(c + 1) * 128], wv_b[:, c * 64:(c + 1) * 64],
                 start=(c == 0), stop=(c == 7))
        p.v("tensor_copy", vsb[:], q1p[:, 64:128])
        p.dma(vo[t0:t0 + 128, :], vsb[:])
        for h in range(8):
            rope16(qsb, qout, h * 64)
            p.v("tensor_copy", qout[:, h * 64 + 32:h * 64 + 64],
                qsb[:, h * 64 + 32:h * 64 + 64])
        p.s("activation", sqq[:], qout[:], AF.Square)
        for h in range(8):
            hs = slice(h * 64, (h + 1) * 64)
            p.v("reduce_sum", red[:, 0:1], sqq[:, hs], axis=AXX)
            p.v("tensor_scalar", red[:, 0:1], red[:, 0:1], 1.0 / 64, None, OP.mult)
            p.rsqrt_pos(red[:, 1:2], red[:, 0:1], 1e-6)
            p.v("tensor_scalar", qout[:, hs], qout[:, hs], red[:, 1:2], None, OP.mult)
            p.v("tensor_tensor", qout[:, hs], qout[:, hs], gq_s[:], OP.mult)
        p.dma(qn[t0:t0 + 128, :], qout[:])
        rope16(ksb, kout, 0)
        p.v("tensor_copy", kout[:, 32:64], ksb[:, 32:64])
        p.s("activation", sqq[:, 0:64], kout[:], AF.Square)
        p.v("reduce_sum", red[:, 2:3], sqq[:, 0:64], axis=AXX)
        p.v("tensor_scalar", red[:, 2:3], red[:, 2:3], 1.0 / 64, None, OP.mult)
        p.rsqrt_pos(red[:, 3:4], red[:, 2:3], 1e-6)
        p.v("tensor_scalar", kout[:], kout[:], red[:, 3:4], None, OP.mult)
        p.v("tensor_tensor", kout[:], kout[:], gk_s[:], OP.mult)
        p.dma(kn[t0:t0 + 128, :], kout[:])
    return p.build()


def run_l0qkv(st, inp):
    nc = _prog_cache.get("l0qkv")
    if nc is None:
        nc = _prog_cache["l0qkv"] = build_l0qkv()
    xnT = np.ascontiguousarray(st["xn0"].T)
    cosF, sinF = _l0_rope_tables()
    ident = np.eye(128, dtype=np.float32)
    maps = []
    for c in range(NCORE):
        t0 = (c % 2) * 2048
        maps.append({
            "xnT": _f32(xnT[:, c * TPC:(c + 1) * TPC]),
            "wqc": _f32(inp["L0_attn_w_qc"]),
            "wqup": _f32(inp["L0_attn_w_qup"]),
            "wk": _f32(inp["L0_attn_w_k"]),
            "wv": _f32(inp["L0_attn_w_v"]),
            "cosT": _f32(cosF[t0:t0 + TPC]),
            "sinT": _f32(sinF[t0:t0 + TPC]),
            "gq": _rep(inp["L0_attn_g_qnorm"]),
            "gk": _rep(inp["L0_attn_g_knorm"]),
            "ident": ident,
        })
    res = run_bass_kernel_spmd(nc, maps, core_ids=list(range(NCORE)))
    st["l0_qn"] = np.concatenate([r["qn"] for r in res.results], axis=0)
    st["l0_kn"] = np.concatenate([r["kn"] for r in res.results], axis=0)
    st["l0_v"] = np.concatenate([r["vo"] for r in res.results], axis=0)



def build_l0attn():
    p = Prog()
    q4 = p.inp("q4", (256, T))
    kfm = p.inp("kfm", (64, T))
    ktm = p.inp("ktm", (T, 64))
    vtm = p.inp("vtm", (T, 64))
    sinkrep = p.inp("sinkrep", (128, 4))
    wgate = p.inp("wgate", (64, 3))
    bgrep = p.inp("bgrep", (128, 3))
    E_in = p.inp("E_in", (128, T), BF16)
    P4_in = p.inp("P4_in", (128, 4), BF16)
    P4f = p.inp("P4f", (128, 4))
    ident = p.inp("ident", (128, 128), BF16)
    msafe = p.inp("msafe", (32, 128, 128))
    cvg = p.inp("cvg", (8, 128, 512))
    cdiag = p.inp("cdiag", (4, 128, 512))
    wfirst = p.inp("wfirst", (128, 128))
    anyb0 = p.inp("anyb0", (128, 1))
    Mout = p.outp("Mout", (4, T, 64))

    ids = p.sb((128, 128), BF16)
    E_sb = p.sb((128, T), BF16)
    P4_sb = p.sb((128, 4), BF16)
    p.dma(ids[:], ident[:])
    p.dma(E_sb[:], E_in[:])
    p.dma(P4_sb[:], P4_in[:])
    msafe_sb = p.sb((128, 32 * 128))
    cvg_sb = p.sb((128, 8 * 512))
    cdiag_sb = p.sb((128, 4 * 512))
    wfirst_sb = p.sb((128, 128))
    anyb0_sb = p.sb((128, 1))
    for i in range(32):
        p.dma(msafe_sb[:, i * 128:(i + 1) * 128], msafe[i, :, :])
    for i in range(8):
        p.dma(cvg_sb[:, i * 512:(i + 1) * 512], cvg[i, :, :])
    for i in range(4):
        p.dma(cdiag_sb[:, i * 512:(i + 1) * 512], cdiag[i, :, :])
    p.dma(wfirst_sb[:], wfirst[:])
    p.dma(anyb0_sb[:], anyb0[:])
    sink_sb = p.sb((128, 4))
    esink_sb = p.sb((128, 4))
    p.dma(sink_sb[:], sinkrep[:])
    p.s("activation", esink_sb[:], sink_sb[:], AF.Exp)
    wg_b = p.sb((64, 3))
    bg_sb = p.sb((128, 3))
    p.dma(wg_b[:], wgate[:])
    p.dma(bg_sb[:], bgrep[:])

    kb = p.sb((64, T))
    p.dma(kb[:], kfm[:])
    ktb = p.sb((128, 32 * 64))
    vtb = p.sb((128, 32 * 64), BF16)
    P4f_sb = p.sb((128, 4))
    p.dma(P4f_sb[:], P4f[:])
    tf = p.sb((128, 64))
    for c in range(32):
        p.dma(ktb[:, c * 64:(c + 1) * 64], ktm[c * 128:(c + 1) * 128, :])
        p.dma(tf[:], vtm[c * 128:(c + 1) * 128, :])
        p.v("tensor_copy", vtb[:, c * 64:(c + 1) * 64], tf[:])
    pkc = p.ps((64, 128))
    kcb = p.sb((64, 128))
    vcfb = p.sb((64, 128), BF16)
    vctb = p.sb((128, 64), BF16)
    ptrb = p.ps((128, 128), BF16)
    for c in range(32):
        p.mm(pkc[:, 4 * c:4 * c + 4], ktb[:, c * 64:(c + 1) * 64], P4f_sb[:],
             start=True, stop=True)
    p.v("tensor_copy", kcb[:], pkc[:])
    for c in range(32):
        p.mm(pkc[:, 4 * c:4 * c + 4], vtb[:, c * 64:(c + 1) * 64], P4_sb[:],
             start=True, stop=True)
    p.v("tensor_copy", vcfb[:], pkc[:])
    p.t("transpose", ptrb[:, 0:64], vcfb[:], ids[0:64, 0:64])
    p.v("tensor_copy", vctb[:], ptrb[:, 0:64])

    q4f = p.sb((128, 2 * T))
    for r in range(2):
        p.dma(q4f[:, r * T:(r + 1) * T], q4[r * 128:(r + 1) * 128, :])
    qb = p.sb((64, T))

    psc = p.ps((128, 512))
    psS = p.ps((128, 512))
    pcomp = p.ps((128, 64))
    psel = p.ps((128, 64))
    pwin = p.ps((128, 64))
    scg = p.sb((128, 512))
    work = p.sb((128, 512))
    msel = p.sb((128, 512))
    eq = p.sb((128, 512))
    scr5 = p.sb((128, 512))
    red = p.sb((128, 8))
    Ssb = p.sb((128, T))
    msbuf = p.sb((128, T))
    esel = p.sb((128, T), BF16)
    ecb = p.sb((128, 128), BF16)
    cmask = p.sb((128, 128))
    scr1 = p.sb((128, 128))
    mselb = p.sb((128, 128), BF16)
    mselT = p.sb((128, 128), BF16)
    eT = p.sb((128, 128), BF16)
    mskf = p.sb((128, 512))
    wbuf = p.sb((128, 640))
    ew = p.sb((128, 640), BF16)
    gsb = p.sb((128, 3))
    rr = p.sb((128, 8))
    mix = p.sb((128, 64))
    t64 = p.sb((128, 64))

    for j in range(4):
        p.v("tensor_copy", qb[:], q4f[(j % 2) * 64:(j % 2) * 64 + 64,
                                      (j // 2) * T:(j // 2 + 1) * T])
        sinkcol = sink_sb[:, j:j + 1]
        esinkcol = esink_sb[:, j:j + 1]
        for G in range(8):
            for u in range(4):
                qt = 4 * G + u
                p.mm(psc[:, u * 128:(u + 1) * 128], qb[:, qt * 128:(qt + 1) * 128],
                     kcb[:], start=True, stop=True)
            p.s("mul", scg[:], psc[:], 0.125)
            cm = cvg_sb[:, G * 512:(G + 1) * 512]
            p.v("tensor_tensor", work[:], scg[:], cm, OP.mult)
            p.v("tensor_scalar", scr5[:], cm, 1.0, None, OP.subtract)
            p.v("scalar_tensor_tensor", work[:], scr5[:], 1e38, work[:],
                OP.mult, OP.add)
            p.v("memset", msel[:], 0.0)
            for _ in range(16):
                for u in range(4):
                    p.v("reduce_max", red[:, u:u + 1], work[:, u * 128:(u + 1) * 128],
                        axis=AXX)
                for u in range(4):
                    p.v("tensor_scalar", eq[:, u * 128:(u + 1) * 128],
                        work[:, u * 128:(u + 1) * 128], red[:, u:u + 1], None, OP.is_ge)
                p.v("tensor_tensor", msel[:], msel[:], eq[:], OP.max)
                p.v("scalar_tensor_tensor", work[:], eq[:], -1e38, work[:],
                    OP.mult, OP.add)
            for u in range(4):
                qt = 4 * G + u
                klen = (qt + 1) * 128
                nkk = qt // 4 + 1
                qtile = qb[:, qt * 128:(qt + 1) * 128]
                for kk in range(nkk):
                    w = min(512, klen - 512 * kk)
                    p.mm(psS[:, 0:w], qtile, kb[:, 512 * kk:512 * kk + w],
                         start=True, stop=True)
                    p.s("mul", Ssb[:, 512 * kk:512 * kk + w], psS[:, 0:w], 0.125)
                p.v("tensor_tensor", cmask[:], scg[:, u * 128:(u + 1) * 128],
                    msafe_sb[:, qt * 128:(qt + 1) * 128], OP.mult)
                p.v("tensor_scalar", scr1[:], msafe_sb[:, qt * 128:(qt + 1) * 128],
                    1.0, None, OP.subtract)
                p.v("scalar_tensor_tensor", cmask[:], scr1[:], 1e30, cmask[:],
                    OP.mult, OP.add)
                p.v("reduce_max", red[:, 4:5], cmask[:], axis=AXX)
                p.v("tensor_tensor", red[:, 4:5], red[:, 4:5], sinkcol, OP.max)
                p.v("tensor_scalar", red[:, 5:6], red[:, 4:5], -1.0, None, OP.mult)
                p.s("activation", ecb[:], cmask[:], AF.Exp, bias=red[:, 5:6])
                p.v("reduce_sum", red[:, 6:7], ecb[:], axis=AXX)
                p.s("activation", red[:, 7:8], red[:, 5:6], AF.Exp)
                p.v("tensor_tensor", red[:, 7:8], red[:, 7:8], esinkcol, OP.mult)
                p.v("tensor_tensor", red[:, 6:7], red[:, 6:7], red[:, 7:8], OP.add)
                p.recip_pos(rr[:, 0:1], red[:, 6:7])
                p.t("transpose", ptrb[:], ecb[:], ids[:])
                p.v("tensor_copy", eT[:], ptrb[:])
                p.mm(pcomp[:], eT[:], vctb[:], start=True, stop=True)
                p.v("tensor_copy", mselb[:], msel[:, u * 128:(u + 1) * 128])
                p.t("transpose", ptrb[:], mselb[:], ids[:])
                p.v("tensor_copy", mselT[:], ptrb[:])
                for kk in range(nkk):
                    w = min(512, klen - 512 * kk)
                    p.mm(psS[:, 0:w], mselT[:], E_sb[:, 512 * kk:512 * kk + w],
                         start=True, stop=True)
                    p.v("tensor_copy", mskf[:, 0:w], psS[:, 0:w])
                    if kk == nkk - 1:
                        p.v("tensor_tensor", mskf[:, 0:w], mskf[:, 0:w],
                            cdiag_sb[:, (qt % 4) * 512:(qt % 4) * 512 + w], OP.mult)
                    sl = slice(512 * kk, 512 * kk + w)
                    p.v("tensor_tensor", msbuf[:, sl], Ssb[:, sl], mskf[:, 0:w],
                        OP.mult)
                    p.v("tensor_scalar", scr5[:, 0:w], mskf[:, 0:w], 1.0, None,
                        OP.subtract)
                    p.v("scalar_tensor_tensor", msbuf[:, sl], scr5[:, 0:w], 1e30,
                        msbuf[:, sl], OP.mult, OP.add)
                p.v("reduce_max", red[:, 4:5], msbuf[:, 0:klen], axis=AXX)
                p.v("tensor_scalar", red[:, 5:6], red[:, 4:5], -1.0, None, OP.mult)
                p.s("activation", esel[:, 0:klen], msbuf[:, 0:klen], AF.Exp,
                    bias=red[:, 5:6])
                p.v("reduce_sum", red[:, 6:7], esel[:, 0:klen], axis=AXX)
                p.recip_pos(rr[:, 1:2], red[:, 6:7])
                for kt in range(qt + 1):
                    p.t("transpose", ptrb[:], esel[:, kt * 128:(kt + 1) * 128], ids[:])
                    p.v("tensor_copy", eT[:], ptrb[:])
                    p.mm(psel[:], eT[:], vtb[:, kt * 64:(kt + 1) * 64],
                         start=(kt == 0), stop=(kt == qt))
                kt0 = max(0, qt - 4)
                nwt = qt - kt0 + 1
                wl = nwt * 128
                for idx in range(nwt):
                    kt = kt0 + idx
                    src = Ssb[:, kt * 128:(kt + 1) * 128]
                    dst = wbuf[:, idx * 128:(idx + 1) * 128]
                    if kt == qt:
                        mm_ = cdiag_sb[:, 0:128]
                    elif qt >= 4 and idx == 0:
                        mm_ = wfirst_sb[:]
                    else:
                        p.v("tensor_copy", dst, src)
                        continue
                    p.v("tensor_tensor", dst, src, mm_, OP.mult)
                    p.v("tensor_scalar", scr1[:], mm_, 1.0, None, OP.subtract)
                    p.v("scalar_tensor_tensor", dst, scr1[:], 1e30, dst,
                        OP.mult, OP.add)
                p.v("reduce_max", red[:, 4:5], wbuf[:, 0:wl], axis=AXX)
                p.v("tensor_tensor", red[:, 4:5], red[:, 4:5], sinkcol, OP.max)
                p.v("tensor_scalar", red[:, 5:6], red[:, 4:5], -1.0, None, OP.mult)
                p.s("activation", ew[:, 0:wl], wbuf[:, 0:wl], AF.Exp, bias=red[:, 5:6])
                p.v("reduce_sum", red[:, 6:7], ew[:, 0:wl], axis=AXX)
                p.s("activation", red[:, 7:8], red[:, 5:6], AF.Exp)
                p.v("tensor_tensor", red[:, 7:8], red[:, 7:8], esinkcol, OP.mult)
                p.v("tensor_tensor", red[:, 6:7], red[:, 6:7], red[:, 7:8], OP.add)
                p.recip_pos(rr[:, 2:3], red[:, 6:7])
                for idx in range(nwt):
                    kt = kt0 + idx
                    p.t("transpose", ptrb[:], ew[:, idx * 128:(idx + 1) * 128], ids[:])
                    p.v("tensor_copy", eT[:], ptrb[:])
                    p.mm(pwin[:], eT[:], vtb[:, kt * 64:(kt + 1) * 64],
                         start=(idx == 0), stop=(idx == nwt - 1))
                p.mm(psS[:, 0:3], qtile, wg_b[:], start=True, stop=True)
                p.v("tensor_tensor", gsb[:], psS[:, 0:3], bg_sb[:], OP.add)
                p.s("activation", gsb[:], gsb[:], AF.Sigmoid)
                p.v("reduce_sum", rr[:, 3:4], gsb[:], axis=AXX)
                p.v("tensor_scalar", rr[:, 3:4], rr[:, 3:4], 1e-6, None, OP.max)
                p.recip_pos(rr[:, 4:5], rr[:, 3:4])
                p.v("tensor_tensor", rr[:, 5:6], gsb[:, 0:1], rr[:, 4:5], OP.mult)
                p.v("tensor_tensor", rr[:, 5:6], rr[:, 5:6], rr[:, 0:1], OP.mult)
                if qt == 0:
                    p.v("tensor_tensor", rr[:, 5:6], rr[:, 5:6], anyb0_sb[:], OP.mult)
                p.v("tensor_tensor", rr[:, 6:7], gsb[:, 1:2], rr[:, 4:5], OP.mult)
                p.v("tensor_tensor", rr[:, 6:7], rr[:, 6:7], rr[:, 1:2], OP.mult)
                p.v("tensor_tensor", rr[:, 7:8], gsb[:, 2:3], rr[:, 4:5], OP.mult)
                p.v("tensor_tensor", rr[:, 7:8], rr[:, 7:8], rr[:, 2:3], OP.mult)
                p.v("tensor_scalar", mix[:], pcomp[:], rr[:, 5:6], None, OP.mult)
                p.v("tensor_scalar", t64[:], psel[:], rr[:, 6:7], None, OP.mult)
                p.v("tensor_tensor", mix[:], mix[:], t64[:], OP.add)
                p.v("tensor_scalar", t64[:], pwin[:], rr[:, 7:8], None, OP.mult)
                p.v("tensor_tensor", mix[:], mix[:], t64[:], OP.add)
                p.dma(Mout[j, qt * 128:(qt + 1) * 128, :], mix[:])
    return p.build()


def run_l0attn(st, inp):
    nc = _prog_cache.get("l0attn")
    if nc is None:
        nc = _prog_cache["l0attn"] = build_l0attn()
    qn = st["l0_qn"]
    kn = st["l0_kn"]
    v = st["l0_v"]
    cvalid_f, safe_f, anyb = _l0_masks()
    cdiag, wfirst = _tri_masks()
    msafe_t = np.ascontiguousarray(safe_f.reshape(32, 128, 128))
    cvg = np.ascontiguousarray(
        cvalid_f.reshape(8, 4, 128, 128).transpose(0, 2, 1, 3).reshape(8, 128, 512))
    E = _expand_E()
    P4 = _pool_P4()
    ident = np.eye(128, dtype=BF)
    sink = np.asarray(inp["L0_attn_sink_logit"], dtype=np.float32)
    maps = []
    for c in range(NCORE):
        b = c // 2
        h0 = (c % 2) * 4
        bs = slice(b * T, (b + 1) * T)
        q4 = np.ascontiguousarray(
            qn[bs, h0 * 64:(h0 + 4) * 64].T)
        maps.append({
            "q4": _f32(q4),
            "kfm": _f32(kn[bs].T),
            "ktm": _f32(kn[bs]),
            "vtm": _f32(v[bs]),
            "sinkrep": _rep(sink[h0:h0 + 4]),
            "wgate": _f32(inp["L0_attn_w_gate"]),
            "bgrep": _rep(inp["L0_attn_b_gate"]),
            "E_in": E,
            "P4_in": P4,
            "P4f": _f32(np.zeros((128, 4)) + np.asarray(P4, np.float32)),
            "ident": ident,
            "msafe": msafe_t,
            "cvg": cvg,
            "cdiag": cdiag,
            "wfirst": wfirst,
            "anyb0": _f32(anyb[:128].reshape(128, 1)),
        })
    res = run_bass_kernel_spmd(nc, maps, core_ids=list(range(NCORE)))
    M = np.zeros((B, 8, T, 64), dtype=np.float32)
    for c in range(NCORE):
        b = c // 2
        h0 = (c % 2) * 4
        M[b, h0:h0 + 4] = res.results[c]["Mout"]
    st["l0_mix"] = M



def build_l0out():
    p = Prog()
    mixg = p.inp("mixg", (2, 256, TPC))
    wog = p.inp("wog", (2, 256, 512))
    wout = p.inp("wout", (D, D))
    attn0 = p.outp("attn0", (D, TPC))
    wog_b = p.sb((128, 2 * 2 * 512), BF16)
    wout_b = p.sb((128, 8 * D), BF16)
    wt = p.sb((128, 512))
    wt2 = p.sb((128, D))
    for g in range(2):
        for c in range(2):
            p.dma(wt[:], wog[g, c * 128:(c + 1) * 128, :])
            p.v("tensor_copy", wog_b[:, (g * 2 + c) * 512:(g * 2 + c + 1) * 512], wt[:])
    for kc in range(8):
        p.dma(wt2[:], wout[kc * 128:(kc + 1) * 128, :])
        p.v("tensor_copy", wout_b[:, kc * D:(kc + 1) * D], wt2[:])
    mg = p.sb((128, 512))
    mgb = p.sb((128, 4 * 512), BF16)
    zb = p.sb((128, 8 * 512), BF16)
    pz = p.ps((128, 512))
    asb = p.sb((128, 512))
    for ti in range(TPC // 512):
        cs = slice(ti * 512, (ti + 1) * 512)
        for g in range(2):
            for c in range(2):
                p.dma(mg[:], mixg[g, c * 128:(c + 1) * 128, cs])
                p.v("tensor_copy", mgb[:, (g * 2 + c) * 512:(g * 2 + c + 1) * 512],
                    mg[:])
        for g in range(2):
            for mc in range(4):
                for c in range(2):
                    p.mm(pz[:], wog_b[:, (g * 2 + c) * 512 + mc * 128:
                                      (g * 2 + c) * 512 + (mc + 1) * 128],
                         mgb[:, (g * 2 + c) * 512:(g * 2 + c + 1) * 512],
                         start=(c == 0), stop=(c == 1))
                p.v("tensor_copy", zb[:, (g * 4 + mc) * 512:(g * 4 + mc + 1) * 512],
                    pz[:])
        for mc2 in range(8):
            for kc in range(8):
                p.mm(pz[:], wout_b[:, kc * D + mc2 * 128:kc * D + (mc2 + 1) * 128],
                     zb[:, kc * 512:(kc + 1) * 512], start=(kc == 0), stop=(kc == 7))
            p.v("tensor_copy", asb[:], pz[:])
            p.dma(attn0[mc2 * 128:(mc2 + 1) * 128, cs], asb[:])
    return p.build()


def run_l0out(st, inp):
    nc = _prog_cache.get("l0out")
    if nc is None:
        nc = _prog_cache["l0out"] = build_l0out()
    M = st["l0_mix"]
    Mt = np.ascontiguousarray(M.transpose(0, 1, 3, 2))
    maps = []
    for c in range(NCORE):
        b = c // 2
        t0 = (c % 2) * 2048
        mixg = np.ascontiguousarray(
            Mt[b].reshape(2, 256, T)[:, :, t0:t0 + TPC])
        maps.append({
            "mixg": _f32(mixg),
            "wog": _f32(inp["L0_attn_w_out_group"]),
            "wout": _f32(inp["L0_attn_w_out"]),
        })
    res = run_bass_kernel_spmd(nc, maps, core_ids=list(range(NCORE)))
    a = np.concatenate([r["attn0"] for r in res.results], axis=1)
    st["attn0"] = a



def build_ffnqkv():
    p = Prog()
    xiT = p.inp("xiT", (D, TPC))
    xpT = p.inp("xpT", (D, TPC))
    wqkv = p.inp("wqkv", (D, 1536))
    qs = p.inp("qs", (128, 512))
    ks = p.inp("ks", (128, 512))
    csq = p.inp("csq", (TPC, 64))
    ssq = p.inp("ssq", (TPC, 64))
    csk = p.inp("csk", (TPC, 64))
    ssk = p.inp("ssk", (TPC, 64))
    qr = p.outp("qr", (TPC, 512))
    kr = p.outp("kr", (TPC, 512))
    vr = p.outp("vr", (TPC, 512))

    wb = p.sb((128, 8 * 1536), BF16)
    wt = p.sb((128, 1536))
    for c in range(8):
        p.dma(wt[:], wqkv[c * 128:(c + 1) * 128, :])
        p.v("tensor_copy", wb[:, c * 1536:(c + 1) * 1536], wt[:])
    qs_s = p.sb((128, 512))
    ks_s = p.sb((128, 512))
    p.dma(qs_s[:], qs[:])
    p.dma(ks_s[:], ks[:])
    hb = p.sb((128, 8 * 128), BF16)
    hf = p.sb((128, 128))
    pq = p.ps((128, 512))
    qsb = p.sb((128, 512))
    ksb = p.sb((128, 512))
    vsb = p.sb((128, 512))
    qo = p.sb((128, 512))
    ko = p.sb((128, 512))
    sqq = p.sb((128, 512))
    red = p.sb((128, 2))
    csT = p.sb((128, 64))
    snT = p.sb((128, 64))
    ra = p.sb((128, 32))
    rb = p.sb((128, 32))

    def xrope(src, dst):
        for h in range(8):
            c0 = h * 64
            p.v("tensor_tensor", ra[:], src[:, c0:c0 + 32], snT[:, 0:32], OP.mult)
            p.v("tensor_tensor", rb[:], src[:, c0:c0 + 32], csT[:, 0:32], OP.mult)
            p.v("tensor_tensor", dst[:, c0:c0 + 32], rb[:], ra[:], OP.bypass)

    for tt in range(TPC // 128):
        t0 = tt * 128
        for c in range(8):
            src = xiT if c < 4 else xpT
            p.dma(hf[:], src[c * 128:(c + 1) * 128, t0:t0 + 128])
            p.v("tensor_copy", hb[:, c * 128:(c + 1) * 128], hf[:])
        for part, dstp in ((0, qsb), (1, ksb), (2, vsb)):
            for c in range(8):
                p.mm(pq[:], hb[:, c * 128:(c + 1) * 128],
                     wb[:, c * 1536 + part * 512:c * 1536 + (part + 1) * 512],
                     start=(c == 0), stop=(c == 7))
            p.v("tensor_copy", dstp[:], pq[:])
        p.dma(vr[t0:t0 + 128, :], vsb[:])
        for src, dst, sc in ((qsb, qo, qs_s), (ksb, ko, ks_s)):
            p.s("activation", sqq[:], src[:], AF.Square)
            for h in range(8):
                hs = slice(h * 64, (h + 1) * 64)
                p.v("reduce_sum", red[:, 0:1], sqq[:, hs], axis=AXX)
                p.s("activation", red[:, 0:1], red[:, 0:1], AF.Sqrt)
                p.v("tensor_scalar", red[:, 0:1], red[:, 0:1], 1e-12, None, OP.max)
                p.recip_pos(red[:, 1:2], red[:, 0:1])
                p.v("tensor_scalar", dst[:, hs], src[:, hs], red[:, 1:2], None,
                    OP.mult)
            p.v("tensor_tensor", dst[:], dst[:], sc[:], OP.mult)
        for src, csi, sni in ((qo, csq, ssq), (ko, csk, ssk)):
            p.dma(csT[:], csi[t0:t0 + 128, :])
            p.dma(snT[:], sni[t0:t0 + 128, :])
            for h in range(8):
                c0 = h * 64
                x1 = src[:, c0:c0 + 32]
                x2 = src[:, c0 + 32:c0 + 64]
                p.v("tensor_tensor", ra[:], x1, csT[:, 0:32], OP.mult)
                p.v("tensor_tensor", rb[:], x2, snT[:, 0:32], OP.mult)
                p.v("tensor_tensor", sqq[:, c0:c0 + 32], ra[:], rb[:], OP.subtract)
                p.v("tensor_tensor", ra[:], x2, csT[:, 32:64], OP.mult)
                p.v("tensor_tensor", rb[:], x1, snT[:, 32:64], OP.mult)
                p.v("tensor_tensor", sqq[:, c0 + 32:c0 + 64], ra[:], rb[:], OP.add)
            if src is qo:
                p.dma(qr[t0:t0 + 128, :], sqq[:])
            else:
                p.dma(kr[t0:t0 + 128, :], sqq[:])
    return p.build()


def run_ffnqkv(st, inp, layer, xi_key, out_prefix):
    nc = _prog_cache.get("ffnqkv")
    if nc is None:
        nc = _prog_cache["ffnqkv"] = build_ffnqkv()
    xi = st[xi_key]
    xiT = np.ascontiguousarray(xi.T)
    xp = np.zeros_like(xi)
    xp[1:] = xi[:-1]
    xp[0::T] = 0.0
    for bb in range(B):
        xp[bb * T] = 0.0
    xpT = np.ascontiguousarray(xp.T)
    csq, ssq, csk, ssk = _xpos_tables()
    wqkv = np.asarray(inp["L0_ffn_wqkv"][layer])
    qsv = np.tile(np.asarray(inp["L0_ffn_q_scale"][layer]), 8)
    ksv = np.tile(np.asarray(inp["L0_ffn_k_scale"][layer]), 8)
    maps = []
    for c in range(NCORE):
        t0 = (c % 2) * 2048
        sl = slice(c * TPC, (c + 1) * TPC)
        maps.append({
            "xiT": _f32(xiT[:, sl]),
            "xpT": _f32(xpT[:, sl]),
            "wqkv": _f32(wqkv),
            "qs": _rep(qsv),
            "ks": _rep(ksv),
            "csq": _f32(csq[t0:t0 + TPC]),
            "ssq": _f32(ssq[t0:t0 + TPC]),
            "csk": _f32(csk[t0:t0 + TPC]),
            "ssk": _f32(ssk[t0:t0 + TPC]),
        })
    res = run_bass_kernel_spmd(nc, maps, core_ids=list(range(NCORE)))
    st[out_prefix + "_q"] = np.concatenate([r["qr"] for r in res.results], axis=0)
    st[out_prefix + "_k"] = np.concatenate([r["kr"] for r in res.results], axis=0)
    st[out_prefix + "_v"] = np.concatenate([r["vr"] for r in res.results], axis=0)



def build_locattn():
    p = Prog()
    qf4 = p.inp("qf4", (256, T))
    kf4 = p.inp("kf4", (256, T))
    vt4 = p.inp("vt4", (4, T, 64))
    ident = p.inp("ident", (128, 128), BF16)
    mfirst = p.inp("mfirst", (128, 128))
    mlast = p.inp("mlast", (128, 128))
    Oout = p.outp("Oout", (4, T, 64))

    ids = p.sb((128, 128), BF16)
    p.dma(ids[:], ident[:])
    mf_sb = p.sb((128, 128))
    ml_sb = p.sb((128, 128))
    p.dma(mf_sb[:], mfirst[:])
    p.dma(ml_sb[:], mlast[:])
    q4f = p.sb((128, 2 * T))
    k4f = p.sb((128, 2 * T))
    for r in range(2):
        p.dma(q4f[:, r * T:(r + 1) * T], qf4[r * 128:(r + 1) * 128, :])
        p.dma(k4f[:, r * T:(r + 1) * T], kf4[r * 128:(r + 1) * 128, :])
    qb = p.sb((64, T))
    kb = p.sb((64, T))
    vtb = p.sb((128, 32 * 64), BF16)
    tf = p.sb((128, 64))
    psS = p.ps((128, 128))
    ppv = p.ps((128, 64))
    ptrb = p.ps((128, 128), BF16)
    wbuf = p.sb((128, 640))
    scr1 = p.sb((128, 128))
    ew = p.sb((128, 640), BF16)
    eT = p.sb((128, 128), BF16)
    red = p.sb((128, 3))
    osb = p.sb((128, 64))

    for j in range(4):
        p.v("tensor_copy", qb[:], q4f[(j % 2) * 64:(j % 2) * 64 + 64,
                                      (j // 2) * T:(j // 2 + 1) * T])
        p.v("tensor_copy", kb[:], k4f[(j % 2) * 64:(j % 2) * 64 + 64,
                                      (j // 2) * T:(j // 2 + 1) * T])
        for c in range(32):
            p.dma(tf[:], vt4[j, c * 128:(c + 1) * 128, :])
            p.v("tensor_copy", vtb[:, c * 64:(c + 1) * 64], tf[:])
        for qt in range(32):
            qtile = qb[:, qt * 128:(qt + 1) * 128]
            kt0 = max(0, qt - 4)
            nwt = qt - kt0 + 1
            wl = nwt * 128
            for idx in range(nwt):
                kt = kt0 + idx
                p.mm(psS[:], qtile, kb[:, kt * 128:(kt + 1) * 128],
                     start=True, stop=True)
                dst = wbuf[:, idx * 128:(idx + 1) * 128]
                p.s("mul", dst, psS[:], 8.0)
                if kt == qt:
                    mmk = ml_sb[:]
                elif qt >= 4 and idx == 0:
                    mmk = mf_sb[:]
                else:
                    continue
                p.v("tensor_tensor", dst, dst, mmk, OP.mult)
                p.v("tensor_scalar", scr1[:], mmk, 1.0, None, OP.subtract)
                p.v("scalar_tensor_tensor", dst, scr1[:], 1e30, dst,
                    OP.mult, OP.add)
            p.v("reduce_max", red[:, 0:1], wbuf[:, 0:wl], axis=AXX)
            p.v("tensor_scalar", red[:, 1:2], red[:, 0:1], -1.0, None, OP.mult)
            p.s("activation", ew[:, 0:wl], wbuf[:, 0:wl], AF.Exp, bias=red[:, 1:2])
            p.v("reduce_sum", red[:, 2:3], ew[:, 0:wl], axis=AXX)
            p.recip_pos(red[:, 2:3], red[:, 2:3])
            for idx in range(nwt):
                kt = kt0 + idx
                p.t("transpose", ptrb[:], ew[:, idx * 128:(idx + 1) * 128], ids[:])
                p.v("tensor_copy", eT[:], ptrb[:])
                p.mm(ppv[:], eT[:], vtb[:, kt * 64:(kt + 1) * 64],
                     start=(idx == 0), stop=(idx == nwt - 1))
            p.v("tensor_scalar", osb[:], ppv[:], red[:, 2:3], None, OP.mult)
            p.dma(Oout[j, qt * 128:(qt + 1) * 128, :], osb[:])
    return p.build()


def run_locattn(st, prefix):
    nc = _prog_cache.get("locattn")
    if nc is None:
        nc = _prog_cache["locattn"] = build_locattn()
    qn = st[prefix + "_q"]
    kn = st[prefix + "_k"]
    v = st[prefix + "_v"]
    i = np.arange(128)
    mfirst = _f32(i[None, :] >= i[:, None])
    mlast = _f32(i[None, :] <= i[:, None])
    ident = np.eye(128, dtype=BF)
    maps = []
    for c in range(NCORE):
        b = c // 2
        h0 = (c % 2) * 4
        bs = slice(b * T, (b + 1) * T)
        maps.append({
            "qf4": _f32(qn[bs, h0 * 64:(h0 + 4) * 64].T),
            "kf4": _f32(kn[bs, h0 * 64:(h0 + 4) * 64].T),
            "vt4": _f32(v[bs].reshape(T, 8, 64).transpose(1, 0, 2)[h0:h0 + 4]),
            "ident": ident,
            "mfirst": mfirst,
            "mlast": mlast,
        })
    res = run_bass_kernel_spmd(nc, maps, core_ids=list(range(NCORE)))
    O = np.zeros((B, 8, T, 64), dtype=np.float32)
    for c in range(NCORE):
        O[c // 2, (c % 2) * 4:(c % 2) * 4 + 4] = res.results[c]["Oout"]
    st[prefix + "_o"] = O



HALO = 128
HTOT = TPC + HALO

def build_mlp():
    p = Prog()
    xiT = p.inp("xiT", (D, HTOT))
    ogT = p.inp("ogT", (512, HTOT))
    wo = p.inp("wo", (512, D))
    lng = p.inp("lng", (D, 1))
    lnb = p.inp("lnb", (D, 1))
    w1a = p.inp("w1a", (D, 2816))
    w1g = p.inp("w1g", (D, 2816))
    b1a = p.inp("b1a", (2816, 1))
    b1g = p.inp("b1g", (2816, 1))
    w2p = p.inp("w2p", (2816, D))
    b2 = p.inp("b2", (D, 1))
    ones1 = p.inp("ones1", (1, 128))
    onescol = p.inp("onescol", (128, 1))
    xo = p.outp("xo", (D, TPC))

    o1_sb = p.sb((1, 128))
    oc_sb = p.sb((128, 1))
    p.dma(o1_sb[:], ones1[:])
    p.dma(oc_sb[:], onescol[:])
    lng_sb = p.sb((128, 8))
    lnb_sb = p.sb((128, 8))
    b1a_sb = p.sb((128, 22))
    b1g_sb = p.sb((128, 22))
    b2_sb = p.sb((128, 8))
    for c in range(8):
        p.dma(lng_sb[:, c:c + 1], lng[c * 128:(c + 1) * 128, :])
        p.dma(lnb_sb[:, c:c + 1], lnb[c * 128:(c + 1) * 128, :])
        p.dma(b2_sb[:, c:c + 1], b2[c * 128:(c + 1) * 128, :])
    for c in range(22):
        p.dma(b1a_sb[:, c:c + 1], b1a[c * 128:(c + 1) * 128, :])
        p.dma(b1g_sb[:, c:c + 1], b1g[c * 128:(c + 1) * 128, :])
    wo_b = p.sb((128, 4 * D), BF16)
    wt2 = p.sb((128, D))
    for kc in range(4):
        p.dma(wt2[:], wo[kc * 128:(kc + 1) * 128, :])
        p.v("tensor_copy", wo_b[:, kc * D:(kc + 1) * D], wt2[:])

    x1sb = p.sb((128, 8 * HTOT))
    ogb = p.sb((128, 4 * 512), BF16)
    ogf = p.sb((128, 512))
    xif = p.sb((128, 512))
    pz = p.ps((128, 512))
    tiles = [(0, HALO)] + [(HALO + i * 512, 512) for i in range(4)]
    for (c0, w) in tiles:
        cs = slice(c0, c0 + w)
        for kc in range(4):
            p.dma(ogf[:, 0:w], ogT[kc * 128:(kc + 1) * 128, cs])
            p.v("tensor_copy", ogb[:, kc * 512:kc * 512 + w], ogf[:, 0:w])
        for mc in range(8):
            for kc in range(4):
                p.mm(pz[:, 0:w], wo_b[:, kc * D + mc * 128:kc * D + (mc + 1) * 128],
                     ogb[:, kc * 512:kc * 512 + w], start=(kc == 0), stop=(kc == 3))
            p.dma(xif[:, 0:w], xiT[mc * 128:(mc + 1) * 128, cs])
            p.v("tensor_tensor", x1sb[:, mc * HTOT + c0:mc * HTOT + c0 + w],
                pz[:, 0:w], xif[:, 0:w], OP.add)

    pS1 = p.ps((1, 512))
    pS2 = p.ps((1, 512))
    pA = p.ps((128, 512))
    pB = p.ps((128, 512))
    pH = p.ps((128, 512))
    sqf = p.sb((128, 512))
    srow = p.sb((1, 512))
    arow = p.sb((1, 512))
    brow = p.sb((1, 512))
    m2r = p.sb((1, 512))
    xnf = p.sb((128, 512))
    xnb_all = p.sb((128, 8 * 512), BF16)
    wtile = p.sb((128, 8 * 128))
    wbt = p.sb((128, 8 * 128), BF16)
    w2tile = p.sb((128, 22 * 128))
    w2bt = p.sb((128, 22 * 128), BF16)
    haf = p.sb((128, 512))
    gel = p.sb((128, 512))
    ghb = p.sb((128, 22 * 512), BF16)
    xof = p.sb((128, 512))

    def h2slice(c, rel0, w):
        off = rel0 if c < 4 else rel0 - 1
        return x1sb[:, c * HTOT + off:c * HTOT + off + w]

    for ti in range(4):
        rel0 = HALO + ti * 512
        for c in range(8):
            p.mm(pS1[:], oc_sb[:], h2slice(c, rel0, 512), start=(c == 0),
                 stop=(c == 7))
        for c in range(8):
            p.s("activation", sqf[:], h2slice(c, rel0, 512), AF.Square)
            p.mm(pS2[:], oc_sb[:], sqf[:], start=(c == 0), stop=(c == 7))
        p.v("tensor_scalar", srow[:], pS1[:], 1.0 / D, None, OP.mult)
        p.v("tensor_scalar", arow[:], pS2[:], 1.0 / D, None, OP.mult)
        p.s("activation", m2r[:], srow[:], AF.Square)
        p.v("tensor_tensor", arow[:], arow[:], m2r[:], OP.subtract)
        p.rsqrt_pos(arow[:], arow[:], 1e-5)
        p.v("tensor_tensor", brow[:], srow[:], arow[:], OP.mult)
        p.v("tensor_scalar", brow[:], brow[:], -1.0, None, OP.mult)
        p.mm(pA[:], o1_sb[:], arow[:], start=True, stop=True)
        p.mm(pB[:], o1_sb[:], brow[:], start=True, stop=True)
        for c in range(8):
            p.v("tensor_tensor", xnf[:], h2slice(c, rel0, 512), pA[:], OP.mult)
            p.v("tensor_tensor", xnf[:], xnf[:], pB[:], OP.add)
            p.v("tensor_scalar", xnf[:], xnf[:], lng_sb[:, c:c + 1], None, OP.mult)
            p.v("tensor_scalar", xnf[:], xnf[:], lnb_sb[:, c:c + 1], None, OP.add)
            p.v("tensor_copy", xnb_all[:, c * 512:(c + 1) * 512], xnf[:])
        for mcb in range(22):
            for wsrc, bcol, isgelu in ((w1a, b1a_sb, False), (w1g, b1g_sb, True)):
                for c in range(8):
                    p.dma(wtile[:, c * 128:(c + 1) * 128],
                          wsrc[c * 128:(c + 1) * 128, mcb * 128:(mcb + 1) * 128])
                p.v("tensor_copy", wbt[:], wtile[:])
                for c in range(8):
                    p.mm(pH[:], wbt[:, c * 128:(c + 1) * 128],
                         xnb_all[:, c * 512:(c + 1) * 512],
                         start=(c == 0), stop=(c == 7))
                if isgelu:
                    p.s("activation", gel[:], pH[:], AF.Gelu,
                        bias=b1g_sb[:, mcb:mcb + 1])
                else:
                    p.v("tensor_scalar", haf[:], pH[:], b1a_sb[:, mcb:mcb + 1],
                        None, OP.add)
            p.v("tensor_tensor", ghb[:, mcb * 512:(mcb + 1) * 512], haf[:],
                gel[:], OP.mult)
        for mc2 in range(8):
            for kc in range(22):
                p.dma(w2tile[:, kc * 128:(kc + 1) * 128],
                      w2p[kc * 128:(kc + 1) * 128, mc2 * 128:(mc2 + 1) * 128])
            p.v("tensor_copy", w2bt[:], w2tile[:])
            for kc in range(22):
                p.mm(pH[:], w2bt[:, kc * 128:(kc + 1) * 128],
                     ghb[:, kc * 512:(kc + 1) * 512], start=(kc == 0),
                     stop=(kc == 21))
            p.v("tensor_tensor", xof[:], pH[:],
                x1sb[:, mc2 * HTOT + rel0:mc2 * HTOT + rel0 + 512], OP.add)
            p.v("tensor_scalar", xof[:], xof[:], b2_sb[:, mc2:mc2 + 1], None,
                OP.add)
            p.dma(xo[mc2 * 128:(mc2 + 1) * 128, ti * 512:(ti + 1) * 512], xof[:])
    return p.build()


def run_mlp(st, inp, layer, xi_key, o_key, out_key):
    nc = _prog_cache.get("mlp")
    if nc is None:
        nc = _prog_cache["mlp"] = build_mlp()
    xi = st[xi_key]
    O = st[o_key]
    xiT = np.ascontiguousarray(xi.T)
    OgT = np.ascontiguousarray(
        O.transpose(0, 1, 3, 2).reshape(B, 512, T))
    HID = 2730
    w1 = np.asarray(inp["L0_ffn_w1"][layer])
    b1 = np.asarray(inp["L0_ffn_b1"][layer])
    w2 = np.asarray(inp["L0_ffn_w2"][layer])
    b2 = np.asarray(inp["L0_ffn_b2"][layer])
    w1a = np.zeros((D, 2816), np.float32); w1a[:, :HID] = w1[:, :HID]
    w1g = np.zeros((D, 2816), np.float32); w1g[:, :HID] = w1[:, HID:]
    b1a = np.zeros((2816, 1), np.float32); b1a[:HID, 0] = b1[:HID]
    b1g = np.zeros((2816, 1), np.float32); b1g[:HID, 0] = b1[HID:]
    w2p = np.zeros((2816, D), np.float32); w2p[:HID] = w2
    maps = []
    for c in range(NCORE):
        b = c // 2
        t0g = c * TPC
        xiT_h = np.zeros((D, HTOT), np.float32)
        ogT_h = np.zeros((512, HTOT), np.float32)
        xiT_h[:, HALO:] = xiT[:, t0g:t0g + TPC]
        t0b = (c % 2) * 2048
        ogT_h[:, HALO:] = OgT[b][:, t0b:t0b + TPC]
        if c % 2 == 1:
            xiT_h[:, :HALO] = xiT[:, t0g - HALO:t0g]
            ogT_h[:, :HALO] = OgT[b][:, t0b - HALO:t0b]
        maps.append({
            "xiT": _f32(xiT_h), "ogT": _f32(ogT_h),
            "wo": _f32(inp["L0_ffn_wo"][layer]),
            "lng": _f32(np.asarray(inp["L0_ffn_ln_g"][layer]).reshape(D, 1)),
            "lnb": _f32(np.asarray(inp["L0_ffn_ln_b"][layer]).reshape(D, 1)),
            "w1a": w1a, "w1g": w1g, "b1a": b1a, "b1g": b1g, "w2p": w2p,
            "b2": _f32(b2.reshape(D, 1)),
            "ones1": np.ones((1, 128), np.float32),
            "onescol": np.ones((128, 1), np.float32),
        })
    res = run_bass_kernel_spmd(nc, maps, core_ids=list(range(NCORE)))
    xoT = np.concatenate([r["xo"] for r in res.results], axis=1)
    st[out_key] = np.ascontiguousarray(xoT.T)



def build_comb0():
    p = Prog()
    x = p.inp("x", (TPC, D))
    a = p.inp("a", (TPC, D))
    f = p.inp("f", (TPC, D))
    x1 = p.outp("x1", (TPC, D))
    xn = p.outp("xn", (TPC, D))
    xs = p.sb((128, D)); as_ = p.sb((128, D)); fs = p.sb((128, D))
    xc = p.sb((128, D)); sq = p.sb((128, D)); st = p.sb((128, 4))
    for i in range(TPC // 128):
        sl = slice(i * 128, (i + 1) * 128)
        p.dma(xs[:], x[sl, :]); p.dma(as_[:], a[sl, :]); p.dma(fs[:], f[sl, :])
        p.v("tensor_tensor", xs[:], xs[:], as_[:], OP.add)
        p.v("tensor_tensor", xs[:], xs[:], fs[:], OP.add)
        p.dma(x1[sl, :], xs[:])
        p.v("reduce_sum", st[:, 0:1], xs[:], axis=AXX)
        p.v("tensor_scalar", st[:, 1:2], st[:, 0:1], 1.0 / D, None, OP.mult)
        p.v("tensor_scalar", xc[:], xs[:], st[:, 1:2], None, OP.subtract)
        p.s("activation", sq[:], xc[:], AF.Square)
        p.v("reduce_sum", st[:, 2:3], sq[:], axis=AXX)
        p.v("tensor_scalar", st[:, 2:3], st[:, 2:3], 1.0 / D, None, OP.mult)
        p.rsqrt_pos(st[:, 3:4], st[:, 2:3], 1e-5)
        p.v("tensor_scalar", xc[:], xc[:], st[:, 3:4], None, OP.mult)
        p.dma(xn[sl, :], xc[:])
    return p.build()


def run_comb0(st):
    nc = _prog_cache.get("comb0")
    if nc is None:
        nc = _prog_cache["comb0"] = build_comb0()
    attn0 = np.ascontiguousarray(st["attn0"].T)
    maps = []
    for c in range(NCORE):
        sl = slice(c * TPC, (c + 1) * TPC)
        maps.append({"x": _f32(st["x0"][sl]), "a": _f32(attn0[sl]),
                     "f": _f32(st["ffn0"][sl])})
    res = run_bass_kernel_spmd(nc, maps, core_ids=list(range(NCORE)))
    st["x1"] = np.concatenate([r["x1"] for r in res.results], axis=0)
    st["xn1"] = np.concatenate([r["xn"] for r in res.results], axis=0)



def build_l1proj():
    p = Prog()
    xnT = p.inp("xnT", (D, TPC))
    wqT = p.inp("wqT", (D, D))
    wkT = p.inp("wkT", (D, D))
    wvT = p.inp("wvT", (D, D))
    bq = p.inp("bq", (128, D))
    bk = p.inp("bk", (128, D))
    bv = p.inp("bv", (128, D))
    grep = p.inp("grep", (128, D))
    brep = p.inp("brep", (128, D))
    lrwT = p.inp("lrwT", (D, 16))
    lrb = p.inp("lrb", (128, 16))
    XQ = p.outp("XQ", (TPC, D))
    XK = p.outp("XK", (TPC, D))
    XV = p.outp("XV", (TPC, D))
    ETA = p.outp("ETA", (TPC, 16))

    wq_b = p.sb((128, 8 * D), BF16)
    wk_b = p.sb((128, 8 * D), BF16)
    wv_b = p.sb((128, 8 * D), BF16)
    lrw_s = p.sb((128, 8 * 16))
    wt = p.sb((128, D))
    for c in range(8):
        p.dma(wt[:], wqT[c * 128:(c + 1) * 128, :])
        p.v("tensor_copy", wq_b[:, c * D:(c + 1) * D], wt[:])
        p.dma(wt[:], wkT[c * 128:(c + 1) * 128, :])
        p.v("tensor_copy", wk_b[:, c * D:(c + 1) * D], wt[:])
        p.dma(wt[:], wvT[c * 128:(c + 1) * 128, :])
        p.v("tensor_copy", wv_b[:, c * D:(c + 1) * D], wt[:])
        p.dma(lrw_s[:, c * 16:(c + 1) * 16], lrwT[c * 128:(c + 1) * 128, :])
    bq_s = p.sb((128, D)); bk_s = p.sb((128, D)); bv_s = p.sb((128, D))
    g_s = p.sb((128, D)); b_s = p.sb((128, D)); lrb_s = p.sb((128, 16))
    p.dma(bq_s[:], bq[:]); p.dma(bk_s[:], bk[:]); p.dma(bv_s[:], bv[:])
    p.dma(g_s[:], grep[:]); p.dma(b_s[:], brep[:]); p.dma(lrb_s[:], lrb[:])

    xb = p.sb((128, 8 * 128), BF16)
    xf = p.sb((128, 8 * 128))
    pq = p.ps((128, 512))
    pe = p.ps((128, 16))
    qsb = p.sb((128, D)); ksb = p.sb((128, D)); vsb = p.sb((128, D))
    sqq = p.sb((128, D))
    red = p.sb((128, 4))
    dsb = p.sb((128, D))
    esb = p.sb((128, 16))

    for tt in range(TPC // 128):
        t0 = tt * 128
        for c in range(8):
            p.dma(xf[:, c * 128:(c + 1) * 128], xnT[c * 128:(c + 1) * 128,
                                                    t0:t0 + 128])
        p.v("tensor_copy", xb[:], xf[:])
        for wsel, bsel, dst in ((wq_b, bq_s, qsb), (wk_b, bk_s, ksb),
                                (wv_b, bv_s, vsb)):
            for half in range(2):
                for c in range(8):
                    p.mm(pq[:], xb[:, c * 128:(c + 1) * 128],
                         wsel[:, c * D + half * 512:c * D + (half + 1) * 512],
                         start=(c == 0), stop=(c == 7))
                p.v("tensor_copy", dst[:, half * 512:(half + 1) * 512], pq[:])
            p.v("tensor_tensor", dst[:], dst[:], bsel[:], OP.add)
        for src in (qsb, ksb):
            p.s("activation", sqq[:], src[:], AF.Square)
            for h in range(16):
                hs = slice(h * 64, (h + 1) * 64)
                p.v("reduce_sum", red[:, 0:1], sqq[:, hs], axis=AXX)
                p.s("activation", red[:, 0:1], red[:, 0:1], AF.Sqrt)
                p.v("tensor_scalar", red[:, 0:1], red[:, 0:1], 1e-12, None, OP.max)
                p.recip_pos(red[:, 1:2], red[:, 0:1])
                p.v("tensor_scalar", src[:, hs], src[:, hs], red[:, 1:2], None,
                    OP.mult)
        p.dma(XQ[t0:t0 + 128, :], qsb[:])
        p.dma(XK[t0:t0 + 128, :], ksb[:])
        p.v("tensor_tensor", dsb[:], vsb[:], ksb[:], OP.subtract)
        for h in range(16):
            hs = slice(h * 64, (h + 1) * 64)
            p.v("reduce_sum", red[:, 0:1], dsb[:, hs], axis=AXX)
            p.v("tensor_scalar", red[:, 0:1], red[:, 0:1], 1.0 / 64, None, OP.mult)
            p.v("tensor_scalar", dsb[:, hs], dsb[:, hs], red[:, 0:1], None,
                OP.subtract)
            p.s("activation", sqq[:, hs], dsb[:, hs], AF.Square)
            p.v("reduce_sum", red[:, 1:2], sqq[:, hs], axis=AXX)
            p.v("tensor_scalar", red[:, 1:2], red[:, 1:2], 1.0 / 63, None, OP.mult)
            p.s("activation", red[:, 1:2], red[:, 1:2], AF.Sqrt)
            p.v("tensor_scalar", red[:, 1:2], red[:, 1:2], 1e-5, None, OP.add)
            p.recip_pos(red[:, 2:3], red[:, 1:2])
            p.v("tensor_scalar", dsb[:, hs], dsb[:, hs], red[:, 2:3], None,
                OP.mult)
        p.v("tensor_tensor", dsb[:], dsb[:], g_s[:], OP.mult)
        p.v("tensor_tensor", dsb[:], dsb[:], b_s[:], OP.add)
        p.v("tensor_tensor", dsb[:], dsb[:], ksb[:], OP.add)
        p.dma(XV[t0:t0 + 128, :], dsb[:])
        for c in range(8):
            p.mm(pe[:], xf[:, c * 128:(c + 1) * 128], lrw_s[:, c * 16:(c + 1) * 16],
                 start=(c == 0), stop=(c == 7))
        p.v("tensor_tensor", esb[:], pe[:], lrb_s[:], OP.add)
        p.s("activation", esb[:], esb[:], AF.Sigmoid)
        p.v("tensor_scalar", esb[:], esb[:], 1.0 / 4096.0, None, OP.mult)
        p.dma(ETA[t0:t0 + 128, :], esb[:])
    return p.build()


def run_l1proj(st, inp):
    nc = _prog_cache.get("l1proj")
    if nc is None:
        nc = _prog_cache["l1proj"] = build_l1proj()
    xnT = np.ascontiguousarray(st["xn1"].T)
    gw = np.asarray(inp["L1_attn_ttt_norm_weight"]).reshape(-1)
    bw = np.asarray(inp["L1_attn_ttt_norm_bias"]).reshape(-1)
    lrw = np.asarray(inp["L1_attn_ttt_lr_weight"]).reshape(16, D)
    maps = []
    for c in range(NCORE):
        sl = slice(c * TPC, (c + 1) * TPC)
        maps.append({
            "xnT": _f32(xnT[:, sl]),
            "wqT": _f32(np.asarray(inp["L1_attn_wq_w"]).T),
            "wkT": _f32(np.asarray(inp["L1_attn_wk_w"]).T),
            "wvT": _f32(np.asarray(inp["L1_attn_wv_w"]).T),
            "bq": _rep(inp["L1_attn_wq_b"]),
            "bk": _rep(inp["L1_attn_wk_b"]),
            "bv": _rep(inp["L1_attn_wv_b"]),
            "grep": _rep(gw), "brep": _rep(bw),
            "lrwT": _f32(lrw.T),
            "lrb": _rep(np.asarray(inp["L1_attn_ttt_lr_bias"]).reshape(16)),
        })
    res = run_bass_kernel_spmd(nc, maps, core_ids=list(range(NCORE)))
    st["XQ"] = np.concatenate([r["XQ"] for r in res.results], axis=0)
    st["XK"] = np.concatenate([r["XK"] for r in res.results], axis=0)
    st["XV"] = np.concatenate([r["XV"] for r in res.results], axis=0)
    st["ETA"] = np.concatenate([r["ETA"] for r in res.results], axis=0)



def build_l1scan():
    p = Prog()
    XQs = p.inp("XQs", (4, 64, 128, 64))
    XKs = p.inp("XKs", (4, 64, 128, 64))
    XVs = p.inp("XVs", (4, 64, 128, 64))
    ETAs = p.inp("ETAs", (4, 64, 128, 1))
    W1s = p.inp("W1s", (4, 128, 64))
    b1s = p.inp("b1s", (4, 2, 64))
    GR = p.inp("GR", (4, 128, 64))
    BR = p.inp("BR", (4, 128, 64))
    tril2 = p.inp("tril2", (128, 64))
    TRIL2T = p.inp("TRIL2T", (128, 128), BF16)
    CB = p.inp("CB", (2, 128), BF16)
    CBL = p.inp("CBL", (128, 2), BF16)
    ident = p.inp("ident", (128, 128), BF16)
    Y = p.outp("Y", (4, 64, 128, 64))

    ids = p.sb((128, 128), BF16)
    p.dma(ids[:], ident[:])
    tril_s = p.sb((128, 64))
    p.dma(tril_s[:], tril2[:])
    T2T_s = p.sb((128, 128), BF16)
    p.dma(T2T_s[:], TRIL2T[:])
    CB_s = p.sb((2, 128), BF16)
    CBL_s = p.sb((128, 2), BF16)
    p.dma(CB_s[:], CB[:])
    p.dma(CBL_s[:], CBL[:])
    Wst = [p.sb((128, 64)) for _ in range(4)]
    Wb = p.sb((128, 64), BF16)
    bst = [p.sb((2, 64)) for _ in range(4)]
    bb = p.sb((2, 64), BF16)
    gr_s = p.sb((128, 4 * 64))
    br_s = p.sb((128, 4 * 64))
    for g in range(4):
        p.dma(Wst[g][:], W1s[g, :, :])
        p.dma(bst[g][:], b1s[g, :, :])
        p.dma(gr_s[:, g * 64:(g + 1) * 64], GR[g, :, :])
        p.dma(br_s[:, g * 64:(g + 1) * 64], BR[g, :, :])

    xqf = p.sb((128, 64)); xkf = p.sb((128, 64)); xvf = p.sb((128, 64))
    etf = p.sb((128, 1))
    xqb = p.sb((128, 64), BF16); xkb = p.sb((128, 64), BF16)
    bd1 = p.sb((128, 128), BF16); bd2 = p.sb((128, 128), BF16)
    bd3 = p.sb((128, 128), BF16); bdA = p.sb((128, 128), BF16)
    p.v("memset", bd1[:], 0.0)
    p.v("memset", bd2[:], 0.0)
    p.v("memset", bd3[:], 0.0)
    p.v("memset", bdA[:], 0.0)
    ptrb = p.ps((128, 128), BF16)
    xkT_bd = p.sb((128, 128), BF16); xqT_bd = p.sb((128, 128), BF16)
    xkT_st = p.sb((128, 64), BF16)
    pZ = p.ps((128, 64))
    pZb = p.ps((128, 64))
    pS64 = p.ps((128, 64))
    pW = p.ps((128, 64))
    pB2 = p.ps((2, 64))
    rm = p.sb((128, 8))
    xcf = p.sb((128, 64)); sqf = p.sb((128, 64)); xhf = p.sb((128, 64))
    gof = p.sb((128, 64)); tvf = p.sb((128, 64)); gxf = p.sb((128, 64))
    t1f = p.sb((128, 64)); t2f = p.sb((128, 64)); grf = p.sb((128, 64))
    Gf = p.sb((128, 64)); Gnb = p.sb((128, 64), BF16)
    grb = p.sb((128, 64), BF16); Gb = p.sb((128, 64), BF16)
    af = p.sb((128, 64)); ab = p.sb((128, 64), BF16)
    AT_bd = p.sb((128, 128), BF16)
    gkf = p.sb((128, 64)); gkb = p.sb((128, 64), BF16)
    osb = p.sb((128, 64))

    for g in range(4):
        gs = slice(g * 64, (g + 1) * 64)
        grep = gr_s[:, gs]
        brep = br_s[:, gs]
        p.v("tensor_copy", Wb[:], Wst[g][:])
        p.v("tensor_copy", bb[:], bst[g][:])
        for n in range(64):
            p.dma(xqf[:], XQs[g, n, :, :])
            p.dma(xkf[:], XKs[g, n, :, :])
            p.dma(xvf[:], XVs[g, n, :, :])
            p.dma(etf[:], ETAs[g, n, :, :])
            p.v("tensor_copy", xqb[:], xqf[:])
            p.v("tensor_copy", xkb[:], xkf[:])
            p.v("tensor_copy", bd1[0:64, 0:64], xkb[0:64, :])
            p.v("tensor_copy", bd1[64:128, 64:128], xkb[64:128, :])
            p.t("transpose", ptrb[:], bd1[:], ids[:])
            p.v("tensor_copy", xkT_bd[:], ptrb[:])
            p.v("tensor_copy", bd2[0:64, 0:64], xqb[0:64, :])
            p.v("tensor_copy", bd2[64:128, 64:128], xqb[64:128, :])
            p.t("transpose", ptrb[:], bd2[:], ids[:])
            p.v("tensor_copy", xqT_bd[:], ptrb[:])
            p.v("tensor_copy", xkT_st[0:64, :], xkT_bd[0:64, 0:64])
            p.v("tensor_copy", xkT_st[64:128, :], xkT_bd[64:128, 64:128])
            p.mm(pZ[:], xkT_bd[:], Wb[:], start=True, stop=False)
            p.mm(pZ[:], CB_s[:], bb[:], start=False, stop=True)
            p.v("reduce_sum", rm[:, 0:1], pZ[:], axis=AXX)
            p.v("tensor_scalar", rm[:, 0:1], rm[:, 0:1], 1.0 / 64, None, OP.mult)
            p.v("tensor_scalar", xcf[:], pZ[:], rm[:, 0:1], None, OP.subtract)
            p.s("activation", sqf[:], xcf[:], AF.Square)
            p.v("reduce_sum", rm[:, 1:2], sqf[:], axis=AXX)
            p.v("tensor_scalar", rm[:, 1:2], rm[:, 1:2], 1.0 / 64, None, OP.mult)
            p.rsqrt_pos(rm[:, 2:3], rm[:, 1:2], 1e-6)
            p.v("tensor_scalar", xhf[:], xcf[:], rm[:, 2:3], None, OP.mult)
            p.v("tensor_tensor", gof[:], xhf[:], grep, OP.mult)
            p.v("tensor_tensor", gof[:], gof[:], brep, OP.add)
            p.v("tensor_tensor", tvf[:], xvf[:], xkf[:], OP.subtract)
            p.v("tensor_tensor", gof[:], gof[:], tvf[:], OP.subtract)
            p.v("tensor_tensor", gxf[:], gof[:], grep, OP.mult)
            p.v("reduce_sum", rm[:, 3:4], gxf[:], axis=AXX)
            p.v("tensor_tensor", t1f[:], gxf[:], xhf[:], OP.mult)
            p.v("reduce_sum", rm[:, 4:5], t1f[:], axis=AXX)
            p.v("tensor_scalar", t1f[:], xhf[:], rm[:, 4:5], None, OP.mult)
            p.v("tensor_scalar", t2f[:], gxf[:], 64.0, None, OP.mult)
            p.v("tensor_scalar", t2f[:], t2f[:], rm[:, 3:4], None, OP.subtract)
            p.v("tensor_tensor", t2f[:], t2f[:], t1f[:], OP.subtract)
            p.v("tensor_scalar", rm[:, 5:6], rm[:, 2:3], 1.0 / 64, None, OP.mult)
            p.v("tensor_scalar", grf[:], t2f[:], rm[:, 5:6], None, OP.mult)
            p.v("tensor_scalar", Gf[:], grf[:], etf[:, 0:1], None, OP.mult)
            p.v("tensor_scalar", t1f[:], Gf[:], -1.0, None, OP.mult)
            p.v("tensor_copy", Gnb[:], t1f[:])
            p.v("tensor_copy", Gb[:], Gf[:])
            p.v("tensor_copy", grb[:], grf[:])
            p.mm(pS64[:], xqT_bd[:], xkT_st[:], start=True, stop=True)
            p.v("tensor_tensor", af[:], pS64[:], tril_s[:], OP.mult)
            p.v("tensor_copy", ab[:], af[:])
            p.v("tensor_copy", bdA[0:64, 0:64], ab[0:64, :])
            p.v("tensor_copy", bdA[64:128, 64:128], ab[64:128, :])
            p.t("transpose", ptrb[:], bdA[:], ids[:])
            p.v("tensor_copy", AT_bd[:], ptrb[:])
            p.mm(pZb[:], xqT_bd[:], Wb[:], start=True, stop=False)
            p.mm(pZb[:], CB_s[:], bb[:], start=False, stop=False)
            p.mm(pZb[:], T2T_s[:], Gnb[:], start=False, stop=False)
            p.mm(pZb[:], AT_bd[:], Gnb[:], start=False, stop=True)
            p.v("reduce_sum", rm[:, 0:1], pZb[:], axis=AXX)
            p.v("tensor_scalar", rm[:, 0:1], rm[:, 0:1], 1.0 / 64, None, OP.mult)
            p.v("tensor_scalar", xcf[:], pZb[:], rm[:, 0:1], None, OP.subtract)
            p.s("activation", sqf[:], xcf[:], AF.Square)
            p.v("reduce_sum", rm[:, 1:2], sqf[:], axis=AXX)
            p.v("tensor_scalar", rm[:, 1:2], rm[:, 1:2], 1.0 / 64, None, OP.mult)
            p.rsqrt_pos(rm[:, 2:3], rm[:, 1:2], 1e-6)
            p.v("tensor_scalar", xhf[:], xcf[:], rm[:, 2:3], None, OP.mult)
            p.v("tensor_tensor", osb[:], xhf[:], grep, OP.mult)
            p.v("tensor_tensor", osb[:], osb[:], brep, OP.add)
            p.v("tensor_tensor", osb[:], osb[:], xqf[:], OP.add)
            p.dma(Y[g, n, :, :], osb[:])
            p.v("tensor_scalar", gkf[:], xkf[:], etf[:, 0:1], None, OP.mult)
            p.v("tensor_copy", gkb[:], gkf[:])
            p.v("tensor_copy", bd3[0:64, 0:64], gkb[0:64, :])
            p.v("tensor_copy", bd3[64:128, 64:128], gkb[64:128, :])
            p.mm(pW[:], bd3[:], grb[:], start=True, stop=True)
            p.v("tensor_tensor", Wst[g][:], Wst[g][:], pW[:], OP.subtract)
            p.v("tensor_copy", Wb[:], Wst[g][:])
            p.mm(pB2[:], CBL_s[:], Gb[:], start=True, stop=True)
            p.v("tensor_tensor", bst[g][:], bst[g][:], pB2[:], OP.subtract)
            p.v("tensor_copy", bb[:], bst[g][:])
    return p.build()


def run_l1scan(st, inp):
    nc = _prog_cache.get("l1scan")
    if nc is None:
        nc = _prog_cache["l1scan"] = build_l1scan()
    H, C = 16, 64
    NC_ = T // C
    XQ = st["XQ"].reshape(B, T, H, 64)
    XK = st["XK"].reshape(B, T, H, 64)
    XV = st["XV"].reshape(B, T, H, 64)
    ETA = st["ETA"].reshape(B, T, H)
    W1 = np.asarray(inp["L1_attn_W1"])
    b1 = np.asarray(inp["L1_attn_b1"])
    gw = np.asarray(inp["L1_attn_ttt_norm_weight"])
    bw = np.asarray(inp["L1_attn_ttt_norm_bias"])
    i64 = np.arange(64)
    trilI = (i64[None, :] <= i64[:, None]).astype(np.float32)
    tril2 = _f32(np.concatenate([trilI, trilI], axis=0))
    TT = np.zeros((128, 128), np.float32)
    up = (i64[:, None] <= i64[None, :]).astype(np.float32)
    TT[0:64, 0:64] = up
    TT[64:128, 64:128] = up
    CBv = np.zeros((2, 128), np.float32)
    CBv[0, 0:64] = 1.0
    CBv[1, 64:128] = 1.0
    maps = []
    for c in range(NCORE):
        xq = np.zeros((4, 64, 128, 64), np.float32)
        xk = np.zeros_like(xq); xv = np.zeros_like(xq)
        et = np.zeros((4, 64, 128, 1), np.float32)
        W1s = np.zeros((4, 128, 64), np.float32)
        b1s = np.zeros((4, 2, 64), np.float32)
        GRs = np.zeros((4, 128, 64), np.float32)
        BRs = np.zeros((4, 128, 64), np.float32)
        for g in range(4):
            for pi in range(2):
                pair = c * 8 + g * 2 + pi
                b = pair // 16
                h = pair % 16
                rows = slice(pi * 64, (pi + 1) * 64)
                for n in range(NC_):
                    ts_ = slice(n * 64, (n + 1) * 64)
                    xq[g, n, rows] = XQ[b, ts_, h]
                    xk[g, n, rows] = XK[b, ts_, h]
                    xv[g, n, rows] = XV[b, ts_, h]
                    et[g, n, rows, 0] = ETA[b, ts_, h]
                W1s[g, rows] = W1[h]
                b1s[g, pi] = b1[h, 0]
                GRs[g, rows] = np.broadcast_to(gw[h], (64, 64))
                BRs[g, rows] = np.broadcast_to(bw[h], (64, 64))
        maps.append({
            "XQs": xq, "XKs": xk, "XVs": xv, "ETAs": et,
            "W1s": W1s, "b1s": b1s, "GR": GRs, "BR": BRs,
            "tril2": tril2, "TRIL2T": _bf(TT), "CB": _bf(CBv),
            "CBL": _bf(CBv.T), "ident": np.eye(128, dtype=BF),
        })
    res = run_bass_kernel_spmd(nc, maps, core_ids=list(range(NCORE)))
    Yf = np.zeros((B, H, T, 64), np.float32)
    for c in range(NCORE):
        Yc = res.results[c]["Y"]
        for g in range(4):
            for pi in range(2):
                pair = c * 8 + g * 2 + pi
                b = pair // 16
                h = pair % 16
                Yf[b, h] = Yc[g, :, pi * 64:(pi + 1) * 64, :].reshape(T, 64)
    st["Yttt"] = Yf



def build_l1post():
    p = Prog()
    yT = p.inp("yT", (D, TPC))
    xnT = p.inp("xnT", (D, TPC))
    pw = p.inp("pw", (D, 1))
    pb = p.inp("pb", (D, 1))
    woT = p.inp("woT", (D, D))
    wob = p.inp("wob", (D, 1))
    gwT = p.inp("gwT", (D, 8))
    ones1 = p.inp("ones1", (1, 128))
    onescol = p.inp("onescol", (128, 1))
    attn1 = p.outp("attn1", (D, TPC))
    comb = p.outp("comb", (TPC, 8))

    o1_sb = p.sb((1, 128)); oc_sb = p.sb((128, 1))
    p.dma(o1_sb[:], ones1[:]); p.dma(oc_sb[:], onescol[:])
    pw_sb = p.sb((128, 8)); pb_sb = p.sb((128, 8)); wob_sb = p.sb((128, 8))
    for c in range(8):
        p.dma(pw_sb[:, c:c + 1], pw[c * 128:(c + 1) * 128, :])
        p.dma(pb_sb[:, c:c + 1], pb[c * 128:(c + 1) * 128, :])
        p.dma(wob_sb[:, c:c + 1], wob[c * 128:(c + 1) * 128, :])
    wo_b = p.sb((128, 8 * D), BF16)
    gw_s = p.sb((128, 8 * 8))
    wt = p.sb((128, D))
    for c in range(8):
        p.dma(wt[:], woT[c * 128:(c + 1) * 128, :])
        p.v("tensor_copy", wo_b[:, c * D:(c + 1) * D], wt[:])
        p.dma(gw_s[:, c * 8:(c + 1) * 8], gwT[c * 128:(c + 1) * 128, :])

    yf = p.sb((128, 8 * 512))
    xnf = p.sb((128, 8 * 512))
    sqf = p.sb((128, 512))
    pS1 = p.ps((1, 512)); pS2 = p.ps((1, 512))
    pA = p.ps((128, 512)); pB = p.ps((128, 512))
    pY = p.ps((128, 512))
    pG = p.ps((128, 8))
    srow = p.sb((1, 512)); arow = p.sb((1, 512)); brow = p.sb((1, 512))
    m2r = p.sb((1, 512))
    lnb_all = p.sb((128, 8 * 512), BF16)
    lnf = p.sb((128, 512))
    asb = p.sb((128, 512))
    gsc = p.sb((128, 8))
    eqs = p.sb((128, 8)); eqs2 = p.sb((128, 8))
    rr = p.sb((128, 8))
    cmb = p.sb((128, 8))

    for ti in range(TPC // 512):
        cs = slice(ti * 512, (ti + 1) * 512)
        for c in range(8):
            p.dma(yf[:, c * 512:(c + 1) * 512], yT[c * 128:(c + 1) * 128, cs])
            p.dma(xnf[:, c * 512:(c + 1) * 512], xnT[c * 128:(c + 1) * 128, cs])
        for c in range(8):
            p.mm(pS1[:], oc_sb[:], yf[:, c * 512:(c + 1) * 512], start=(c == 0),
                 stop=(c == 7))
        for c in range(8):
            p.s("activation", sqf[:], yf[:, c * 512:(c + 1) * 512], AF.Square)
            p.mm(pS2[:], oc_sb[:], sqf[:], start=(c == 0), stop=(c == 7))
        p.v("tensor_scalar", srow[:], pS1[:], 1.0 / D, None, OP.mult)
        p.v("tensor_scalar", arow[:], pS2[:], 1.0 / D, None, OP.mult)
        p.s("activation", m2r[:], srow[:], AF.Square)
        p.v("tensor_tensor", arow[:], arow[:], m2r[:], OP.subtract)
        p.rsqrt_pos(arow[:], arow[:], 1e-6)
        p.v("tensor_tensor", brow[:], srow[:], arow[:], OP.mult)
        p.v("tensor_scalar", brow[:], brow[:], -1.0, None, OP.mult)
        p.mm(pA[:], o1_sb[:], arow[:], start=True, stop=True)
        p.mm(pB[:], o1_sb[:], brow[:], start=True, stop=True)
        for c in range(8):
            p.v("tensor_tensor", lnf[:], yf[:, c * 512:(c + 1) * 512], pA[:],
                OP.mult)
            p.v("tensor_tensor", lnf[:], lnf[:], pB[:], OP.add)
            p.v("tensor_scalar", lnf[:], lnf[:], pw_sb[:, c:c + 1], None, OP.mult)
            p.v("tensor_scalar", lnf[:], lnf[:], pb_sb[:, c:c + 1], None, OP.add)
            p.v("tensor_copy", lnb_all[:, c * 512:(c + 1) * 512], lnf[:])
        for mc in range(8):
            for kc in range(8):
                p.mm(pY[:], wo_b[:, kc * D + mc * 128:kc * D + (mc + 1) * 128],
                     lnb_all[:, kc * 512:(kc + 1) * 512], start=(kc == 0),
                     stop=(kc == 7))
            p.v("tensor_scalar", asb[:], pY[:], wob_sb[:, mc:mc + 1], None, OP.add)
            p.dma(attn1[mc * 128:(mc + 1) * 128, cs], asb[:])
        for sub in range(4):
            t0 = ti * 512 + sub * 128
            for c in range(8):
                p.mm(pG[:], xnf[:, c * 512 + sub * 128:c * 512 + (sub + 1) * 128],
                     gw_s[:, c * 8:(c + 1) * 8], start=(c == 0), stop=(c == 7))
            p.v("tensor_copy", gsc[:], pG[:])
            p.v("reduce_max", rr[:, 0:1], gsc[:], axis=AXX)
            p.v("tensor_scalar", eqs[:], gsc[:], rr[:, 0:1], None, OP.is_ge)
            p.v("scalar_tensor_tensor", gsc[:], eqs[:], -1e30, gsc[:],
                OP.mult, OP.add)
            p.v("reduce_max", rr[:, 1:2], gsc[:], axis=AXX)
            p.v("tensor_scalar", eqs2[:], gsc[:], rr[:, 1:2], None, OP.is_ge)
            p.v("tensor_tensor", rr[:, 2:3], rr[:, 1:2], rr[:, 0:1], OP.subtract)
            p.s("activation", rr[:, 3:4], rr[:, 2:3], AF.Exp)
            p.v("tensor_scalar", rr[:, 4:5], rr[:, 3:4], 1.0, None, OP.add)
            p.recip_pos(rr[:, 5:6], rr[:, 4:5])
            p.v("tensor_tensor", rr[:, 6:7], rr[:, 3:4], rr[:, 5:6], OP.mult)
            p.v("tensor_scalar", cmb[:], eqs[:], rr[:, 5:6], None, OP.mult)
            p.v("tensor_scalar", eqs2[:], eqs2[:], rr[:, 6:7], None, OP.mult)
            p.v("tensor_tensor", cmb[:], cmb[:], eqs2[:], OP.add)
            p.dma(comb[t0:t0 + 128, :], cmb[:])
    return p.build()


def run_l1post(st, inp):
    nc = _prog_cache.get("l1post")
    if nc is None:
        nc = _prog_cache["l1post"] = build_l1post()
    Y = st["Yttt"]
    ytm = np.ascontiguousarray(Y.transpose(0, 2, 1, 3).reshape(TOK, D))
    yT = np.ascontiguousarray(ytm.T)
    xnT = np.ascontiguousarray(st["xn1"].T)
    maps = []
    for c in range(NCORE):
        sl = slice(c * TPC, (c + 1) * TPC)
        maps.append({
            "yT": _f32(yT[:, sl]),
            "xnT": _f32(xnT[:, sl]),
            "pw": _f32(np.asarray(inp["L1_attn_post_norm_w"]).reshape(D, 1)),
            "pb": _f32(np.asarray(inp["L1_attn_post_norm_b"]).reshape(D, 1)),
            "woT": _f32(np.asarray(inp["L1_attn_wo_w"]).T),
            "wob": _f32(np.asarray(inp["L1_attn_wo_b"]).reshape(D, 1)),
            "gwT": _f32(np.asarray(inp["L1_ffn_gate_w"]).T),
            "ones1": np.ones((1, 128), np.float32),
            "onescol": np.ones((128, 1), np.float32),
        })
    res = run_bass_kernel_spmd(nc, maps, core_ids=list(range(NCORE)))
    st["attn1"] = np.concatenate([r["attn1"] for r in res.results], axis=1)
    st["comb"] = np.concatenate([r["comb"] for r in res.results], axis=0)



def build_moe():
    p = Prog()
    xnT = p.inp("xnT", (D, TOK))
    w1T = p.inp("w1T", (D, 4096))
    w3T = p.inp("w3T", (D, 4096))
    w2T = p.inp("w2T", (4096, D))
    combr = p.inp("combr", (1, TOK))
    ones1 = p.inp("ones1", (1, 128))
    ye = p.outp("ye", (D, TOK))
    ghat = p.scratch("ghat", (4096, TOK), BF16)

    o1_sb = p.sb((1, 128))
    p.dma(o1_sb[:], ones1[:])
    QT = TOK // 4
    xqb = p.sb((128, 8 * QT), BF16)
    xf = p.sb((128, QT))
    w13 = p.sb((128, 1024))
    w13b = p.sb((128, 1024), BF16)
    w13g = p.sb((128, 1024), BF16)
    pH1 = p.ps((128, 512))
    pH2 = p.ps((128, 512))
    sil = p.sb((128, 512))
    ghb = p.sb((128, 512), BF16)
    for Q in range(4):
        qs = slice(Q * QT, (Q + 1) * QT)
        for c in range(8):
            p.dma(xf[:], xnT[c * 128:(c + 1) * 128, qs])
            p.v("tensor_copy", xqb[:, c * QT:(c + 1) * QT], xf[:])
        for hc in range(32):
            for c in range(8):
                p.dma(w13[:, c * 128:(c + 1) * 128],
                      w1T[c * 128:(c + 1) * 128, hc * 128:(hc + 1) * 128])
            p.v("tensor_copy", w13b[:], w13[:])
            for c in range(8):
                p.dma(w13[:, c * 128:(c + 1) * 128],
                      w3T[c * 128:(c + 1) * 128, hc * 128:(hc + 1) * 128])
            p.v("tensor_copy", w13g[:], w13[:])
            for tt in range(QT // 512):
                for c in range(8):
                    p.mm(pH1[:], w13b[:, c * 128:(c + 1) * 128],
                         xqb[:, c * QT + tt * 512:c * QT + (tt + 1) * 512],
                         start=(c == 0), stop=(c == 7))
                for c in range(8):
                    p.mm(pH2[:], w13g[:, c * 128:(c + 1) * 128],
                         xqb[:, c * QT + tt * 512:c * QT + (tt + 1) * 512],
                         start=(c == 0), stop=(c == 7))
                p.s("activation", sil[:], pH1[:], AF.Silu)
                p.v("tensor_tensor", ghb[:], sil[:], pH2[:], OP.mult)
                p.dma(ghat[hc * 128:(hc + 1) * 128,
                           Q * QT + tt * 512:Q * QT + (tt + 1) * 512], ghb[:])
    w2f = p.sb((128, 1024))
    w2b = p.sb((128, 32 * 1024), BF16)
    for kc in range(32):
        p.dma(w2f[:], w2T[kc * 128:(kc + 1) * 128, :])
        p.v("tensor_copy", w2b[:, kc * 1024:(kc + 1) * 1024], w2f[:])
    gh_sb = p.sb((128, 32 * 512), BF16)
    crow = p.sb((1, 512))
    pC = p.ps((128, 512))
    cbc = p.sb((128, 512))
    pY = p.ps((128, 512))
    ysb = p.sb((128, 512))
    for tt in range(TOK // 512):
        cs = slice(tt * 512, (tt + 1) * 512)
        for kc in range(32):
            p.dma(gh_sb[:, kc * 512:(kc + 1) * 512], ghat[kc * 128:(kc + 1) * 128,
                                                          cs])
        p.dma(crow[:], combr[:, cs])
        p.mm(pC[:], o1_sb[:], crow[:], start=True, stop=True)
        p.v("tensor_copy", cbc[:], pC[:])
        for mc in range(8):
            for kc in range(32):
                p.mm(pY[:], w2b[:, kc * 1024 + mc * 128:kc * 1024 + (mc + 1) * 128],
                     gh_sb[:, kc * 512:(kc + 1) * 512], start=(kc == 0),
                     stop=(kc == 31))
            p.v("tensor_tensor", ysb[:], pY[:], cbc[:], OP.mult)
            p.dma(ye[mc * 128:(mc + 1) * 128, cs], ysb[:])
    return p.build()


def run_moe(st, inp):
    nc = _prog_cache.get("moe")
    if nc is None:
        nc = _prog_cache["moe"] = build_moe()
    xnT = _f32(np.ascontiguousarray(st["xn1"].T))
    combT = np.ascontiguousarray(st["comb"].T)
    w1 = np.asarray(inp["L1_ffn_w1"])
    w2 = np.asarray(inp["L1_ffn_w2"])
    w3 = np.asarray(inp["L1_ffn_w3"])
    maps = []
    for e in range(NCORE):
        maps.append({
            "xnT": xnT,
            "w1T": _f32(w1[e].T),
            "w3T": _f32(w3[e].T),
            "w2T": _f32(w2[e].T),
            "combr": _f32(combT[e:e + 1]),
            "ones1": np.ones((1, 128), np.float32),
        })
    res = run_bass_kernel_spmd(nc, maps, core_ids=list(range(NCORE)))
    st["yes"] = [r["ye"] for r in res.results]



def build_final():
    p = Prog()
    x1 = p.inp("x1", (TPC, D))
    a1 = p.inp("a1", (TPC, D))
    yes = [p.inp(f"ye{e}", (TPC, D)) for e in range(8)]
    out = p.outp("out", (TPC, D))
    xs = p.sb((128, D))
    ts_ = p.sb((128, D))
    for i in range(TPC // 128):
        sl = slice(i * 128, (i + 1) * 128)
        p.dma(xs[:], x1[sl, :])
        p.dma(ts_[:], a1[sl, :])
        p.v("tensor_tensor", xs[:], xs[:], ts_[:], OP.add)
        for e in range(8):
            p.dma(ts_[:], yes[e][sl, :])
            p.v("tensor_tensor", xs[:], xs[:], ts_[:], OP.add)
        p.dma(out[sl, :], xs[:])
    return p.build()


def run_final(st):
    nc = _prog_cache.get("final")
    if nc is None:
        nc = _prog_cache["final"] = build_final()
    a1 = np.ascontiguousarray(st["attn1"].T)
    yes = [np.ascontiguousarray(y.T) for y in st["yes"]]
    maps = []
    for c in range(NCORE):
        sl = slice(c * TPC, (c + 1) * TPC)
        m = {"x1": _f32(st["x1"][sl]), "a1": _f32(a1[sl])}
        for e in range(8):
            m[f"ye{e}"] = _f32(yes[e][sl])
        maps.append(m)
    res = run_bass_kernel_spmd(nc, maps, core_ids=list(range(NCORE)))
    st["out"] = np.concatenate([r["out"] for r in res.results], axis=0)



def kernel(**inputs):
    st = {"x0": _f32(np.asarray(inputs["L0_attn_h"]).reshape(TOK, D))}
    run_ln0(st)
    run_l0qkv(st, inputs)
    run_l0attn(st, inputs)
    run_l0out(st, inputs)
    run_ffnqkv(st, inputs, 0, "xn0", "f0")
    run_locattn(st, "f0")
    run_mlp(st, inputs, 0, "xn0", "f0_o", "xi1")
    run_ffnqkv(st, inputs, 1, "xi1", "f1")
    run_locattn(st, "f1")
    run_mlp(st, inputs, 1, "xi1", "f1_o", "ffn0")
    run_comb0(st)
    run_l1proj(st, inputs)
    run_l1scan(st, inputs)
    run_l1post(st, inputs)
    run_moe(st, inputs)
    run_final(st)
    return st["out"].reshape(B, T, D).astype(np.float32)


# version 02
# speedup vs baseline: 1.0692x; 1.0692x over previous
import contextlib
import math

import numpy as np
import ml_dtypes

import concourse.bass as bass
import concourse.mybir as mybir
from concourse.bass_utils import run_bass_kernel_spmd

F32 = mybir.dt.float32
BF16 = mybir.dt.bfloat16
U8 = mybir.dt.uint8
AF = mybir.ActivationFunctionType
OP = mybir.AluOpType
AXX = mybir.AxisListType.X
BF = ml_dtypes.bfloat16

NCORE = 8
B, T, D = 4, 4096, 1024
TOK = B * T
TPC = TOK // NCORE
SEG_LIMIT = 30000

_prog_cache = {}


class Prog:

    def __init__(self):
        self.nc = bass.Bass("TRN2", target_bir_lowering=False)
        self._ops = []
        self._stack = contextlib.ExitStack()
        self._n = 0

    def sb(self, shape, dtype=F32, name=None):
        name = name or f"sb{self._n}"
        self._n += 1
        return self._stack.enter_context(self.nc.sbuf_tensor(name, list(shape), dtype))

    def ps(self, shape, dtype=F32, name=None):
        name = name or f"ps{self._n}"
        self._n += 1
        return self._stack.enter_context(self.nc.psum_tensor(name, list(shape), dtype))

    def inp(self, name, shape, dtype=F32):
        return self.nc.dram_tensor(name, list(shape), dtype, kind="ExternalInput").ap()

    def outp(self, name, shape, dtype=F32):
        return self.nc.dram_tensor(name, list(shape), dtype, kind="ExternalOutput").ap()

    def scratch(self, name, shape, dtype=F32):
        return self.nc.dram_tensor(name, list(shape), dtype, kind="Internal").ap()

    def op(self, eng, fn, inc=1, fence=False):
        self._ops.append((eng, fn, inc, fence))

    def dma(self, dst, src, eng="sync"):
        self.op(eng, lambda e, d=dst, s=src: e.dma_start(out=d, in_=s), inc=16)

    def vf(self, meth, *args, **kw):
        self.op("vector", lambda e: getattr(e, meth)(*args, **kw), 1, fence=True)

    def sf(self, meth, *args, **kw):
        self.op("scalar", lambda e: getattr(e, meth)(*args, **kw), 1, fence=True)

    @staticmethod
    def _needs_fence(meth, args, kw):
        def is_imm(x):
            return x is None or isinstance(x, (int, float))
        if meth in ("tensor_scalar", "tensor_single_scalar", "tensor_scalar_add",
                    "tensor_scalar_sub", "tensor_scalar_mul", "tensor_scalar_max",
                    "tensor_scalar_min"):
            sc1 = args[2] if len(args) > 2 else kw.get("scalar1")
            sc2 = args[3] if len(args) > 3 else kw.get("scalar2")
            return not (is_imm(sc1) and is_imm(sc2))
        if meth == "activation":
            return not is_imm(kw.get("bias", None))
        if meth == "scalar_tensor_tensor":
            sc = args[2] if len(args) > 2 else kw.get("scalar")
            return not is_imm(sc)
        return False

    def v(self, meth, *args, **kw):
        self.op("vector", lambda e: getattr(e, meth)(*args, **kw), 1, fence=True)

    def s(self, meth, *args, **kw):
        self.op("scalar", lambda e: getattr(e, meth)(*args, **kw), 1, fence=True)

    def t(self, meth, *args, **kw):
        self.op("tensor", lambda e: getattr(e, meth)(*args, **kw), 1)

    def g(self, meth, *args, **kw):
        self.op("gpsimd", lambda e: getattr(e, meth)(*args, **kw), 1, fence=True)

    def mm(self, psum, lhsT, rhs, start, stop):
        self.t("matmul", psum, lhsT, rhs, start=start, stop=stop)

    def recip_pos(self, out, in_):
        self.s("activation", out, in_, AF.Ln)
        self.s("activation", out, out, AF.Exp, scale=-1.0)

    def rsqrt_pos(self, out, in_, eps):
        self.v("tensor_scalar", out, in_, float(eps), None, OP.add)
        self.s("activation", out, out, AF.Ln)
        self.s("activation", out, out, AF.Exp, scale=-0.5)

    def maskc(self, out, S, m, madd):
        self.v("tensor_tensor", out, S, m, OP.mult)
        self.v("tensor_tensor", out, out, madd, OP.add)

    def build(self, final_eng="sync"):
        nc = self.nc
        sched = []
        seg = 0
        cnt = 0
        seg_totals = []
        prev_eng = None
        pending = {}
        engs_seen = set()
        for eng, fn, inc, fence in self._ops:
            if cnt + inc > SEG_LIMIT:
                seg_totals.append(cnt)
                for e2 in engs_seen:
                    pending.setdefault(e2, []).append((seg, cnt))
                seg += 1
                cnt = 0
                prev_eng = None
            waits = list(pending.pop(eng, []))
            if (prev_eng is not None and eng != prev_eng) or (fence and cnt > 0):
                waits.append((seg, cnt))
            sched.append((eng, fn, inc, seg, waits))
            cnt += inc
            prev_eng = eng
            engs_seen.add(eng)
        seg_totals.append(cnt)
        nseg = seg + 1
        per = {}
        for item in sched:
            per.setdefault(item[0], []).append(item)
        with self._stack:
            sems = [self._stack.enter_context(self.nc.semaphore(f"gsem{i}"))
                    for i in range(nseg)]
            with nc.Block() as block:
                def mk(items, final):
                    def emit(eng):
                        for _, fn, inc, sg, waits in items:
                            for (wsg, wval) in waits:
                                if wval > 0:
                                    eng.wait_ge(sems[wsg], wval)
                            instr = fn(eng)
                            instr.then_inc(sems[sg], inc)
                        if final:
                            eng.wait_ge(sems[nseg - 1], seg_totals[-1])
                            for i in range(nseg - 1):
                                eng.wait_ge(sems[i], seg_totals[i])
                    return emit
                for engname, items in per.items():
                    dec = getattr(block, engname)
                    dec(mk(items, final=(engname == final_eng)))
        return nc


def _f32(x):
    return np.ascontiguousarray(x, dtype=np.float32)


def _bf(x):
    return np.ascontiguousarray(np.asarray(x, dtype=np.float32).astype(BF))


def _rep(v, rows=128):
    v = np.asarray(v, dtype=np.float32).reshape(-1)
    return np.ascontiguousarray(np.broadcast_to(v[None, :], (rows, v.size)))



def _l0_rope_tables():
    half = 16
    inv = 1.0 / (10000.0 ** (np.arange(half, dtype=np.float64) * 2.0 / 32.0))
    ang = np.arange(T, dtype=np.float64)[:, None] * inv[None, :]
    return _f32(np.cos(ang)), _f32(np.sin(ang))


def _xpos_tables():
    half_idx = np.arange(0, 64, 2, dtype=np.float64)
    inv_freq = 1.0 / (10000.0 ** (half_idx / 64.0))
    t = np.arange(T, dtype=np.float64)
    freqs = t[:, None] * inv_freq[None, :]
    freqs = np.concatenate([freqs, freqs], axis=-1)
    base = (half_idx + 0.4 * 64.0) / (1.4 * 64.0)
    power = (t - T // 2) / 256.0
    scale = base[None, :] ** power[:, None]
    scale = np.concatenate([scale, scale], axis=-1)
    cos = np.cos(freqs)
    sin = np.sin(freqs)
    return (_f32(cos * scale), _f32(sin * scale),
            _f32(cos / scale), _f32(sin / scale))


def _l0_masks():
    CR, NB = 32, 128
    qpos = np.arange(T)
    cvalid = ((np.arange(NB)[None, :] + 1) * CR - 1) <= qpos[:, None]
    anyb = cvalid.any(-1)
    safe = np.where(anyb[:, None], cvalid, True)
    cvalid_f = _f32(cvalid)
    safe_f = _f32(safe)
    return cvalid_f, safe_f, _f32(anyb.astype(np.float32))


def _tri_masks():
    i = np.arange(128)[:, None]
    j = np.arange(512)[None, :]
    cdiag = np.stack([(j <= i + 128 * r) for r in range(4)]).astype(np.float32)
    wfirst = (np.arange(128)[None, :] >= np.arange(128)[:, None] + 1
              ).astype(np.float32)
    return _f32(cdiag), _f32(wfirst)


def _expand_E():
    E = np.zeros((128, 4096), dtype=np.float32)
    for jj in range(128):
        E[jj, jj * 32:(jj + 1) * 32] = 1.0
    return _bf(E)


def _pool_P4():
    P = np.zeros((128, 4), dtype=np.float32)
    for c in range(4):
        P[c * 32:(c + 1) * 32, c] = 1.0 / 32.0
    return _bf(P)



def build_ln0():
    p = Prog()
    x = p.inp("x", (TPC, D))
    xn = p.outp("xn", (TPC, D))
    xs = p.sb((128, D))
    xc = p.sb((128, D))
    sq = p.sb((128, D))
    st = p.sb((128, 4))
    for i in range(TPC // 128):
        p.dma(xs[:], x[i * 128:(i + 1) * 128, :])
        p.v("reduce_sum", st[:, 0:1], xs[:], axis=AXX)
        p.v("tensor_scalar", st[:, 1:2], st[:, 0:1], 1.0 / D, None, OP.mult)
        p.v("tensor_scalar", xc[:], xs[:], st[:, 1:2], None, OP.subtract)
        p.s("activation", sq[:], xc[:], AF.Square)
        p.v("reduce_sum", st[:, 2:3], sq[:], axis=AXX)
        p.v("tensor_scalar", st[:, 2:3], st[:, 2:3], 1.0 / D, None, OP.mult)
        p.rsqrt_pos(st[:, 3:4], st[:, 2:3], 1e-5)
        p.v("tensor_scalar", xc[:], xc[:], st[:, 3:4], None, OP.mult)
        p.dma(xn[i * 128:(i + 1) * 128, :], xc[:])
    return p.build()


def run_ln0(st):
    nc = _prog_cache.get("ln0")
    if nc is None:
        nc = _prog_cache["ln0"] = build_ln0()
    x = st["x0"]
    maps = [{"x": _f32(x[c * TPC:(c + 1) * TPC])} for c in range(NCORE)]
    res = run_bass_kernel_spmd(nc, maps, core_ids=list(range(NCORE)))
    st["xn0"] = np.concatenate([r["xn"] for r in res.results], axis=0)



def build_l0qkv():
    p = Prog()
    xnT = p.inp("xnT", (D, TPC))
    wqc = p.inp("wqc", (D, 256))
    wqup = p.inp("wqup", (256, 512))
    wk = p.inp("wk", (D, 64))
    wv = p.inp("wv", (D, 64))
    cosT = p.inp("cosT", (TPC, 16))
    sinT = p.inp("sinT", (TPC, 16))
    gq = p.inp("gq", (128, 64))
    gk = p.inp("gk", (128, 64))
    ident = p.inp("ident", (128, 128))
    qn = p.outp("qn", (TPC, 512))
    kn = p.outp("kn", (TPC, 64))
    vo = p.outp("vo", (TPC, 64))

    ids = p.sb((128, 128))
    p.dma(ids[:], ident[:])
    wqc_b = p.sb((128, 8 * 256))
    wqup_b = p.sb((128, 2 * 512))
    wk_b = p.sb((128, 8 * 64))
    wv_b = p.sb((128, 8 * 64))
    for c in range(8):
        p.dma(wqc_b[:, c * 256:(c + 1) * 256], wqc[c * 128:(c + 1) * 128, :])
        p.dma(wk_b[:, c * 64:(c + 1) * 64], wk[c * 128:(c + 1) * 128, :])
        p.dma(wv_b[:, c * 64:(c + 1) * 64], wv[c * 128:(c + 1) * 128, :])
    for c in range(2):
        p.dma(wqup_b[:, c * 512:(c + 1) * 512], wqup[c * 128:(c + 1) * 128, :])
    gq_s = p.sb((128, 64))
    gk_s = p.sb((128, 64))
    p.dma(gq_s[:], gq[:])
    p.dma(gk_s[:], gk[:])

    xb = p.sb((128, 8 * 128))
    q1p = p.ps((128, 256))
    q2p = p.ps((128, 512))
    ptr = p.ps((128, 128))
    q1b = p.sb((128, 256))
    q1T = p.sb((128, 2 * 128))
    qsb = p.sb((128, 512))
    qout = p.sb((128, 512))
    sqq = p.sb((128, 512))
    ksb = p.sb((128, 64))
    kout = p.sb((128, 64))
    vsb = p.sb((128, 64))
    cs_t = p.sb((128, 16))
    sn_t = p.sb((128, 16))
    ra = p.sb((128, 16))
    rb = p.sb((128, 16))
    red = p.sb((128, 4))

    def rope16(src, dst, c0):
        p.v("tensor_tensor", ra[:], src[:, c0:c0 + 16], cs_t[:], OP.mult)
        p.v("tensor_tensor", rb[:], src[:, c0 + 16:c0 + 32], sn_t[:], OP.mult)
        p.v("tensor_tensor", dst[:, c0:c0 + 16], ra[:], rb[:], OP.subtract)
        p.v("tensor_tensor", ra[:], src[:, c0:c0 + 16], sn_t[:], OP.mult)
        p.v("tensor_tensor", rb[:], src[:, c0 + 16:c0 + 32], cs_t[:], OP.mult)
        p.v("tensor_tensor", dst[:, c0 + 16:c0 + 32], ra[:], rb[:], OP.add)

    for tt in range(TPC // 128):
        t0 = tt * 128
        for c in range(8):
            p.dma(xb[:, c * 128:(c + 1) * 128], xnT[c * 128:(c + 1) * 128, t0:t0 + 128])
        p.dma(cs_t[:], cosT[t0:t0 + 128, :])
        p.dma(sn_t[:], sinT[t0:t0 + 128, :])
        for c in range(8):
            p.mm(q1p[:], xb[:, c * 128:(c + 1) * 128], wqc_b[:, c * 256:(c + 1) * 256],
                 start=(c == 0), stop=(c == 7))
        p.v("tensor_copy", q1b[:], q1p[:])
        for c in range(2):
            p.t("transpose", ptr[:], q1b[:, c * 128:(c + 1) * 128], ids[:])
            p.v("tensor_copy", q1T[:, c * 128:(c + 1) * 128], ptr[:])
        for c in range(2):
            p.mm(q2p[:], q1T[:, c * 128:(c + 1) * 128], wqup_b[:, c * 512:(c + 1) * 512],
                 start=(c == 0), stop=(c == 1))
        p.v("tensor_copy", qsb[:], q2p[:])
        for c in range(8):
            p.mm(q1p[:, 0:64], xb[:, c * 128:(c + 1) * 128], wk_b[:, c * 64:(c + 1) * 64],
                 start=(c == 0), stop=(c == 7))
        p.v("tensor_copy", ksb[:], q1p[:, 0:64])
        for c in range(8):
            p.mm(q1p[:, 64:128], xb[:, c * 128:(c + 1) * 128], wv_b[:, c * 64:(c + 1) * 64],
                 start=(c == 0), stop=(c == 7))
        p.v("tensor_copy", vsb[:], q1p[:, 64:128])
        p.dma(vo[t0:t0 + 128, :], vsb[:])
        for h in range(8):
            rope16(qsb, qout, h * 64)
            p.v("tensor_copy", qout[:, h * 64 + 32:h * 64 + 64],
                qsb[:, h * 64 + 32:h * 64 + 64])
        p.s("activation", sqq[:], qout[:], AF.Square)
        for h in range(8):
            hs = slice(h * 64, (h + 1) * 64)
            p.v("reduce_sum", red[:, 0:1], sqq[:, hs], axis=AXX)
            p.v("tensor_scalar", red[:, 0:1], red[:, 0:1], 1.0 / 64, None, OP.mult)
            p.rsqrt_pos(red[:, 1:2], red[:, 0:1], 1e-6)
            p.v("tensor_scalar", qout[:, hs], qout[:, hs], red[:, 1:2], None, OP.mult)
            p.v("tensor_tensor", qout[:, hs], qout[:, hs], gq_s[:], OP.mult)
        p.dma(qn[t0:t0 + 128, :], qout[:])
        rope16(ksb, kout, 0)
        p.v("tensor_copy", kout[:, 32:64], ksb[:, 32:64])
        p.s("activation", sqq[:, 0:64], kout[:], AF.Square)
        p.v("reduce_sum", red[:, 2:3], sqq[:, 0:64], axis=AXX)
        p.v("tensor_scalar", red[:, 2:3], red[:, 2:3], 1.0 / 64, None, OP.mult)
        p.rsqrt_pos(red[:, 3:4], red[:, 2:3], 1e-6)
        p.v("tensor_scalar", kout[:], kout[:], red[:, 3:4], None, OP.mult)
        p.v("tensor_tensor", kout[:], kout[:], gk_s[:], OP.mult)
        p.dma(kn[t0:t0 + 128, :], kout[:])
    return p.build()


def run_l0qkv(st, inp):
    nc = _prog_cache.get("l0qkv")
    if nc is None:
        nc = _prog_cache["l0qkv"] = build_l0qkv()
    xnT = np.ascontiguousarray(st["xn0"].T)
    cosF, sinF = _l0_rope_tables()
    ident = np.eye(128, dtype=np.float32)
    maps = []
    for c in range(NCORE):
        t0 = (c % 2) * 2048
        maps.append({
            "xnT": _f32(xnT[:, c * TPC:(c + 1) * TPC]),
            "wqc": _f32(inp["L0_attn_w_qc"]),
            "wqup": _f32(inp["L0_attn_w_qup"]),
            "wk": _f32(inp["L0_attn_w_k"]),
            "wv": _f32(inp["L0_attn_w_v"]),
            "cosT": _f32(cosF[t0:t0 + TPC]),
            "sinT": _f32(sinF[t0:t0 + TPC]),
            "gq": _rep(inp["L0_attn_g_qnorm"]),
            "gk": _rep(inp["L0_attn_g_knorm"]),
            "ident": ident,
        })
    res = run_bass_kernel_spmd(nc, maps, core_ids=list(range(NCORE)))
    st["l0_qn"] = np.concatenate([r["qn"] for r in res.results], axis=0)
    st["l0_kn"] = np.concatenate([r["kn"] for r in res.results], axis=0)
    st["l0_v"] = np.concatenate([r["vo"] for r in res.results], axis=0)



def build_l0attn():
    p = Prog()
    q4 = p.inp("q4", (256, T))
    kfm = p.inp("kfm", (64, T))
    ktm = p.inp("ktm", (T, 64))
    vtm = p.inp("vtm", (T, 64))
    sinkrep = p.inp("sinkrep", (128, 4))
    wgate = p.inp("wgate", (64, 3))
    bgrep = p.inp("bgrep", (128, 3))
    E_in = p.inp("E_in", (128, T), BF16)
    P4_in = p.inp("P4_in", (128, 4), BF16)
    P4f = p.inp("P4f", (128, 4))
    ident = p.inp("ident", (128, 128), BF16)
    msafe = p.inp("msafe", (32, 128, 128))
    cvg = p.inp("cvg", (8, 128, 512))
    cdiag = p.inp("cdiag", (4, 128, 512))
    wfirst = p.inp("wfirst", (128, 128))
    anyb0 = p.inp("anyb0", (128, 1))
    Mout = p.outp("Mout", (4, T, 64))

    ids = p.sb((128, 128), BF16)
    E_sb = p.sb((128, T), BF16)
    P4_sb = p.sb((128, 4), BF16)
    p.dma(ids[:], ident[:])
    p.dma(E_sb[:], E_in[:])
    p.dma(P4_sb[:], P4_in[:])
    msafe_sb = p.sb((128, 32 * 128))
    cvg_sb = p.sb((128, 8 * 512))
    cdiag_sb = p.sb((128, 4 * 512))
    wfirst_sb = p.sb((128, 128))
    anyb0_sb = p.sb((128, 1))
    for i in range(32):
        p.dma(msafe_sb[:, i * 128:(i + 1) * 128], msafe[i, :, :])
    for i in range(8):
        p.dma(cvg_sb[:, i * 512:(i + 1) * 512], cvg[i, :, :])
    for i in range(4):
        p.dma(cdiag_sb[:, i * 512:(i + 1) * 512], cdiag[i, :, :])
    p.dma(wfirst_sb[:], wfirst[:])
    p.dma(anyb0_sb[:], anyb0[:])
    sink_sb = p.sb((128, 4))
    esink_sb = p.sb((128, 4))
    p.dma(sink_sb[:], sinkrep[:])
    p.s("activation", esink_sb[:], sink_sb[:], AF.Exp)
    wg_b = p.sb((64, 3))
    bg_sb = p.sb((128, 3))
    p.dma(wg_b[:], wgate[:])
    p.dma(bg_sb[:], bgrep[:])

    kb = p.sb((64, T))
    p.dma(kb[:], kfm[:])
    ktb = p.sb((128, 32 * 64))
    vtb = p.sb((128, 32 * 64), BF16)
    P4f_sb = p.sb((128, 4))
    p.dma(P4f_sb[:], P4f[:])
    tf = p.sb((128, 64))
    for c in range(32):
        p.dma(ktb[:, c * 64:(c + 1) * 64], ktm[c * 128:(c + 1) * 128, :])
        p.dma(tf[:], vtm[c * 128:(c + 1) * 128, :])
        p.v("tensor_copy", vtb[:, c * 64:(c + 1) * 64], tf[:])
    pkc = p.ps((64, 128))
    kcb = p.sb((64, 128))
    vcfb = p.sb((64, 128), BF16)
    vctb = p.sb((128, 64), BF16)
    ptrb = p.ps((128, 128), BF16)
    for c in range(32):
        p.mm(pkc[:, 4 * c:4 * c + 4], ktb[:, c * 64:(c + 1) * 64], P4f_sb[:],
             start=True, stop=True)
    p.v("tensor_copy", kcb[:], pkc[:])
    for c in range(32):
        p.mm(pkc[:, 4 * c:4 * c + 4], vtb[:, c * 64:(c + 1) * 64], P4_sb[:],
             start=True, stop=True)
    p.v("tensor_copy", vcfb[:], pkc[:])
    p.t("transpose", ptrb[:, 0:64], vcfb[:], ids[0:64, 0:64])
    p.v("tensor_copy", vctb[:], ptrb[:, 0:64])

    q4f = p.sb((128, 2 * T))
    for r in range(2):
        p.dma(q4f[:, r * T:(r + 1) * T], q4[r * 128:(r + 1) * 128, :])
    qb = p.sb((64, T))

    psc = p.ps((128, 512))
    psS = p.ps((128, 512))
    pcomp = p.ps((128, 64))
    psel = p.ps((128, 64))
    pwin = p.ps((128, 64))
    scg = p.sb((128, 512))
    work = p.sb((128, 512))
    msel = p.sb((128, 512))
    eq = p.sb((128, 512))
    scr5 = p.sb((128, 512))
    red = p.sb((128, 8))
    Ssb = p.sb((128, T))
    msbuf = p.sb((128, T))
    esel = p.sb((128, T), BF16)
    ecb = p.sb((128, 128), BF16)
    cmask = p.sb((128, 128))
    scr1 = p.sb((128, 128))
    mselb = p.sb((128, 128), BF16)
    mselT = p.sb((128, 128), BF16)
    eT = p.sb((128, 128), BF16)
    mskf = p.sb((128, 512))
    wbuf = p.sb((128, 640))
    ew = p.sb((128, 640), BF16)
    gsb = p.sb((128, 3))
    rr = p.sb((128, 8))
    mix = p.sb((128, 64))
    t64 = p.sb((128, 64))

    for j in range(4):
        p.v("tensor_copy", qb[:], q4f[(j % 2) * 64:(j % 2) * 64 + 64,
                                      (j // 2) * T:(j // 2 + 1) * T])
        sinkcol = sink_sb[:, j:j + 1]
        esinkcol = esink_sb[:, j:j + 1]
        for G in range(8):
            for u in range(4):
                qt = 4 * G + u
                p.mm(psc[:, u * 128:(u + 1) * 128], qb[:, qt * 128:(qt + 1) * 128],
                     kcb[:], start=True, stop=True)
            p.s("mul", scg[:], psc[:], 0.125)
            cm = cvg_sb[:, G * 512:(G + 1) * 512]
            p.v("tensor_tensor", work[:], scg[:], cm, OP.mult)
            p.v("tensor_scalar", scr5[:], cm, 1.0, None, OP.subtract)
            p.v("scalar_tensor_tensor", work[:], scr5[:], 1e38, work[:],
                OP.mult, OP.add)
            p.v("memset", msel[:], 0.0)
            for _ in range(16):
                for u in range(4):
                    p.v("reduce_max", red[:, u:u + 1], work[:, u * 128:(u + 1) * 128],
                        axis=AXX)
                for u in range(4):
                    p.v("tensor_scalar", eq[:, u * 128:(u + 1) * 128],
                        work[:, u * 128:(u + 1) * 128], red[:, u:u + 1], None, OP.is_ge)
                p.v("tensor_tensor", msel[:], msel[:], eq[:], OP.max)
                p.v("scalar_tensor_tensor", work[:], eq[:], -1e38, work[:],
                    OP.mult, OP.add)
            for u in range(4):
                qt = 4 * G + u
                klen = (qt + 1) * 128
                nkk = qt // 4 + 1
                qtile = qb[:, qt * 128:(qt + 1) * 128]
                for kk in range(nkk):
                    w = min(512, klen - 512 * kk)
                    p.mm(psS[:, 0:w], qtile, kb[:, 512 * kk:512 * kk + w],
                         start=True, stop=True)
                    p.s("mul", Ssb[:, 512 * kk:512 * kk + w], psS[:, 0:w], 0.125)
                p.v("tensor_tensor", cmask[:], scg[:, u * 128:(u + 1) * 128],
                    msafe_sb[:, qt * 128:(qt + 1) * 128], OP.mult)
                p.v("tensor_scalar", scr1[:], msafe_sb[:, qt * 128:(qt + 1) * 128],
                    1.0, None, OP.subtract)
                p.v("scalar_tensor_tensor", cmask[:], scr1[:], 1e30, cmask[:],
                    OP.mult, OP.add)
                p.v("reduce_max", red[:, 4:5], cmask[:], axis=AXX)
                p.v("tensor_tensor", red[:, 4:5], red[:, 4:5], sinkcol, OP.max)
                p.v("tensor_scalar", red[:, 5:6], red[:, 4:5], -1.0, None, OP.mult)
                p.s("activation", ecb[:], cmask[:], AF.Exp, bias=red[:, 5:6])
                p.v("reduce_sum", red[:, 6:7], ecb[:], axis=AXX)
                p.s("activation", red[:, 7:8], red[:, 5:6], AF.Exp)
                p.v("tensor_tensor", red[:, 7:8], red[:, 7:8], esinkcol, OP.mult)
                p.v("tensor_tensor", red[:, 6:7], red[:, 6:7], red[:, 7:8], OP.add)
                p.recip_pos(rr[:, 0:1], red[:, 6:7])
                p.t("transpose", ptrb[:], ecb[:], ids[:])
                p.v("tensor_copy", eT[:], ptrb[:])
                p.mm(pcomp[:], eT[:], vctb[:], start=True, stop=True)
                p.v("tensor_copy", mselb[:], msel[:, u * 128:(u + 1) * 128])
                p.t("transpose", ptrb[:], mselb[:], ids[:])
                p.v("tensor_copy", mselT[:], ptrb[:])
                for kk in range(nkk):
                    w = min(512, klen - 512 * kk)
                    p.mm(psS[:, 0:w], mselT[:], E_sb[:, 512 * kk:512 * kk + w],
                         start=True, stop=True)
                    p.v("tensor_copy", mskf[:, 0:w], psS[:, 0:w])
                    if kk == nkk - 1:
                        p.v("tensor_tensor", mskf[:, 0:w], mskf[:, 0:w],
                            cdiag_sb[:, (qt % 4) * 512:(qt % 4) * 512 + w], OP.mult)
                    sl = slice(512 * kk, 512 * kk + w)
                    p.v("tensor_tensor", msbuf[:, sl], Ssb[:, sl], mskf[:, 0:w],
                        OP.mult)
                    p.v("tensor_scalar", scr5[:, 0:w], mskf[:, 0:w], 1.0, None,
                        OP.subtract)
                    p.v("scalar_tensor_tensor", msbuf[:, sl], scr5[:, 0:w], 1e30,
                        msbuf[:, sl], OP.mult, OP.add)
                p.v("reduce_max", red[:, 4:5], msbuf[:, 0:klen], axis=AXX)
                p.v("tensor_scalar", red[:, 5:6], red[:, 4:5], -1.0, None, OP.mult)
                p.s("activation", esel[:, 0:klen], msbuf[:, 0:klen], AF.Exp,
                    bias=red[:, 5:6])
                p.v("reduce_sum", red[:, 6:7], esel[:, 0:klen], axis=AXX)
                p.recip_pos(rr[:, 1:2], red[:, 6:7])
                for kt in range(qt + 1):
                    p.t("transpose", ptrb[:], esel[:, kt * 128:(kt + 1) * 128], ids[:])
                    p.v("tensor_copy", eT[:], ptrb[:])
                    p.mm(psel[:], eT[:], vtb[:, kt * 64:(kt + 1) * 64],
                         start=(kt == 0), stop=(kt == qt))
                kt0 = max(0, qt - 4)
                nwt = qt - kt0 + 1
                wl = nwt * 128
                for idx in range(nwt):
                    kt = kt0 + idx
                    src = Ssb[:, kt * 128:(kt + 1) * 128]
                    dst = wbuf[:, idx * 128:(idx + 1) * 128]
                    if kt == qt:
                        mm_ = cdiag_sb[:, 0:128]
                    elif qt >= 4 and idx == 0:
                        mm_ = wfirst_sb[:]
                    else:
                        p.v("tensor_copy", dst, src)
                        continue
                    p.v("tensor_tensor", dst, src, mm_, OP.mult)
                    p.v("tensor_scalar", scr1[:], mm_, 1.0, None, OP.subtract)
                    p.v("scalar_tensor_tensor", dst, scr1[:], 1e30, dst,
                        OP.mult, OP.add)
                p.v("reduce_max", red[:, 4:5], wbuf[:, 0:wl], axis=AXX)
                p.v("tensor_tensor", red[:, 4:5], red[:, 4:5], sinkcol, OP.max)
                p.v("tensor_scalar", red[:, 5:6], red[:, 4:5], -1.0, None, OP.mult)
                p.s("activation", ew[:, 0:wl], wbuf[:, 0:wl], AF.Exp, bias=red[:, 5:6])
                p.v("reduce_sum", red[:, 6:7], ew[:, 0:wl], axis=AXX)
                p.s("activation", red[:, 7:8], red[:, 5:6], AF.Exp)
                p.v("tensor_tensor", red[:, 7:8], red[:, 7:8], esinkcol, OP.mult)
                p.v("tensor_tensor", red[:, 6:7], red[:, 6:7], red[:, 7:8], OP.add)
                p.recip_pos(rr[:, 2:3], red[:, 6:7])
                for idx in range(nwt):
                    kt = kt0 + idx
                    p.t("transpose", ptrb[:], ew[:, idx * 128:(idx + 1) * 128], ids[:])
                    p.v("tensor_copy", eT[:], ptrb[:])
                    p.mm(pwin[:], eT[:], vtb[:, kt * 64:(kt + 1) * 64],
                         start=(idx == 0), stop=(idx == nwt - 1))
                p.mm(psS[:, 0:3], qtile, wg_b[:], start=True, stop=True)
                p.v("tensor_tensor", gsb[:], psS[:, 0:3], bg_sb[:], OP.add)
                p.s("activation", gsb[:], gsb[:], AF.Sigmoid)
                p.v("reduce_sum", rr[:, 3:4], gsb[:], axis=AXX)
                p.v("tensor_scalar", rr[:, 3:4], rr[:, 3:4], 1e-6, None, OP.max)
                p.recip_pos(rr[:, 4:5], rr[:, 3:4])
                p.v("tensor_tensor", rr[:, 5:6], gsb[:, 0:1], rr[:, 4:5], OP.mult)
                p.v("tensor_tensor", rr[:, 5:6], rr[:, 5:6], rr[:, 0:1], OP.mult)
                if qt == 0:
                    p.v("tensor_tensor", rr[:, 5:6], rr[:, 5:6], anyb0_sb[:], OP.mult)
                p.v("tensor_tensor", rr[:, 6:7], gsb[:, 1:2], rr[:, 4:5], OP.mult)
                p.v("tensor_tensor", rr[:, 6:7], rr[:, 6:7], rr[:, 1:2], OP.mult)
                p.v("tensor_tensor", rr[:, 7:8], gsb[:, 2:3], rr[:, 4:5], OP.mult)
                p.v("tensor_tensor", rr[:, 7:8], rr[:, 7:8], rr[:, 2:3], OP.mult)
                p.v("tensor_scalar", mix[:], pcomp[:], rr[:, 5:6], None, OP.mult)
                p.v("tensor_scalar", t64[:], psel[:], rr[:, 6:7], None, OP.mult)
                p.v("tensor_tensor", mix[:], mix[:], t64[:], OP.add)
                p.v("tensor_scalar", t64[:], pwin[:], rr[:, 7:8], None, OP.mult)
                p.v("tensor_tensor", mix[:], mix[:], t64[:], OP.add)
                p.dma(Mout[j, qt * 128:(qt + 1) * 128, :], mix[:])
    return p.build()


def run_l0attn(st, inp):
    nc = _prog_cache.get("l0attn")
    if nc is None:
        nc = _prog_cache["l0attn"] = build_l0attn()
    qn = st["l0_qn"]
    kn = st["l0_kn"]
    v = st["l0_v"]
    cvalid_f, safe_f, anyb = _l0_masks()
    cdiag, wfirst = _tri_masks()
    msafe_t = np.ascontiguousarray(safe_f.reshape(32, 128, 128))
    cvg = np.ascontiguousarray(
        cvalid_f.reshape(8, 4, 128, 128).transpose(0, 2, 1, 3).reshape(8, 128, 512))
    E = _expand_E()
    P4 = _pool_P4()
    ident = np.eye(128, dtype=BF)
    sink = np.asarray(inp["L0_attn_sink_logit"], dtype=np.float32)
    maps = []
    for c in range(NCORE):
        b = c // 2
        h0 = (c % 2) * 4
        bs = slice(b * T, (b + 1) * T)
        q4 = np.ascontiguousarray(
            qn[bs, h0 * 64:(h0 + 4) * 64].T)
        maps.append({
            "q4": _f32(q4),
            "kfm": _f32(kn[bs].T),
            "ktm": _f32(kn[bs]),
            "vtm": _f32(v[bs]),
            "sinkrep": _rep(sink[h0:h0 + 4]),
            "wgate": _f32(inp["L0_attn_w_gate"]),
            "bgrep": _rep(inp["L0_attn_b_gate"]),
            "E_in": E,
            "P4_in": P4,
            "P4f": _f32(np.zeros((128, 4)) + np.asarray(P4, np.float32)),
            "ident": ident,
            "msafe": msafe_t,
            "cvg": cvg,
            "cdiag": cdiag,
            "wfirst": wfirst,
            "anyb0": _f32(anyb[:128].reshape(128, 1)),
        })
    res = run_bass_kernel_spmd(nc, maps, core_ids=list(range(NCORE)))
    M = np.zeros((B, 8, T, 64), dtype=np.float32)
    for c in range(NCORE):
        b = c // 2
        h0 = (c % 2) * 4
        M[b, h0:h0 + 4] = res.results[c]["Mout"]
    st["l0_mix"] = M



def build_l0out():
    p = Prog()
    mixg = p.inp("mixg", (2, 256, TPC))
    wog = p.inp("wog", (2, 256, 512))
    wout = p.inp("wout", (D, D))
    attn0 = p.outp("attn0", (D, TPC))
    wog_b = p.sb((128, 2 * 2 * 512), BF16)
    wout_b = p.sb((128, 8 * D), BF16)
    wt = p.sb((128, 512))
    wt2 = p.sb((128, D))
    for g in range(2):
        for c in range(2):
            p.dma(wt[:], wog[g, c * 128:(c + 1) * 128, :])
            p.v("tensor_copy", wog_b[:, (g * 2 + c) * 512:(g * 2 + c + 1) * 512], wt[:])
    for kc in range(8):
        p.dma(wt2[:], wout[kc * 128:(kc + 1) * 128, :])
        p.v("tensor_copy", wout_b[:, kc * D:(kc + 1) * D], wt2[:])
    mg = p.sb((128, 512))
    mgb = p.sb((128, 4 * 512), BF16)
    zb = p.sb((128, 8 * 512), BF16)
    pz = p.ps((128, 512))
    asb = p.sb((128, 512))
    for ti in range(TPC // 512):
        cs = slice(ti * 512, (ti + 1) * 512)
        for g in range(2):
            for c in range(2):
                p.dma(mg[:], mixg[g, c * 128:(c + 1) * 128, cs])
                p.v("tensor_copy", mgb[:, (g * 2 + c) * 512:(g * 2 + c + 1) * 512],
                    mg[:])
        for g in range(2):
            for mc in range(4):
                for c in range(2):
                    p.mm(pz[:], wog_b[:, (g * 2 + c) * 512 + mc * 128:
                                      (g * 2 + c) * 512 + (mc + 1) * 128],
                         mgb[:, (g * 2 + c) * 512:(g * 2 + c + 1) * 512],
                         start=(c == 0), stop=(c == 1))
                p.v("tensor_copy", zb[:, (g * 4 + mc) * 512:(g * 4 + mc + 1) * 512],
                    pz[:])
        for mc2 in range(8):
            for kc in range(8):
                p.mm(pz[:], wout_b[:, kc * D + mc2 * 128:kc * D + (mc2 + 1) * 128],
                     zb[:, kc * 512:(kc + 1) * 512], start=(kc == 0), stop=(kc == 7))
            p.v("tensor_copy", asb[:], pz[:])
            p.dma(attn0[mc2 * 128:(mc2 + 1) * 128, cs], asb[:])
    return p.build()


def run_l0out(st, inp):
    nc = _prog_cache.get("l0out")
    if nc is None:
        nc = _prog_cache["l0out"] = build_l0out()
    M = st["l0_mix"]
    Mt = np.ascontiguousarray(M.transpose(0, 1, 3, 2))
    maps = []
    for c in range(NCORE):
        b = c // 2
        t0 = (c % 2) * 2048
        mixg = np.ascontiguousarray(
            Mt[b].reshape(2, 256, T)[:, :, t0:t0 + TPC])
        maps.append({
            "mixg": _f32(mixg),
            "wog": _f32(inp["L0_attn_w_out_group"]),
            "wout": _f32(inp["L0_attn_w_out"]),
        })
    res = run_bass_kernel_spmd(nc, maps, core_ids=list(range(NCORE)))
    a = np.concatenate([r["attn0"] for r in res.results], axis=1)
    st["attn0"] = a



def build_ffnqkv():
    p = Prog()
    xiT = p.inp("xiT", (D, TPC))
    xpT = p.inp("xpT", (D, TPC))
    wqkv = p.inp("wqkv", (D, 1536))
    qs = p.inp("qs", (128, 512))
    ks = p.inp("ks", (128, 512))
    csq = p.inp("csq", (TPC, 64))
    ssq = p.inp("ssq", (TPC, 64))
    csk = p.inp("csk", (TPC, 64))
    ssk = p.inp("ssk", (TPC, 64))
    qr = p.outp("qr", (TPC, 512))
    kr = p.outp("kr", (TPC, 512))
    vr = p.outp("vr", (TPC, 512))

    wb = p.sb((128, 8 * 1536), BF16)
    wt = p.sb((128, 1536))
    for c in range(8):
        p.dma(wt[:], wqkv[c * 128:(c + 1) * 128, :])
        p.v("tensor_copy", wb[:, c * 1536:(c + 1) * 1536], wt[:])
    qs_s = p.sb((128, 512))
    ks_s = p.sb((128, 512))
    p.dma(qs_s[:], qs[:])
    p.dma(ks_s[:], ks[:])
    hb = p.sb((128, 8 * 128), BF16)
    hf = p.sb((128, 128))
    pq = p.ps((128, 512))
    qsb = p.sb((128, 512))
    ksb = p.sb((128, 512))
    vsb = p.sb((128, 512))
    qo = p.sb((128, 512))
    ko = p.sb((128, 512))
    sqq = p.sb((128, 512))
    red = p.sb((128, 2))
    csT = p.sb((128, 64))
    snT = p.sb((128, 64))
    ra = p.sb((128, 32))
    rb = p.sb((128, 32))

    def xrope(src, dst):
        for h in range(8):
            c0 = h * 64
            p.v("tensor_tensor", ra[:], src[:, c0:c0 + 32], snT[:, 0:32], OP.mult)
            p.v("tensor_tensor", rb[:], src[:, c0:c0 + 32], csT[:, 0:32], OP.mult)
            p.v("tensor_tensor", dst[:, c0:c0 + 32], rb[:], ra[:], OP.bypass)

    for tt in range(TPC // 128):
        t0 = tt * 128
        for c in range(8):
            src = xiT if c < 4 else xpT
            p.dma(hf[:], src[c * 128:(c + 1) * 128, t0:t0 + 128])
            p.v("tensor_copy", hb[:, c * 128:(c + 1) * 128], hf[:])
        for part, dstp in ((0, qsb), (1, ksb), (2, vsb)):
            for c in range(8):
                p.mm(pq[:], hb[:, c * 128:(c + 1) * 128],
                     wb[:, c * 1536 + part * 512:c * 1536 + (part + 1) * 512],
                     start=(c == 0), stop=(c == 7))
            p.v("tensor_copy", dstp[:], pq[:])
        p.dma(vr[t0:t0 + 128, :], vsb[:])
        for src, dst, sc in ((qsb, qo, qs_s), (ksb, ko, ks_s)):
            p.s("activation", sqq[:], src[:], AF.Square)
            for h in range(8):
                hs = slice(h * 64, (h + 1) * 64)
                p.v("reduce_sum", red[:, 0:1], sqq[:, hs], axis=AXX)
                p.s("activation", red[:, 0:1], red[:, 0:1], AF.Sqrt)
                p.v("tensor_scalar", red[:, 0:1], red[:, 0:1], 1e-12, None, OP.max)
                p.recip_pos(red[:, 1:2], red[:, 0:1])
                p.v("tensor_scalar", dst[:, hs], src[:, hs], red[:, 1:2], None,
                    OP.mult)
            p.v("tensor_tensor", dst[:], dst[:], sc[:], OP.mult)
        for src, csi, sni in ((qo, csq, ssq), (ko, csk, ssk)):
            p.dma(csT[:], csi[t0:t0 + 128, :])
            p.dma(snT[:], sni[t0:t0 + 128, :])
            for h in range(8):
                c0 = h * 64
                x1 = src[:, c0:c0 + 32]
                x2 = src[:, c0 + 32:c0 + 64]
                p.v("tensor_tensor", ra[:], x1, csT[:, 0:32], OP.mult)
                p.v("tensor_tensor", rb[:], x2, snT[:, 0:32], OP.mult)
                p.v("tensor_tensor", sqq[:, c0:c0 + 32], ra[:], rb[:], OP.subtract)
                p.v("tensor_tensor", ra[:], x2, csT[:, 32:64], OP.mult)
                p.v("tensor_tensor", rb[:], x1, snT[:, 32:64], OP.mult)
                p.v("tensor_tensor", sqq[:, c0 + 32:c0 + 64], ra[:], rb[:], OP.add)
            if src is qo:
                p.dma(qr[t0:t0 + 128, :], sqq[:])
            else:
                p.dma(kr[t0:t0 + 128, :], sqq[:])
    return p.build()


def run_ffnqkv(st, inp, layer, xi_key, out_prefix):
    nc = _prog_cache.get("ffnqkv")
    if nc is None:
        nc = _prog_cache["ffnqkv"] = build_ffnqkv()
    xi = st[xi_key]
    xiT = np.ascontiguousarray(xi.T)
    xp = np.zeros_like(xi)
    xp[1:] = xi[:-1]
    xp[0::T] = 0.0
    for bb in range(B):
        xp[bb * T] = 0.0
    xpT = np.ascontiguousarray(xp.T)
    csq, ssq, csk, ssk = _xpos_tables()
    wqkv = np.asarray(inp["L0_ffn_wqkv"][layer])
    qsv = np.tile(np.asarray(inp["L0_ffn_q_scale"][layer]), 8)
    ksv = np.tile(np.asarray(inp["L0_ffn_k_scale"][layer]), 8)
    maps = []
    for c in range(NCORE):
        t0 = (c % 2) * 2048
        sl = slice(c * TPC, (c + 1) * TPC)
        maps.append({
            "xiT": _f32(xiT[:, sl]),
            "xpT": _f32(xpT[:, sl]),
            "wqkv": _f32(wqkv),
            "qs": _rep(qsv),
            "ks": _rep(ksv),
            "csq": _f32(csq[t0:t0 + TPC]),
            "ssq": _f32(ssq[t0:t0 + TPC]),
            "csk": _f32(csk[t0:t0 + TPC]),
            "ssk": _f32(ssk[t0:t0 + TPC]),
        })
    res = run_bass_kernel_spmd(nc, maps, core_ids=list(range(NCORE)))
    st[out_prefix + "_q"] = np.concatenate([r["qr"] for r in res.results], axis=0)
    st[out_prefix + "_k"] = np.concatenate([r["kr"] for r in res.results], axis=0)
    st[out_prefix + "_v"] = np.concatenate([r["vr"] for r in res.results], axis=0)



def build_locattn():
    p = Prog()
    qf4 = p.inp("qf4", (256, T))
    kf4 = p.inp("kf4", (256, T))
    vt4 = p.inp("vt4", (4, T, 64))
    ident = p.inp("ident", (128, 128), BF16)
    mfirst = p.inp("mfirst", (128, 128))
    mlast = p.inp("mlast", (128, 128))
    Oout = p.outp("Oout", (4, T, 64))

    ids = p.sb((128, 128), BF16)
    p.dma(ids[:], ident[:])
    mf_sb = p.sb((128, 128))
    ml_sb = p.sb((128, 128))
    p.dma(mf_sb[:], mfirst[:])
    p.dma(ml_sb[:], mlast[:])
    q4f = p.sb((128, 2 * T))
    k4f = p.sb((128, 2 * T))
    for r in range(2):
        p.dma(q4f[:, r * T:(r + 1) * T], qf4[r * 128:(r + 1) * 128, :])
        p.dma(k4f[:, r * T:(r + 1) * T], kf4[r * 128:(r + 1) * 128, :])
    qb = p.sb((64, T))
    kb = p.sb((64, T))
    vtb = p.sb((128, 32 * 64), BF16)
    tf = p.sb((128, 64))
    psS = p.ps((128, 128))
    ppv = p.ps((128, 64))
    ptrb = p.ps((128, 128), BF16)
    wbuf = p.sb((128, 640))
    scr1 = p.sb((128, 128))
    ew = p.sb((128, 640), BF16)
    eT = p.sb((128, 128), BF16)
    red = p.sb((128, 3))
    osb = p.sb((128, 64))

    for j in range(4):
        p.v("tensor_copy", qb[:], q4f[(j % 2) * 64:(j % 2) * 64 + 64,
                                      (j // 2) * T:(j // 2 + 1) * T])
        p.v("tensor_copy", kb[:], k4f[(j % 2) * 64:(j % 2) * 64 + 64,
                                      (j // 2) * T:(j // 2 + 1) * T])
        for c in range(32):
            p.dma(tf[:], vt4[j, c * 128:(c + 1) * 128, :])
            p.v("tensor_copy", vtb[:, c * 64:(c + 1) * 64], tf[:])
        for qt in range(32):
            qtile = qb[:, qt * 128:(qt + 1) * 128]
            kt0 = max(0, qt - 4)
            nwt = qt - kt0 + 1
            wl = nwt * 128
            for idx in range(nwt):
                kt = kt0 + idx
                p.mm(psS[:], qtile, kb[:, kt * 128:(kt + 1) * 128],
                     start=True, stop=True)
                dst = wbuf[:, idx * 128:(idx + 1) * 128]
                p.s("mul", dst, psS[:], 8.0)
                if kt == qt:
                    mmk = ml_sb[:]
                elif qt >= 4 and idx == 0:
                    mmk = mf_sb[:]
                else:
                    continue
                p.v("tensor_tensor", dst, dst, mmk, OP.mult)
                p.v("tensor_scalar", scr1[:], mmk, 1.0, None, OP.subtract)
                p.v("scalar_tensor_tensor", dst, scr1[:], 1e30, dst,
                    OP.mult, OP.add)
            p.v("reduce_max", red[:, 0:1], wbuf[:, 0:wl], axis=AXX)
            p.v("tensor_scalar", red[:, 1:2], red[:, 0:1], -1.0, None, OP.mult)
            p.s("activation", ew[:, 0:wl], wbuf[:, 0:wl], AF.Exp, bias=red[:, 1:2])
            p.v("reduce_sum", red[:, 2:3], ew[:, 0:wl], axis=AXX)
            p.recip_pos(red[:, 2:3], red[:, 2:3])
            for idx in range(nwt):
                kt = kt0 + idx
                p.t("transpose", ptrb[:], ew[:, idx * 128:(idx + 1) * 128], ids[:])
                p.v("tensor_copy", eT[:], ptrb[:])
                p.mm(ppv[:], eT[:], vtb[:, kt * 64:(kt + 1) * 64],
                     start=(idx == 0), stop=(idx == nwt - 1))
            p.v("tensor_scalar", osb[:], ppv[:], red[:, 2:3], None, OP.mult)
            p.dma(Oout[j, qt * 128:(qt + 1) * 128, :], osb[:])
    return p.build()


def run_locattn(st, prefix):
    nc = _prog_cache.get("locattn")
    if nc is None:
        nc = _prog_cache["locattn"] = build_locattn()
    qn = st[prefix + "_q"]
    kn = st[prefix + "_k"]
    v = st[prefix + "_v"]
    i = np.arange(128)
    mfirst = _f32(i[None, :] >= i[:, None])
    mlast = _f32(i[None, :] <= i[:, None])
    ident = np.eye(128, dtype=BF)
    maps = []
    for c in range(NCORE):
        b = c // 2
        h0 = (c % 2) * 4
        bs = slice(b * T, (b + 1) * T)
        maps.append({
            "qf4": _f32(qn[bs, h0 * 64:(h0 + 4) * 64].T),
            "kf4": _f32(kn[bs, h0 * 64:(h0 + 4) * 64].T),
            "vt4": _f32(v[bs].reshape(T, 8, 64).transpose(1, 0, 2)[h0:h0 + 4]),
            "ident": ident,
            "mfirst": mfirst,
            "mlast": mlast,
        })
    res = run_bass_kernel_spmd(nc, maps, core_ids=list(range(NCORE)))
    O = np.zeros((B, 8, T, 64), dtype=np.float32)
    for c in range(NCORE):
        O[c // 2, (c % 2) * 4:(c % 2) * 4 + 4] = res.results[c]["Oout"]
    st[prefix + "_o"] = O



HALO = 128
HTOT = TPC + HALO

def build_mlp():
    p = Prog()
    xiT = p.inp("xiT", (D, HTOT))
    ogT = p.inp("ogT", (512, HTOT))
    wo = p.inp("wo", (512, D))
    lng = p.inp("lng", (D, 1))
    lnb = p.inp("lnb", (D, 1))
    w1a = p.inp("w1a", (D, 2816))
    w1g = p.inp("w1g", (D, 2816))
    b1a = p.inp("b1a", (2816, 1))
    b1g = p.inp("b1g", (2816, 1))
    w2p = p.inp("w2p", (2816, D))
    b2 = p.inp("b2", (D, 1))
    ones1 = p.inp("ones1", (1, 128))
    onescol = p.inp("onescol", (128, 1))
    xo = p.outp("xo", (D, TPC))

    o1_sb = p.sb((1, 128))
    oc_sb = p.sb((128, 1))
    p.dma(o1_sb[:], ones1[:])
    p.dma(oc_sb[:], onescol[:])
    lng_sb = p.sb((128, 8))
    lnb_sb = p.sb((128, 8))
    b1a_sb = p.sb((128, 22))
    b1g_sb = p.sb((128, 22))
    b2_sb = p.sb((128, 8))
    for c in range(8):
        p.dma(lng_sb[:, c:c + 1], lng[c * 128:(c + 1) * 128, :])
        p.dma(lnb_sb[:, c:c + 1], lnb[c * 128:(c + 1) * 128, :])
        p.dma(b2_sb[:, c:c + 1], b2[c * 128:(c + 1) * 128, :])
    for c in range(22):
        p.dma(b1a_sb[:, c:c + 1], b1a[c * 128:(c + 1) * 128, :])
        p.dma(b1g_sb[:, c:c + 1], b1g[c * 128:(c + 1) * 128, :])
    wo_b = p.sb((128, 4 * D), BF16)
    wt2 = p.sb((128, D))
    for kc in range(4):
        p.dma(wt2[:], wo[kc * 128:(kc + 1) * 128, :])
        p.v("tensor_copy", wo_b[:, kc * D:(kc + 1) * D], wt2[:])

    x1sb = p.sb((128, 8 * HTOT))
    ogb = p.sb((128, 4 * 512), BF16)
    ogf = p.sb((128, 512))
    xif = p.sb((128, 512))
    pz = p.ps((128, 512))
    tiles = [(0, HALO)] + [(HALO + i * 512, 512) for i in range(4)]
    for (c0, w) in tiles:
        cs = slice(c0, c0 + w)
        for kc in range(4):
            p.dma(ogf[:, 0:w], ogT[kc * 128:(kc + 1) * 128, cs])
            p.v("tensor_copy", ogb[:, kc * 512:kc * 512 + w], ogf[:, 0:w])
        for mc in range(8):
            for kc in range(4):
                p.mm(pz[:, 0:w], wo_b[:, kc * D + mc * 128:kc * D + (mc + 1) * 128],
                     ogb[:, kc * 512:kc * 512 + w], start=(kc == 0), stop=(kc == 3))
            p.dma(xif[:, 0:w], xiT[mc * 128:(mc + 1) * 128, cs])
            p.v("tensor_tensor", x1sb[:, mc * HTOT + c0:mc * HTOT + c0 + w],
                pz[:, 0:w], xif[:, 0:w], OP.add)

    pS1 = p.ps((1, 512))
    pS2 = p.ps((1, 512))
    pA = p.ps((128, 512))
    pB = p.ps((128, 512))
    pH = p.ps((128, 512))
    sqf = p.sb((128, 512))
    srow = p.sb((1, 512))
    arow = p.sb((1, 512))
    brow = p.sb((1, 512))
    m2r = p.sb((1, 512))
    xnf = p.sb((128, 512))
    xnb_all = p.sb((128, 8 * 512), BF16)
    wtile = p.sb((128, 8 * 128))
    wbt = p.sb((128, 8 * 128), BF16)
    w2tile = p.sb((128, 22 * 128))
    w2bt = p.sb((128, 22 * 128), BF16)
    haf = p.sb((128, 512))
    gel = p.sb((128, 512))
    ghb = p.sb((128, 22 * 512), BF16)
    xof = p.sb((128, 512))

    def h2slice(c, rel0, w):
        off = rel0 if c < 4 else rel0 - 1
        return x1sb[:, c * HTOT + off:c * HTOT + off + w]

    for ti in range(4):
        rel0 = HALO + ti * 512
        for c in range(8):
            p.mm(pS1[:], oc_sb[:], h2slice(c, rel0, 512), start=(c == 0),
                 stop=(c == 7))
        for c in range(8):
            p.s("activation", sqf[:], h2slice(c, rel0, 512), AF.Square)
            p.mm(pS2[:], oc_sb[:], sqf[:], start=(c == 0), stop=(c == 7))
        p.v("tensor_scalar", srow[:], pS1[:], 1.0 / D, None, OP.mult)
        p.v("tensor_scalar", arow[:], pS2[:], 1.0 / D, None, OP.mult)
        p.s("activation", m2r[:], srow[:], AF.Square)
        p.v("tensor_tensor", arow[:], arow[:], m2r[:], OP.subtract)
        p.rsqrt_pos(arow[:], arow[:], 1e-5)
        p.v("tensor_tensor", brow[:], srow[:], arow[:], OP.mult)
        p.v("tensor_scalar", brow[:], brow[:], -1.0, None, OP.mult)
        p.mm(pA[:], o1_sb[:], arow[:], start=True, stop=True)
        p.mm(pB[:], o1_sb[:], brow[:], start=True, stop=True)
        for c in range(8):
            p.v("tensor_tensor", xnf[:], h2slice(c, rel0, 512), pA[:], OP.mult)
            p.v("tensor_tensor", xnf[:], xnf[:], pB[:], OP.add)
            p.v("tensor_scalar", xnf[:], xnf[:], lng_sb[:, c:c + 1], None, OP.mult)
            p.v("tensor_scalar", xnf[:], xnf[:], lnb_sb[:, c:c + 1], None, OP.add)
            p.v("tensor_copy", xnb_all[:, c * 512:(c + 1) * 512], xnf[:])
        for mcb in range(22):
            for wsrc, bcol, isgelu in ((w1a, b1a_sb, False), (w1g, b1g_sb, True)):
                for c in range(8):
                    p.dma(wtile[:, c * 128:(c + 1) * 128],
                          wsrc[c * 128:(c + 1) * 128, mcb * 128:(mcb + 1) * 128])
                p.v("tensor_copy", wbt[:], wtile[:])
                for c in range(8):
                    p.mm(pH[:], wbt[:, c * 128:(c + 1) * 128],
                         xnb_all[:, c * 512:(c + 1) * 512],
                         start=(c == 0), stop=(c == 7))
                if isgelu:
                    p.s("activation", gel[:], pH[:], AF.Gelu,
                        bias=b1g_sb[:, mcb:mcb + 1])
                else:
                    p.v("tensor_scalar", haf[:], pH[:], b1a_sb[:, mcb:mcb + 1],
                        None, OP.add)
            p.v("tensor_tensor", ghb[:, mcb * 512:(mcb + 1) * 512], haf[:],
                gel[:], OP.mult)
        for mc2 in range(8):
            for kc in range(22):
                p.dma(w2tile[:, kc * 128:(kc + 1) * 128],
                      w2p[kc * 128:(kc + 1) * 128, mc2 * 128:(mc2 + 1) * 128])
            p.v("tensor_copy", w2bt[:], w2tile[:])
            for kc in range(22):
                p.mm(pH[:], w2bt[:, kc * 128:(kc + 1) * 128],
                     ghb[:, kc * 512:(kc + 1) * 512], start=(kc == 0),
                     stop=(kc == 21))
            p.v("tensor_tensor", xof[:], pH[:],
                x1sb[:, mc2 * HTOT + rel0:mc2 * HTOT + rel0 + 512], OP.add)
            p.v("tensor_scalar", xof[:], xof[:], b2_sb[:, mc2:mc2 + 1], None,
                OP.add)
            p.dma(xo[mc2 * 128:(mc2 + 1) * 128, ti * 512:(ti + 1) * 512], xof[:])
    return p.build()


def run_mlp(st, inp, layer, xi_key, o_key, out_key):
    nc = _prog_cache.get("mlp")
    if nc is None:
        nc = _prog_cache["mlp"] = build_mlp()
    xi = st[xi_key]
    O = st[o_key]
    xiT = np.ascontiguousarray(xi.T)
    OgT = np.ascontiguousarray(
        O.transpose(0, 1, 3, 2).reshape(B, 512, T))
    HID = 2730
    w1 = np.asarray(inp["L0_ffn_w1"][layer])
    b1 = np.asarray(inp["L0_ffn_b1"][layer])
    w2 = np.asarray(inp["L0_ffn_w2"][layer])
    b2 = np.asarray(inp["L0_ffn_b2"][layer])
    w1a = np.zeros((D, 2816), np.float32); w1a[:, :HID] = w1[:, :HID]
    w1g = np.zeros((D, 2816), np.float32); w1g[:, :HID] = w1[:, HID:]
    b1a = np.zeros((2816, 1), np.float32); b1a[:HID, 0] = b1[:HID]
    b1g = np.zeros((2816, 1), np.float32); b1g[:HID, 0] = b1[HID:]
    w2p = np.zeros((2816, D), np.float32); w2p[:HID] = w2
    maps = []
    for c in range(NCORE):
        b = c // 2
        t0g = c * TPC
        xiT_h = np.zeros((D, HTOT), np.float32)
        ogT_h = np.zeros((512, HTOT), np.float32)
        xiT_h[:, HALO:] = xiT[:, t0g:t0g + TPC]
        t0b = (c % 2) * 2048
        ogT_h[:, HALO:] = OgT[b][:, t0b:t0b + TPC]
        if c % 2 == 1:
            xiT_h[:, :HALO] = xiT[:, t0g - HALO:t0g]
            ogT_h[:, :HALO] = OgT[b][:, t0b - HALO:t0b]
        maps.append({
            "xiT": _f32(xiT_h), "ogT": _f32(ogT_h),
            "wo": _f32(inp["L0_ffn_wo"][layer]),
            "lng": _f32(np.asarray(inp["L0_ffn_ln_g"][layer]).reshape(D, 1)),
            "lnb": _f32(np.asarray(inp["L0_ffn_ln_b"][layer]).reshape(D, 1)),
            "w1a": w1a, "w1g": w1g, "b1a": b1a, "b1g": b1g, "w2p": w2p,
            "b2": _f32(b2.reshape(D, 1)),
            "ones1": np.ones((1, 128), np.float32),
            "onescol": np.ones((128, 1), np.float32),
        })
    res = run_bass_kernel_spmd(nc, maps, core_ids=list(range(NCORE)))
    xoT = np.concatenate([r["xo"] for r in res.results], axis=1)
    st[out_key] = np.ascontiguousarray(xoT.T)



def build_comb0():
    p = Prog()
    x = p.inp("x", (TPC, D))
    a = p.inp("a", (TPC, D))
    f = p.inp("f", (TPC, D))
    x1 = p.outp("x1", (TPC, D))
    xn = p.outp("xn", (TPC, D))
    xnb = p.outp("xnb", (TPC, D), BF16)
    xs = p.sb((128, D)); as_ = p.sb((128, D)); fs = p.sb((128, D))
    xc = p.sb((128, D)); sq = p.sb((128, D)); st = p.sb((128, 4))
    xcb = p.sb((128, D), BF16)
    for i in range(TPC // 128):
        sl = slice(i * 128, (i + 1) * 128)
        p.dma(xs[:], x[sl, :]); p.dma(as_[:], a[sl, :]); p.dma(fs[:], f[sl, :])
        p.v("tensor_tensor", xs[:], xs[:], as_[:], OP.add)
        p.v("tensor_tensor", xs[:], xs[:], fs[:], OP.add)
        p.dma(x1[sl, :], xs[:])
        p.v("reduce_sum", st[:, 0:1], xs[:], axis=AXX)
        p.v("tensor_scalar", st[:, 1:2], st[:, 0:1], 1.0 / D, None, OP.mult)
        p.v("tensor_scalar", xc[:], xs[:], st[:, 1:2], None, OP.subtract)
        p.s("activation", sq[:], xc[:], AF.Square)
        p.v("reduce_sum", st[:, 2:3], sq[:], axis=AXX)
        p.v("tensor_scalar", st[:, 2:3], st[:, 2:3], 1.0 / D, None, OP.mult)
        p.rsqrt_pos(st[:, 3:4], st[:, 2:3], 1e-5)
        p.v("tensor_scalar", xc[:], xc[:], st[:, 3:4], None, OP.mult)
        p.dma(xn[sl, :], xc[:])
        p.v("tensor_copy", xcb[:], xc[:])
        p.dma(xnb[sl, :], xcb[:])
    return p.build()


def run_comb0(st):
    nc = _prog_cache.get("comb0")
    if nc is None:
        nc = _prog_cache["comb0"] = build_comb0()
    attn0 = np.ascontiguousarray(st["attn0"].T)
    maps = []
    for c in range(NCORE):
        sl = slice(c * TPC, (c + 1) * TPC)
        maps.append({"x": _f32(st["x0"][sl]), "a": _f32(attn0[sl]),
                     "f": _f32(st["ffn0"][sl])})
    res = run_bass_kernel_spmd(nc, maps, core_ids=list(range(NCORE)))
    st["x1"] = np.concatenate([r["x1"] for r in res.results], axis=0)
    st["xn1"] = np.concatenate([r["xn"] for r in res.results], axis=0)
    st["xn1b"] = np.concatenate([r["xnb"] for r in res.results], axis=0)



def build_l1proj():
    p = Prog()
    xnT = p.inp("xnT", (D, TPC))
    wqT = p.inp("wqT", (D, D))
    wkT = p.inp("wkT", (D, D))
    wvT = p.inp("wvT", (D, D))
    bq = p.inp("bq", (128, D))
    bk = p.inp("bk", (128, D))
    bv = p.inp("bv", (128, D))
    grep = p.inp("grep", (128, D))
    brep = p.inp("brep", (128, D))
    lrwT = p.inp("lrwT", (D, 16))
    lrb = p.inp("lrb", (128, 16))
    XQ = p.outp("XQ", (TPC, D))
    XK = p.outp("XK", (TPC, D))
    XV = p.outp("XV", (TPC, D))
    ETA = p.outp("ETA", (TPC, 16))

    wq_b = p.sb((128, 8 * D), BF16)
    wk_b = p.sb((128, 8 * D), BF16)
    wv_b = p.sb((128, 8 * D), BF16)
    lrw_s = p.sb((128, 8 * 16))
    wt = p.sb((128, D))
    for c in range(8):
        p.dma(wt[:], wqT[c * 128:(c + 1) * 128, :])
        p.v("tensor_copy", wq_b[:, c * D:(c + 1) * D], wt[:])
        p.dma(wt[:], wkT[c * 128:(c + 1) * 128, :])
        p.v("tensor_copy", wk_b[:, c * D:(c + 1) * D], wt[:])
        p.dma(wt[:], wvT[c * 128:(c + 1) * 128, :])
        p.v("tensor_copy", wv_b[:, c * D:(c + 1) * D], wt[:])
        p.dma(lrw_s[:, c * 16:(c + 1) * 16], lrwT[c * 128:(c + 1) * 128, :])
    bq_s = p.sb((128, D)); bk_s = p.sb((128, D)); bv_s = p.sb((128, D))
    g_s = p.sb((128, D)); b_s = p.sb((128, D)); lrb_s = p.sb((128, 16))
    p.dma(bq_s[:], bq[:]); p.dma(bk_s[:], bk[:]); p.dma(bv_s[:], bv[:])
    p.dma(g_s[:], grep[:]); p.dma(b_s[:], brep[:]); p.dma(lrb_s[:], lrb[:])

    xb = p.sb((128, 8 * 128), BF16)
    xf = p.sb((128, 8 * 128))
    pq = p.ps((128, 512))
    pe = p.ps((128, 16))
    qsb = p.sb((128, D)); ksb = p.sb((128, D)); vsb = p.sb((128, D))
    sqq = p.sb((128, D))
    red = p.sb((128, 4))
    dsb = p.sb((128, D))
    esb = p.sb((128, 16))

    for tt in range(TPC // 128):
        t0 = tt * 128
        for c in range(8):
            p.dma(xf[:, c * 128:(c + 1) * 128], xnT[c * 128:(c + 1) * 128,
                                                    t0:t0 + 128])
        p.v("tensor_copy", xb[:], xf[:])
        for wsel, bsel, dst in ((wq_b, bq_s, qsb), (wk_b, bk_s, ksb),
                                (wv_b, bv_s, vsb)):
            for half in range(2):
                for c in range(8):
                    p.mm(pq[:], xb[:, c * 128:(c + 1) * 128],
                         wsel[:, c * D + half * 512:c * D + (half + 1) * 512],
                         start=(c == 0), stop=(c == 7))
                p.v("tensor_copy", dst[:, half * 512:(half + 1) * 512], pq[:])
            p.v("tensor_tensor", dst[:], dst[:], bsel[:], OP.add)
        for src in (qsb, ksb):
            p.s("activation", sqq[:], src[:], AF.Square)
            for h in range(16):
                hs = slice(h * 64, (h + 1) * 64)
                p.v("reduce_sum", red[:, 0:1], sqq[:, hs], axis=AXX)
                p.s("activation", red[:, 0:1], red[:, 0:1], AF.Sqrt)
                p.v("tensor_scalar", red[:, 0:1], red[:, 0:1], 1e-12, None, OP.max)
                p.recip_pos(red[:, 1:2], red[:, 0:1])
                p.v("tensor_scalar", src[:, hs], src[:, hs], red[:, 1:2], None,
                    OP.mult)
        p.dma(XQ[t0:t0 + 128, :], qsb[:])
        p.dma(XK[t0:t0 + 128, :], ksb[:])
        p.v("tensor_tensor", dsb[:], vsb[:], ksb[:], OP.subtract)
        for h in range(16):
            hs = slice(h * 64, (h + 1) * 64)
            p.v("reduce_sum", red[:, 0:1], dsb[:, hs], axis=AXX)
            p.v("tensor_scalar", red[:, 0:1], red[:, 0:1], 1.0 / 64, None, OP.mult)
            p.v("tensor_scalar", dsb[:, hs], dsb[:, hs], red[:, 0:1], None,
                OP.subtract)
            p.s("activation", sqq[:, hs], dsb[:, hs], AF.Square)
            p.v("reduce_sum", red[:, 1:2], sqq[:, hs], axis=AXX)
            p.v("tensor_scalar", red[:, 1:2], red[:, 1:2], 1.0 / 63, None, OP.mult)
            p.s("activation", red[:, 1:2], red[:, 1:2], AF.Sqrt)
            p.v("tensor_scalar", red[:, 1:2], red[:, 1:2], 1e-5, None, OP.add)
            p.recip_pos(red[:, 2:3], red[:, 1:2])
            p.v("tensor_scalar", dsb[:, hs], dsb[:, hs], red[:, 2:3], None,
                OP.mult)
        p.v("tensor_tensor", dsb[:], dsb[:], g_s[:], OP.mult)
        p.v("tensor_tensor", dsb[:], dsb[:], b_s[:], OP.add)
        p.v("tensor_tensor", dsb[:], dsb[:], ksb[:], OP.add)
        p.dma(XV[t0:t0 + 128, :], dsb[:])
        for c in range(8):
            p.mm(pe[:], xf[:, c * 128:(c + 1) * 128], lrw_s[:, c * 16:(c + 1) * 16],
                 start=(c == 0), stop=(c == 7))
        p.v("tensor_tensor", esb[:], pe[:], lrb_s[:], OP.add)
        p.s("activation", esb[:], esb[:], AF.Sigmoid)
        p.v("tensor_scalar", esb[:], esb[:], 1.0 / 4096.0, None, OP.mult)
        p.dma(ETA[t0:t0 + 128, :], esb[:])
    return p.build()


def run_l1proj(st, inp):
    nc = _prog_cache.get("l1proj")
    if nc is None:
        nc = _prog_cache["l1proj"] = build_l1proj()
    xnT = np.ascontiguousarray(st["xn1"].T)
    gw = np.asarray(inp["L1_attn_ttt_norm_weight"]).reshape(-1)
    bw = np.asarray(inp["L1_attn_ttt_norm_bias"]).reshape(-1)
    lrw = np.asarray(inp["L1_attn_ttt_lr_weight"]).reshape(16, D)
    maps = []
    for c in range(NCORE):
        sl = slice(c * TPC, (c + 1) * TPC)
        maps.append({
            "xnT": _f32(xnT[:, sl]),
            "wqT": _f32(np.asarray(inp["L1_attn_wq_w"]).T),
            "wkT": _f32(np.asarray(inp["L1_attn_wk_w"]).T),
            "wvT": _f32(np.asarray(inp["L1_attn_wv_w"]).T),
            "bq": _rep(inp["L1_attn_wq_b"]),
            "bk": _rep(inp["L1_attn_wk_b"]),
            "bv": _rep(inp["L1_attn_wv_b"]),
            "grep": _rep(gw), "brep": _rep(bw),
            "lrwT": _f32(lrw.T),
            "lrb": _rep(np.asarray(inp["L1_attn_ttt_lr_bias"]).reshape(16)),
        })
    res = run_bass_kernel_spmd(nc, maps, core_ids=list(range(NCORE)))
    st["XQ"] = np.concatenate([r["XQ"] for r in res.results], axis=0)
    st["XK"] = np.concatenate([r["XK"] for r in res.results], axis=0)
    st["XV"] = np.concatenate([r["XV"] for r in res.results], axis=0)
    st["ETA"] = np.concatenate([r["ETA"] for r in res.results], axis=0)



def build_l1scan():
    p = Prog()
    XQs = p.inp("XQs", (4, 64, 128, 64))
    XKs = p.inp("XKs", (4, 64, 128, 64))
    XVs = p.inp("XVs", (4, 64, 128, 64))
    ETAs = p.inp("ETAs", (4, 64, 128, 1))
    W1s = p.inp("W1s", (4, 128, 64))
    b1s = p.inp("b1s", (4, 2, 64))
    GR = p.inp("GR", (4, 128, 64))
    BR = p.inp("BR", (4, 128, 64))
    tril2 = p.inp("tril2", (128, 64))
    TRIL2T = p.inp("TRIL2T", (128, 128), BF16)
    CB = p.inp("CB", (2, 128), BF16)
    CBL = p.inp("CBL", (128, 2), BF16)
    ident = p.inp("ident", (128, 128), BF16)
    Y = p.outp("Y", (4, 64, 128, 64))

    ids = p.sb((128, 128), BF16)
    p.dma(ids[:], ident[:])
    tril_s = p.sb((128, 64))
    p.dma(tril_s[:], tril2[:])
    T2T_s = p.sb((128, 128), BF16)
    p.dma(T2T_s[:], TRIL2T[:])
    CB_s = p.sb((2, 128), BF16)
    CBL_s = p.sb((128, 2), BF16)
    p.dma(CB_s[:], CB[:])
    p.dma(CBL_s[:], CBL[:])
    Wst = [p.sb((128, 64)) for _ in range(4)]
    Wb = p.sb((128, 64), BF16)
    bst = [p.sb((2, 64)) for _ in range(4)]
    bb = p.sb((2, 64), BF16)
    gr_s = p.sb((128, 4 * 64))
    br_s = p.sb((128, 4 * 64))
    for g in range(4):
        p.dma(Wst[g][:], W1s[g, :, :])
        p.dma(bst[g][:], b1s[g, :, :])
        p.dma(gr_s[:, g * 64:(g + 1) * 64], GR[g, :, :])
        p.dma(br_s[:, g * 64:(g + 1) * 64], BR[g, :, :])

    xqf = p.sb((128, 64)); xkf = p.sb((128, 64)); xvf = p.sb((128, 64))
    etf = p.sb((128, 1))
    xqb = p.sb((128, 64), BF16); xkb = p.sb((128, 64), BF16)
    bd1 = p.sb((128, 128), BF16); bd2 = p.sb((128, 128), BF16)
    bd3 = p.sb((128, 128), BF16); bdA = p.sb((128, 128), BF16)
    p.v("memset", bd1[:], 0.0)
    p.v("memset", bd2[:], 0.0)
    p.v("memset", bd3[:], 0.0)
    p.v("memset", bdA[:], 0.0)
    ptrb = p.ps((128, 128), BF16)
    xkT_bd = p.sb((128, 128), BF16); xqT_bd = p.sb((128, 128), BF16)
    xkT_st = p.sb((128, 64), BF16)
    pZ = p.ps((128, 64))
    pZb = p.ps((128, 64))
    pS64 = p.ps((128, 64))
    pW = p.ps((128, 64))
    pB2 = p.ps((2, 64))
    rm = p.sb((128, 8))
    xcf = p.sb((128, 64)); sqf = p.sb((128, 64)); xhf = p.sb((128, 64))
    gof = p.sb((128, 64)); tvf = p.sb((128, 64)); gxf = p.sb((128, 64))
    t1f = p.sb((128, 64)); t2f = p.sb((128, 64)); grf = p.sb((128, 64))
    Gf = p.sb((128, 64)); Gnb = p.sb((128, 64), BF16)
    grb = p.sb((128, 64), BF16); Gb = p.sb((128, 64), BF16)
    af = p.sb((128, 64)); ab = p.sb((128, 64), BF16)
    AT_bd = p.sb((128, 128), BF16)
    gkf = p.sb((128, 64)); gkb = p.sb((128, 64), BF16)
    osb = p.sb((128, 64))

    for g in range(4):
        gs = slice(g * 64, (g + 1) * 64)
        grep = gr_s[:, gs]
        brep = br_s[:, gs]
        p.v("tensor_copy", Wb[:], Wst[g][:])
        p.v("tensor_copy", bb[:], bst[g][:])
        for n in range(64):
            p.dma(xqf[:], XQs[g, n, :, :])
            p.dma(xkf[:], XKs[g, n, :, :])
            p.dma(xvf[:], XVs[g, n, :, :])
            p.dma(etf[:], ETAs[g, n, :, :])
            p.v("tensor_copy", xqb[:], xqf[:])
            p.v("tensor_copy", xkb[:], xkf[:])
            p.v("tensor_copy", bd1[0:64, 0:64], xkb[0:64, :])
            p.v("tensor_copy", bd1[64:128, 64:128], xkb[64:128, :])
            p.t("transpose", ptrb[:], bd1[:], ids[:])
            p.v("tensor_copy", xkT_bd[:], ptrb[:])
            p.v("tensor_copy", bd2[0:64, 0:64], xqb[0:64, :])
            p.v("tensor_copy", bd2[64:128, 64:128], xqb[64:128, :])
            p.t("transpose", ptrb[:], bd2[:], ids[:])
            p.v("tensor_copy", xqT_bd[:], ptrb[:])
            p.v("tensor_copy", xkT_st[0:64, :], xkT_bd[0:64, 0:64])
            p.v("tensor_copy", xkT_st[64:128, :], xkT_bd[64:128, 64:128])
            p.mm(pZ[:], xkT_bd[:], Wb[:], start=True, stop=False)
            p.mm(pZ[:], CB_s[:], bb[:], start=False, stop=True)
            p.v("reduce_sum", rm[:, 0:1], pZ[:], axis=AXX)
            p.v("tensor_scalar", rm[:, 0:1], rm[:, 0:1], 1.0 / 64, None, OP.mult)
            p.v("tensor_scalar", xcf[:], pZ[:], rm[:, 0:1], None, OP.subtract)
            p.s("activation", sqf[:], xcf[:], AF.Square)
            p.v("reduce_sum", rm[:, 1:2], sqf[:], axis=AXX)
            p.v("tensor_scalar", rm[:, 1:2], rm[:, 1:2], 1.0 / 64, None, OP.mult)
            p.rsqrt_pos(rm[:, 2:3], rm[:, 1:2], 1e-6)
            p.v("tensor_scalar", xhf[:], xcf[:], rm[:, 2:3], None, OP.mult)
            p.v("tensor_tensor", gof[:], xhf[:], grep, OP.mult)
            p.v("tensor_tensor", gof[:], gof[:], brep, OP.add)
            p.v("tensor_tensor", tvf[:], xvf[:], xkf[:], OP.subtract)
            p.v("tensor_tensor", gof[:], gof[:], tvf[:], OP.subtract)
            p.v("tensor_tensor", gxf[:], gof[:], grep, OP.mult)
            p.v("reduce_sum", rm[:, 3:4], gxf[:], axis=AXX)
            p.v("tensor_tensor", t1f[:], gxf[:], xhf[:], OP.mult)
            p.v("reduce_sum", rm[:, 4:5], t1f[:], axis=AXX)
            p.v("tensor_scalar", t1f[:], xhf[:], rm[:, 4:5], None, OP.mult)
            p.v("tensor_scalar", t2f[:], gxf[:], 64.0, None, OP.mult)
            p.v("tensor_scalar", t2f[:], t2f[:], rm[:, 3:4], None, OP.subtract)
            p.v("tensor_tensor", t2f[:], t2f[:], t1f[:], OP.subtract)
            p.v("tensor_scalar", rm[:, 5:6], rm[:, 2:3], 1.0 / 64, None, OP.mult)
            p.v("tensor_scalar", grf[:], t2f[:], rm[:, 5:6], None, OP.mult)
            p.v("tensor_scalar", Gf[:], grf[:], etf[:, 0:1], None, OP.mult)
            p.v("tensor_scalar", t1f[:], Gf[:], -1.0, None, OP.mult)
            p.v("tensor_copy", Gnb[:], t1f[:])
            p.v("tensor_copy", Gb[:], Gf[:])
            p.v("tensor_copy", grb[:], grf[:])
            p.mm(pS64[:], xqT_bd[:], xkT_st[:], start=True, stop=True)
            p.v("tensor_tensor", af[:], pS64[:], tril_s[:], OP.mult)
            p.v("tensor_copy", ab[:], af[:])
            p.v("tensor_copy", bdA[0:64, 0:64], ab[0:64, :])
            p.v("tensor_copy", bdA[64:128, 64:128], ab[64:128, :])
            p.t("transpose", ptrb[:], bdA[:], ids[:])
            p.v("tensor_copy", AT_bd[:], ptrb[:])
            p.mm(pZb[:], xqT_bd[:], Wb[:], start=True, stop=False)
            p.mm(pZb[:], CB_s[:], bb[:], start=False, stop=False)
            p.mm(pZb[:], T2T_s[:], Gnb[:], start=False, stop=False)
            p.mm(pZb[:], AT_bd[:], Gnb[:], start=False, stop=True)
            p.v("reduce_sum", rm[:, 0:1], pZb[:], axis=AXX)
            p.v("tensor_scalar", rm[:, 0:1], rm[:, 0:1], 1.0 / 64, None, OP.mult)
            p.v("tensor_scalar", xcf[:], pZb[:], rm[:, 0:1], None, OP.subtract)
            p.s("activation", sqf[:], xcf[:], AF.Square)
            p.v("reduce_sum", rm[:, 1:2], sqf[:], axis=AXX)
            p.v("tensor_scalar", rm[:, 1:2], rm[:, 1:2], 1.0 / 64, None, OP.mult)
            p.rsqrt_pos(rm[:, 2:3], rm[:, 1:2], 1e-6)
            p.v("tensor_scalar", xhf[:], xcf[:], rm[:, 2:3], None, OP.mult)
            p.v("tensor_tensor", osb[:], xhf[:], grep, OP.mult)
            p.v("tensor_tensor", osb[:], osb[:], brep, OP.add)
            p.v("tensor_tensor", osb[:], osb[:], xqf[:], OP.add)
            p.dma(Y[g, n, :, :], osb[:])
            p.v("tensor_scalar", gkf[:], xkf[:], etf[:, 0:1], None, OP.mult)
            p.v("tensor_copy", gkb[:], gkf[:])
            p.v("tensor_copy", bd3[0:64, 0:64], gkb[0:64, :])
            p.v("tensor_copy", bd3[64:128, 64:128], gkb[64:128, :])
            p.mm(pW[:], bd3[:], grb[:], start=True, stop=True)
            p.v("tensor_tensor", Wst[g][:], Wst[g][:], pW[:], OP.subtract)
            p.v("tensor_copy", Wb[:], Wst[g][:])
            p.mm(pB2[:], CBL_s[:], Gb[:], start=True, stop=True)
            p.v("tensor_tensor", bst[g][:], bst[g][:], pB2[:], OP.subtract)
            p.v("tensor_copy", bb[:], bst[g][:])
    return p.build()


def run_l1scan(st, inp):
    nc = _prog_cache.get("l1scan")
    if nc is None:
        nc = _prog_cache["l1scan"] = build_l1scan()
    H, C = 16, 64
    NC_ = T // C
    XQ = st["XQ"].reshape(B, T, H, 64)
    XK = st["XK"].reshape(B, T, H, 64)
    XV = st["XV"].reshape(B, T, H, 64)
    ETA = st["ETA"].reshape(B, T, H)
    W1 = np.asarray(inp["L1_attn_W1"])
    b1 = np.asarray(inp["L1_attn_b1"])
    gw = np.asarray(inp["L1_attn_ttt_norm_weight"])
    bw = np.asarray(inp["L1_attn_ttt_norm_bias"])
    i64 = np.arange(64)
    trilI = (i64[None, :] <= i64[:, None]).astype(np.float32)
    tril2 = _f32(np.concatenate([trilI, trilI], axis=0))
    TT = np.zeros((128, 128), np.float32)
    up = (i64[:, None] <= i64[None, :]).astype(np.float32)
    TT[0:64, 0:64] = up
    TT[64:128, 64:128] = up
    CBv = np.zeros((2, 128), np.float32)
    CBv[0, 0:64] = 1.0
    CBv[1, 64:128] = 1.0
    maps = []
    for c in range(NCORE):
        xq = np.zeros((4, 64, 128, 64), np.float32)
        xk = np.zeros_like(xq); xv = np.zeros_like(xq)
        et = np.zeros((4, 64, 128, 1), np.float32)
        W1s = np.zeros((4, 128, 64), np.float32)
        b1s = np.zeros((4, 2, 64), np.float32)
        GRs = np.zeros((4, 128, 64), np.float32)
        BRs = np.zeros((4, 128, 64), np.float32)
        for g in range(4):
            for pi in range(2):
                pair = c * 8 + g * 2 + pi
                b = pair // 16
                h = pair % 16
                rows = slice(pi * 64, (pi + 1) * 64)
                for n in range(NC_):
                    ts_ = slice(n * 64, (n + 1) * 64)
                    xq[g, n, rows] = XQ[b, ts_, h]
                    xk[g, n, rows] = XK[b, ts_, h]
                    xv[g, n, rows] = XV[b, ts_, h]
                    et[g, n, rows, 0] = ETA[b, ts_, h]
                W1s[g, rows] = W1[h]
                b1s[g, pi] = b1[h, 0]
                GRs[g, rows] = np.broadcast_to(gw[h], (64, 64))
                BRs[g, rows] = np.broadcast_to(bw[h], (64, 64))
        maps.append({
            "XQs": xq, "XKs": xk, "XVs": xv, "ETAs": et,
            "W1s": W1s, "b1s": b1s, "GR": GRs, "BR": BRs,
            "tril2": tril2, "TRIL2T": _bf(TT), "CB": _bf(CBv),
            "CBL": _bf(CBv.T), "ident": np.eye(128, dtype=BF),
        })
    res = run_bass_kernel_spmd(nc, maps, core_ids=list(range(NCORE)))
    Yf = np.zeros((B, H, T, 64), np.float32)
    for c in range(NCORE):
        Yc = res.results[c]["Y"]
        for g in range(4):
            for pi in range(2):
                pair = c * 8 + g * 2 + pi
                b = pair // 16
                h = pair % 16
                Yf[b, h] = Yc[g, :, pi * 64:(pi + 1) * 64, :].reshape(T, 64)
    st["Yttt"] = Yf



def build_l1post():
    p = Prog()
    yT = p.inp("yT", (D, TPC))
    xnT = p.inp("xnT", (D, TPC))
    pw = p.inp("pw", (D, 1))
    pb = p.inp("pb", (D, 1))
    woT = p.inp("woT", (D, D))
    wob = p.inp("wob", (D, 1))
    gwT = p.inp("gwT", (D, 8))
    ones1 = p.inp("ones1", (1, 128))
    onescol = p.inp("onescol", (128, 1))
    attn1 = p.outp("attn1", (D, TPC))
    comb = p.outp("comb", (TPC, 8))

    o1_sb = p.sb((1, 128)); oc_sb = p.sb((128, 1))
    p.dma(o1_sb[:], ones1[:]); p.dma(oc_sb[:], onescol[:])
    pw_sb = p.sb((128, 8)); pb_sb = p.sb((128, 8)); wob_sb = p.sb((128, 8))
    for c in range(8):
        p.dma(pw_sb[:, c:c + 1], pw[c * 128:(c + 1) * 128, :])
        p.dma(pb_sb[:, c:c + 1], pb[c * 128:(c + 1) * 128, :])
        p.dma(wob_sb[:, c:c + 1], wob[c * 128:(c + 1) * 128, :])
    wo_b = p.sb((128, 8 * D), BF16)
    gw_s = p.sb((128, 8 * 8))
    wt = p.sb((128, D))
    for c in range(8):
        p.dma(wt[:], woT[c * 128:(c + 1) * 128, :])
        p.v("tensor_copy", wo_b[:, c * D:(c + 1) * D], wt[:])
        p.dma(gw_s[:, c * 8:(c + 1) * 8], gwT[c * 128:(c + 1) * 128, :])

    yf = p.sb((128, 8 * 512))
    xnf = p.sb((128, 8 * 512))
    sqf = p.sb((128, 512))
    pS1 = p.ps((1, 512)); pS2 = p.ps((1, 512))
    pA = p.ps((128, 512)); pB = p.ps((128, 512))
    pY = p.ps((128, 512))
    pG = p.ps((128, 8))
    srow = p.sb((1, 512)); arow = p.sb((1, 512)); brow = p.sb((1, 512))
    m2r = p.sb((1, 512))
    lnb_all = p.sb((128, 8 * 512), BF16)
    lnf = p.sb((128, 512))
    asb = p.sb((128, 512))
    gsc = p.sb((128, 8))
    eqs = p.sb((128, 8)); eqs2 = p.sb((128, 8))
    rr = p.sb((128, 8))
    cmb = p.sb((128, 8))

    for ti in range(TPC // 512):
        cs = slice(ti * 512, (ti + 1) * 512)
        for c in range(8):
            p.dma(yf[:, c * 512:(c + 1) * 512], yT[c * 128:(c + 1) * 128, cs])
            p.dma(xnf[:, c * 512:(c + 1) * 512], xnT[c * 128:(c + 1) * 128, cs])
        for c in range(8):
            p.mm(pS1[:], oc_sb[:], yf[:, c * 512:(c + 1) * 512], start=(c == 0),
                 stop=(c == 7))
        for c in range(8):
            p.s("activation", sqf[:], yf[:, c * 512:(c + 1) * 512], AF.Square)
            p.mm(pS2[:], oc_sb[:], sqf[:], start=(c == 0), stop=(c == 7))
        p.v("tensor_scalar", srow[:], pS1[:], 1.0 / D, None, OP.mult)
        p.v("tensor_scalar", arow[:], pS2[:], 1.0 / D, None, OP.mult)
        p.s("activation", m2r[:], srow[:], AF.Square)
        p.v("tensor_tensor", arow[:], arow[:], m2r[:], OP.subtract)
        p.rsqrt_pos(arow[:], arow[:], 1e-6)
        p.v("tensor_tensor", brow[:], srow[:], arow[:], OP.mult)
        p.v("tensor_scalar", brow[:], brow[:], -1.0, None, OP.mult)
        p.mm(pA[:], o1_sb[:], arow[:], start=True, stop=True)
        p.mm(pB[:], o1_sb[:], brow[:], start=True, stop=True)
        for c in range(8):
            p.v("tensor_tensor", lnf[:], yf[:, c * 512:(c + 1) * 512], pA[:],
                OP.mult)
            p.v("tensor_tensor", lnf[:], lnf[:], pB[:], OP.add)
            p.v("tensor_scalar", lnf[:], lnf[:], pw_sb[:, c:c + 1], None, OP.mult)
            p.v("tensor_scalar", lnf[:], lnf[:], pb_sb[:, c:c + 1], None, OP.add)
            p.v("tensor_copy", lnb_all[:, c * 512:(c + 1) * 512], lnf[:])
        for mc in range(8):
            for kc in range(8):
                p.mm(pY[:], wo_b[:, kc * D + mc * 128:kc * D + (mc + 1) * 128],
                     lnb_all[:, kc * 512:(kc + 1) * 512], start=(kc == 0),
                     stop=(kc == 7))
            p.v("tensor_scalar", asb[:], pY[:], wob_sb[:, mc:mc + 1], None, OP.add)
            p.dma(attn1[mc * 128:(mc + 1) * 128, cs], asb[:])
        for sub in range(4):
            t0 = ti * 512 + sub * 128
            for c in range(8):
                p.mm(pG[:], xnf[:, c * 512 + sub * 128:c * 512 + (sub + 1) * 128],
                     gw_s[:, c * 8:(c + 1) * 8], start=(c == 0), stop=(c == 7))
            p.v("tensor_copy", gsc[:], pG[:])
            p.v("reduce_max", rr[:, 0:1], gsc[:], axis=AXX)
            p.v("tensor_scalar", eqs[:], gsc[:], rr[:, 0:1], None, OP.is_ge)
            p.v("scalar_tensor_tensor", gsc[:], eqs[:], -1e30, gsc[:],
                OP.mult, OP.add)
            p.v("reduce_max", rr[:, 1:2], gsc[:], axis=AXX)
            p.v("tensor_scalar", eqs2[:], gsc[:], rr[:, 1:2], None, OP.is_ge)
            p.v("tensor_tensor", rr[:, 2:3], rr[:, 1:2], rr[:, 0:1], OP.subtract)
            p.s("activation", rr[:, 3:4], rr[:, 2:3], AF.Exp)
            p.v("tensor_scalar", rr[:, 4:5], rr[:, 3:4], 1.0, None, OP.add)
            p.recip_pos(rr[:, 5:6], rr[:, 4:5])
            p.v("tensor_tensor", rr[:, 6:7], rr[:, 3:4], rr[:, 5:6], OP.mult)
            p.v("tensor_scalar", cmb[:], eqs[:], rr[:, 5:6], None, OP.mult)
            p.v("tensor_scalar", eqs2[:], eqs2[:], rr[:, 6:7], None, OP.mult)
            p.v("tensor_tensor", cmb[:], cmb[:], eqs2[:], OP.add)
            p.dma(comb[t0:t0 + 128, :], cmb[:])
    return p.build()


def run_l1post(st, inp):
    nc = _prog_cache.get("l1post")
    if nc is None:
        nc = _prog_cache["l1post"] = build_l1post()
    Y = st["Yttt"]
    ytm = np.ascontiguousarray(Y.transpose(0, 2, 1, 3).reshape(TOK, D))
    yT = np.ascontiguousarray(ytm.T)
    xnT = np.ascontiguousarray(st["xn1"].T)
    maps = []
    for c in range(NCORE):
        sl = slice(c * TPC, (c + 1) * TPC)
        maps.append({
            "yT": _f32(yT[:, sl]),
            "xnT": _f32(xnT[:, sl]),
            "pw": _f32(np.asarray(inp["L1_attn_post_norm_w"]).reshape(D, 1)),
            "pb": _f32(np.asarray(inp["L1_attn_post_norm_b"]).reshape(D, 1)),
            "woT": _f32(np.asarray(inp["L1_attn_wo_w"]).T),
            "wob": _f32(np.asarray(inp["L1_attn_wo_b"]).reshape(D, 1)),
            "gwT": _f32(np.asarray(inp["L1_ffn_gate_w"]).T),
            "ones1": np.ones((1, 128), np.float32),
            "onescol": np.ones((128, 1), np.float32),
        })
    res = run_bass_kernel_spmd(nc, maps, core_ids=list(range(NCORE)))
    st["attn1"] = np.concatenate([r["attn1"] for r in res.results], axis=1)
    st["comb"] = np.concatenate([r["comb"] for r in res.results], axis=0)



def build_moe():
    p = Prog()
    xnbT = p.inp("xnbT", (D, TOK), BF16)
    w1T = p.inp("w1T", (D, 4096))
    w3T = p.inp("w3T", (D, 4096))
    w2T = p.inp("w2T", (4096, D))
    combr = p.inp("combr", (1, TOK))
    ones1 = p.inp("ones1", (1, 128))
    yes = [p.outp(f"yeh{hh}", (D, TOK), BF16) for hh in range(2)]

    o1_sb = p.sb((1, 128))
    p.dma(o1_sb[:], ones1[:])
    HH = 2048
    w1b = p.sb((128, 8 * HH), BF16)
    w3b = p.sb((128, 8 * HH), BF16)
    w2b = p.sb((128, 16 * D), BF16)
    wtf = p.sb((128, HH))
    wtf2 = p.sb((128, D))
    xnb = p.sb((128, 8 * 512), BF16)
    pH1 = p.ps((128, 512))
    pH2 = p.ps((128, 512))
    pY = p.ps((128, 512))
    pC = p.ps((128, 512))
    sil = p.sb((128, 512))
    hb = p.sb((128, 16 * 512), BF16)
    crow = p.sb((1, 512))
    cbc = p.sb((128, 512))
    ysb = p.sb((128, 512), BF16)
    yw = p.sb((128, 512))
    for hh in range(2):
        hs = slice(hh * HH, (hh + 1) * HH)
        for c in range(8):
            p.dma(wtf[:], w1T[c * 128:(c + 1) * 128, hs])
            p.v("tensor_copy", w1b[:, c * HH:(c + 1) * HH], wtf[:])
            p.dma(wtf[:], w3T[c * 128:(c + 1) * 128, hs])
            p.v("tensor_copy", w3b[:, c * HH:(c + 1) * HH], wtf[:])
        for kc in range(16):
            p.dma(wtf2[:], w2T[hh * HH + kc * 128:hh * HH + (kc + 1) * 128, :])
            p.v("tensor_copy", w2b[:, kc * D:(kc + 1) * D], wtf2[:])
        for tt in range(TOK // 512):
            cs = slice(tt * 512, (tt + 1) * 512)
            for c in range(8):
                p.dma(xnb[:, c * 512:(c + 1) * 512], xnbT[c * 128:(c + 1) * 128, cs])
            for hc in range(16):
                for c in range(8):
                    p.mm(pH1[:], w1b[:, c * HH + hc * 128:c * HH + (hc + 1) * 128],
                         xnb[:, c * 512:(c + 1) * 512], start=(c == 0), stop=(c == 7))
                for c in range(8):
                    p.mm(pH2[:], w3b[:, c * HH + hc * 128:c * HH + (hc + 1) * 128],
                         xnb[:, c * 512:(c + 1) * 512], start=(c == 0), stop=(c == 7))
                p.s("activation", sil[:], pH1[:], AF.Silu)
                p.v("tensor_tensor", hb[:, hc * 512:(hc + 1) * 512], sil[:],
                    pH2[:], OP.mult)
            p.dma(crow[:], combr[:, cs])
            p.mm(pC[:], o1_sb[:], crow[:], start=True, stop=True)
            p.v("tensor_copy", cbc[:], pC[:])
            for mc in range(8):
                for kc in range(16):
                    p.mm(pY[:], w2b[:, kc * D + mc * 128:kc * D + (mc + 1) * 128],
                         hb[:, kc * 512:(kc + 1) * 512], start=(kc == 0),
                         stop=(kc == 15))
                p.v("tensor_tensor", yw[:], pY[:], cbc[:], OP.mult)
                p.v("tensor_copy", ysb[:], yw[:])
                p.dma(yes[hh][mc * 128:(mc + 1) * 128, cs], ysb[:])
    return p.build()


def run_moe(st, inp):
    nc = _prog_cache.get("moe")
    if nc is None:
        nc = _prog_cache["moe"] = build_moe()
    xnbT = np.ascontiguousarray(st["xn1b"].T)
    combT = np.ascontiguousarray(st["comb"].T)
    w1 = np.asarray(inp["L1_ffn_w1"])
    w2 = np.asarray(inp["L1_ffn_w2"])
    w3 = np.asarray(inp["L1_ffn_w3"])
    maps = []
    for e in range(NCORE):
        maps.append({
            "xnbT": xnbT,
            "w1T": _f32(w1[e].T),
            "w3T": _f32(w3[e].T),
            "w2T": _f32(w2[e].T),
            "combr": _f32(combT[e:e + 1]),
            "ones1": np.ones((1, 128), np.float32),
        })
    res = run_bass_kernel_spmd(nc, maps, core_ids=list(range(NCORE)))
    st["yes"] = [r[f"yeh{hh}"] for r in res.results for hh in range(2)]



def build_final():
    p = Prog()
    x1 = p.inp("x1", (TPC, D))
    a1 = p.inp("a1", (TPC, D))
    yes = [p.inp(f"ye{e}", (TPC, D), BF16) for e in range(16)]
    out = p.outp("out", (TPC, D))
    xs = p.sb((128, D))
    ts_ = p.sb((128, D))
    tb_ = p.sb((128, D), BF16)
    for i in range(TPC // 128):
        sl = slice(i * 128, (i + 1) * 128)
        p.dma(xs[:], x1[sl, :])
        p.dma(ts_[:], a1[sl, :])
        p.v("tensor_tensor", xs[:], xs[:], ts_[:], OP.add)
        for e in range(16):
            p.dma(tb_[:], yes[e][sl, :])
            p.v("tensor_tensor", xs[:], xs[:], tb_[:], OP.add)
        p.dma(out[sl, :], xs[:])
    return p.build()


def run_final(st):
    nc = _prog_cache.get("final")
    if nc is None:
        nc = _prog_cache["final"] = build_final()
    a1 = np.ascontiguousarray(st["attn1"].T)
    yes = [np.ascontiguousarray(y.T) for y in st["yes"]]
    maps = []
    for c in range(NCORE):
        sl = slice(c * TPC, (c + 1) * TPC)
        m = {"x1": _f32(st["x1"][sl]), "a1": _f32(a1[sl])}
        for e in range(16):
            m[f"ye{e}"] = np.ascontiguousarray(yes[e][sl])
        maps.append(m)
    res = run_bass_kernel_spmd(nc, maps, core_ids=list(range(NCORE)))
    st["out"] = np.concatenate([r["out"] for r in res.results], axis=0)



def kernel(**inputs):
    st = {"x0": _f32(np.asarray(inputs["L0_attn_h"]).reshape(TOK, D))}
    run_ln0(st)
    run_l0qkv(st, inputs)
    run_l0attn(st, inputs)
    run_l0out(st, inputs)
    run_ffnqkv(st, inputs, 0, "xn0", "f0")
    run_locattn(st, "f0")
    run_mlp(st, inputs, 0, "xn0", "f0_o", "xi1")
    run_ffnqkv(st, inputs, 1, "xi1", "f1")
    run_locattn(st, "f1")
    run_mlp(st, inputs, 1, "xi1", "f1_o", "ffn0")
    run_comb0(st)
    run_l1proj(st, inputs)
    run_l1scan(st, inputs)
    run_l1post(st, inputs)
    run_moe(st, inputs)
    run_final(st)
    return st["out"].reshape(B, T, D).astype(np.float32)
